# version 20
# baseline (speedup 1.0000x reference)
import numpy as np
import ml_dtypes
import concourse.bass as bass
import concourse.mybir as mybir
from concourse.ap import AP
from concourse.bass_utils import run_bass_kernel_spmd

F32 = mybir.dt.float32
BF16 = mybir.dt.bfloat16
AF = mybir.ActivationFunctionType
ALU = mybir.AluOpType
AX = mybir.AxisListType.X

B, L, D = 4, 4096, 1024
E, S, FF = 2048, 128, 4096
OWN = 2048
NCH = 8
NG = 32
ITC = 4
EPS = 1e-5

ENGINES = ("sync", "gpsimd", "tensor", "vector", "scalar")


class Sched:

    def __init__(self):
        from collections import defaultdict
        self.ops = {e: [] for e in ENGINES}
        self.cnt = defaultdict(int)
        self.dma_eng = {}

    def emit(self, eng, fn, waits=(), inc=None, dma=False, pre_drain=False):
        if fn is not None and (eng in ("vector", "scalar")
                               or (eng == "gpsimd" and not dma)):
            fn = (lambda f: lambda e_: (e_.drain(), f(e_))[1])(fn)
        val = None
        if inc is not None:
            self.cnt[inc] += 16 if dma else 1
            val = self.cnt[inc]
            if dma:
                self.dma_eng[inc] = eng
        for (s_, v_) in waits:
            assert v_ is None or isinstance(v_, int), (eng, s_, v_)
        ws = tuple((s_, v_) for (s_, v_) in waits if v_ is not None)
        self.ops[eng].append((fn, ws, inc, 16 if dma else 1))
        return val

    def tag_last(self, eng, inc, dma=False):
        self.cnt[inc] += 16 if dma else 1
        val = self.cnt[inc]
        fn, ws, inc0, n0 = self.ops[eng][-1]
        assert inc0 is None, inc0
        self.ops[eng][-1] = (fn, ws, inc, 16 if dma else 1)
        return val

    def barrier(self):
        for nm, eng in sorted(self.dma_eng.items()):
            self.ops[eng].append((None, ((nm, self.cnt[nm]),), None, 1))
        self.cnt["ph"] += len(ENGINES)
        target = self.cnt["ph"]
        for e in ENGINES:
            self.ops[e].append((lambda eng: eng.nop(), (), "ph", 1))
        for e in ENGINES:
            self.ops[e].append((None, (("ph", target),), None, 1))


def _trig_tables():
    try:
        import jax
        import jax.numpy as jnp
        cpu = jax.devices("cpu")[0]
        with jax.default_device(cpu):
            pos = jnp.arange(L, dtype=jnp.float32)[:, None]
            inv_freq = 10000.0 ** (jnp.arange(64, dtype=jnp.float32) / 64)
            t = pos * inv_freq[None, :]
            sin = np.asarray(jnp.sin(t))
            cos = np.asarray(jnp.cos(t))
    except Exception:
        inv = (10000.0 ** (np.arange(64, dtype=np.float64) / 64.0)).astype(np.float32)
        t = (np.arange(L, dtype=np.float32)[:, None] * inv[None, :]).astype(np.float32)
        sin = np.sin(t.astype(np.float64)).astype(np.float32)
        cos = np.cos(t.astype(np.float64)).astype(np.float32)
    return sin, cos



def rap(base, off, dims):
    return AP(base.tensor, off, [list(d) for d in dims])

def build_program(dump=(), phases=("p1", "c", "d")):
    nc = bass.Bass("TRN2", target_bir_lowering=False)

    X = nc.dram_tensor("x", [L, D], F32, kind="ExternalInput").ap()
    UVWT = nc.dram_tensor("uvwT", [D, 2 * E + S], F32, kind="ExternalInput").ap()
    OWT = nc.dram_tensor("owT", [E, D], F32, kind="ExternalInput").ap()
    W1T = nc.dram_tensor("w1T", [D, FF], F32, kind="ExternalInput").ap()
    W2T = nc.dram_tensor("w2T", [FF, D], F32, kind="ExternalInput").ap()
    BIAST = nc.dram_tensor("biasT", [L, OWN], F32, kind="ExternalInput").ap()
    COST = nc.dram_tensor("cosT", [128, L], F32, kind="ExternalInput").ap()
    SINT = nc.dram_tensor("sinT", [128, L], F32, kind="ExternalInput").ap()
    GB = nc.dram_tensor("gb", [128, 8], F32, kind="ExternalInput").ap()
    GREP = nc.dram_tensor("grep", [128, 1], F32, kind="ExternalInput").ap()
    RS = nc.dram_tensor("rs", [128, 8], F32, kind="ExternalInput").ap()
    B1 = nc.dram_tensor("b1", [128, 32], F32, kind="ExternalInput").ap()
    B2 = nc.dram_tensor("b2", [128, 8], F32, kind="ExternalInput").ap()
    IDENT = nc.dram_tensor("ident", [128, 128], F32, kind="ExternalInput").ap()
    CONSTS = nc.dram_tensor("consts", [128, 8], F32, kind="ExternalInput").ap()
    WRELC = nc.dram_tensor("wrelc", [128, 1], F32, kind="ExternalInput").ap()
    OUT = nc.dram_tensor("out", [OWN, D], F32, kind="ExternalOutput").ap()

    VSCR = nc.dram_tensor("v_scr", [L, E], BF16).ap()
    XNSCR = nc.dram_tensor("xn_scr", [D, OWN], BF16).ap()
    SSCR = nc.dram_tensor("s_scr", [D, OWN], BF16).ap()
    W1SCR = nc.dram_tensor("w1_scr", [D, FF], BF16).ap()
    UVUSCR = nc.dram_tensor("uvu_scr", [D, E], BF16).ap()
    W2SCR = nc.dram_tensor("w2_scr", [FF, D], BF16).ap()

    dumps = {}
    if dump:
        shapes = {"xn": [D, OWN], "v": [L, E], "k": [128, L], "q": [128, OWN],
                  "alpha": [128, 32], "s": [D, OWN], "w1t": [D, FF],
                  "w2t": [FF, D]}
        for name in dump:
            dumps[name] = nc.dram_tensor("dump_" + name, shapes[name], F32,
                                         kind="ExternalOutput").ap()

    sch = Sched()
    E_ = sch.emit
    guards = []

    def sb(name, shape, dtype):
        g = nc.sbuf_tensor("sb_" + name, shape, dtype)
        t = g.__enter__()
        guards.append(g)
        return t

    def free_from(mark):
        while len(guards) > mark:
            guards.pop().__exit__(None, None, None)

    consts = sb("consts", [128, 8], F32)
    gb = sb("gbv", [128, 8], F32)
    gbq = sb("gbq", [128, 4], F32)
    grep = sb("grep", [128, 1], F32)
    isq = sb("isq", [128, 1], F32)
    rs = sb("rs", [128, 8], F32)
    b1s = sb("b1s", [128, 32], F32)
    b2s = sb("b2s", [128, 8], F32)
    identb = sb("identb", [128, 128], BF16)
    identf = sb("identf", [128, 128], F32)
    alpha = sb("alpha", [128, NG], F32)
    kbf = sb("kbf", [128, L], BF16)
    qbf = sb("qbf", [128, OWN], BF16)
    wacc = sb("wacc", [128, 2], F32)
    wsc = sb("wsc", [128, 2], F32)
    wisq = sb("wisq", [128, 1], F32)

    nc.const_aps.register(F32, 0.5, consts[:, 2:3])
    nc.const_aps.register(F32, -0.5, consts[:, 3:4])

    ps = [nc.psum_tensor(f"ps{i}", [128, 512], F32) for i in range(8)]
    psg = []
    for p in ps:
        psg.append(p.__enter__())
        guards.append(p)

    for apdst, apsrc in ((consts[:], CONSTS), (gb[:], GB), (grep[:], GREP),
                         (rs[:], RS), (b1s[:], B1), (b2s[:], B2),
                         (identf[:], IDENT)):
        E_("sync", (lambda d_, s_: lambda eng: eng.dma_start(out=d_, in_=s_))(apdst, apsrc),
           inc="ld0", dma=True)
    d0 = sch.cnt["ld0"]
    E_("gpsimd", lambda eng: eng.dma_start(out=identb[:], in_=IDENT), inc="ldg", dma=True)
    E_("scalar", lambda eng: eng.activation(isq[:], consts[:, 0:1], AF.Ln),
       waits=(("ld0", d0),))
    E_("scalar", lambda eng: eng.drain())
    sv = E_("scalar", lambda eng: eng.activation(isq[:], isq[:], AF.Exp, scale=-0.5),
            inc="s")
    E_("sync", lambda eng: eng.dma_start(out=wisq[:], in_=WRELC), inc="ldw", dma=True)
    E_("vector", lambda eng: eng.tensor_scalar_mul(gbq[:, 0:2], gb[:, 0:2], isq[:, 0:1]),
       waits=(("s", sv),))
    E_("vector", lambda eng: eng.tensor_mul(wisq[:], wisq[:], isq[:, 0:1]),
       waits=(("ldw", sch.cnt["ldw"]),))
    p0_v = E_("vector", lambda eng: eng.tensor_scalar_mul(gbq[:, 2:4], gb[:, 4:6], isq[:, 0:1]),
              inc="v")

    mark_p1 = len(guards)
    cosT = sb("cosT", [128, L], F32)
    sinT = sb("sinT", [128, L], F32)
    E_("sync", lambda eng: eng.dma_start(out=cosT[:], in_=COST), inc="trg", dma=True)
    E_("sync", lambda eng: eng.dma_start(out=sinT[:], in_=SINT), inc="trg", dma=True)
    uvb = [sb(f"uvb{i}", [128, E + S], BF16) for i in range(8)]
    xtm = [sb(f"xtm{i}", [128, L], F32) for i in range(3)]
    sqscr = sb("sqscr", [128, 1024], F32)
    xnt = [sb(f"xnt{i}", [128, L], BF16) for i in range(2)]
    xnt_used = [None, None]
    xnf = [sb(f"xnf{i}", [128, L], BF16) for i in range(2)]
    vout = [sb(f"vout{i}", [128, E], BF16) for i in range(3)]
    tq = sb("tq", [128, 512], F32)
    tk = sb("tk", [128, 512], F32)
    rt = [sb(f"rt{i}", [128, 512], F32) for i in range(2)]
    tqs = sb("tqs", [128, 512], F32)
    tks = sb("tks", [128, 512], F32)
    wst = [sb(f"wst{i}", [128, 2048], F32) for i in range(2)]
    stat = sb("stat", [128, 40], F32)
    btm = sb("btm", [128, 128], BF16)
    ones_f = sb("ones_f", [128, 1], F32)
    onesr = sb("onesr", [1, 128], F32)

    for dc in range(8):
        E_("gpsimd", (lambda dc: lambda eng: eng.dma_start(
            out=uvb[dc][:], in_=UVWT[128 * dc:128 * dc + 128, E:]))(dc),
           inc="uvb", dma=True)
    uvb_gd = sch.cnt["uvb"]
    for dc in range(8):
        E_("gpsimd", (lambda dc: lambda eng: eng.dma_start(
            out=UVUSCR[128 * dc:128 * dc + 128, :],
            in_=UVWT[128 * dc:128 * dc + 128, 0:E]))(dc),
           inc="uvc", dma=True)

    E_("vector", lambda eng: eng.memset(wacc[:], 0.0), inc="v")

    wslot_used = [None, None]
    wacc_v = [None]

    def w_scale_stream(i):
        slot = i % 2
        if i < 16:
            src = W1T[128 * (i // 2):128 * (i // 2) + 128,
                      2048 * (i % 2):2048 * (i % 2) + 2048]
            col = 0
        else:
            j = i - 16
            src = rap(W2T, 256 * j * D, [[D, 128], [128 * D, 2], [1, D]])
            col = 1
        dv = E_("gpsimd", (lambda s_, sl: lambda eng: eng.dma_start(out=wst[sl][:], in_=s_))(src, slot),
                waits=((wslot_used[slot],) if wslot_used[slot] else ()), inc=f"ws{slot}", dma=True)
        E_("vector", (lambda sl: lambda eng: eng.reduce_sum(
            stat[:, 36:37], wst[sl][:], axis=AX, apply_absolute_value=True))(slot),
           waits=((f"ws{slot}", dv),))
        wacc_v[0] = E_("vector", (lambda c_: lambda eng: eng.tensor_add(
            wacc[:, c_:c_ + 1], wacc[:, c_:c_ + 1], stat[:, 36:37]))(col), inc="v")
        wslot_used[slot] = ("v", wacc_v[0])

    xtm_used = [None, None, None]
    xnf_used = [None, None]
    vout_used = [None, None, None]
    tb_used = [None, None]
    vb_used = [None, None]
    bb_used = [None]
    b5_used = [None]
    rope_v = [None]

    s_norms = {}
    dx_store = {}
    stats_sv = [None]

    def p1_xdma(c):
        slot = c % 3
        src = rap(X, 512 * c * D, [[D, 128], [128 * D, 4], [1, D]])
        dx_store[c] = E_("sync", (lambda s_, sl: lambda eng: eng.dma_start(
            out=xtm[sl][:], in_=s_))(src, slot),
            waits=((xtm_used[slot],) if xtm_used[slot] else ()), inc=f"x{slot}", dma=True)

    def p1_stats_a(c):
        slot = c % 3
        dx = dx_store[c]

        for g4 in range(4):
            xt = xtm[slot][:, 1024 * g4:1024 * g4 + 1024]
            E_("vector", (lambda xt, g4: lambda eng: eng.reduce_sum(
                stat[:, 10 + g4:11 + g4], xt, axis=AX))(xt, g4),
               waits=((f"x{slot}", dx),) if g4 == 0 else ())
        vv = E_("vector", lambda eng: eng.tensor_scalar_mul(
            stat[:, 0:4], stat[:, 10:14], -1.0 / 1024.0), inc="v")
        for g4 in range(4):
            xt = xtm[slot][:, 1024 * g4:1024 * g4 + 1024]
            E_("scalar", (lambda xt, g4: lambda eng: eng.activation(
                sqscr[:], xt, AF.Square, bias=stat[:, g4:g4 + 1],
                accum_out=stat[:, 4 + g4:5 + g4]))(xt, g4),
                waits=(("v", vv),) if g4 == 0 else ())
        stats_sv[0] = sch.tag_last("scalar", "s")

    def p1_stats_b(c):
        slot = c % 3
        sv = stats_sv[0]
        E_("vector", lambda eng: eng.tensor_scalar_mul(
            stat[:, 8:12], stat[:, 4:8], 1.0 / 1024.0), waits=(("s", sv),))
        E_("vector", lambda eng: eng.tensor_scalar_add(stat[:, 12:16], stat[:, 8:12], EPS))
        E_("vector", lambda eng: eng.reciprocal(stat[:, 16:20], stat[:, 12:16]))
        E_("vector", lambda eng: eng.tensor_mul(
            stat[:, 20:24], stat[:, 8:12], stat[:, 16:20]))
        E_("vector", lambda eng: eng.tensor_scalar(
            stat[:, 24:28], stat[:, 12:16], -0.5, 1.5, ALU.mult, ALU.add))
        for _ in range(3):
            E_("vector", lambda eng: eng.tensor_mul(stat[:, 16:20], stat[:, 24:28], stat[:, 24:28]))
            E_("vector", lambda eng: eng.tensor_mul(stat[:, 16:20], stat[:, 16:20], stat[:, 12:16]))
            E_("vector", lambda eng: eng.tensor_scalar(
                stat[:, 16:20], stat[:, 16:20], -0.5, 1.5, ALU.mult, ALU.add))
            E_("vector", lambda eng: eng.tensor_mul(stat[:, 24:28], stat[:, 24:28], stat[:, 16:20]))
        E_("vector", lambda eng: eng.tensor_scalar(
            stat[:, 28:32], stat[:, 20:24], -0.5, 1.5, ALU.mult, ALU.add))
        for _ in range(2):
            E_("vector", lambda eng: eng.tensor_mul(stat[:, 16:20], stat[:, 28:32], stat[:, 28:32]))
            E_("vector", lambda eng: eng.tensor_mul(stat[:, 16:20], stat[:, 16:20], stat[:, 20:24]))
            E_("vector", lambda eng: eng.tensor_scalar(
                stat[:, 16:20], stat[:, 16:20], -0.5, 1.5, ALU.mult, ALU.add))
            E_("vector", lambda eng: eng.tensor_mul(stat[:, 28:32], stat[:, 28:32], stat[:, 16:20]))
        E_("vector", lambda eng: eng.tensor_mul(stat[:, 28:32], stat[:, 28:32], stat[:, 20:24]))
        E_("vector", lambda eng: eng.tensor_scalar_max(stat[:, 28:32], stat[:, 28:32], EPS))
        E_("vector", lambda eng: eng.reciprocal(stat[:, 28:32], stat[:, 28:32]))
        E_("vector", lambda eng: eng.tensor_scalar_mul(
            alpha[:, 4 * c:4 * c + 4], stat[:, 28:32], grep[:, 0:1]))
        E_("vector", lambda eng: eng.tensor_mul(
            stat[:, 32:36], stat[:, 0:4], stat[:, 24:28]), inc="v")
        xn_t = xnt[c % 2]
        wxt = xnt_used[c % 2]
        for g4 in range(4):
            xt = xtm[slot][:, 1024 * g4:1024 * g4 + 1024]
            E_("vector", (lambda xt, g4, xn_t: lambda eng: eng.tensor_scalar(
                xn_t[:, 1024 * g4:1024 * g4 + 1024], xt, stat[:, 24 + g4:25 + g4],
                stat[:, 32 + g4:33 + g4], ALU.mult, ALU.add))(xt, g4, xn_t), inc="v",
                waits=((wxt,) if (g4 == 0 and wxt) else ()))
        s_norms[c] = ("v", sch.cnt["v"])
        xtm_used[slot] = s_norms[c]
        for i in (4 * c, 4 * c + 1, 4 * c + 2, 4 * c + 3):
            if i < 32:
                w_scale_stream(i)

    v_evacs = {}
    gd_xns = {}

    def p1_compute1(c):
        s_norm = s_norms[c]
        xn_t = xnt[c % 2]
        fslot = c % 2
        wv = xnf_used[fslot] or ()
        xf = xnf[fslot]
        for dc in range(8):
            bank = psg[dc % 2]
            for g4 in range(4):
                wt = ()
                if g4 == 0:
                    wt = (s_norm,)
                    if tb_used[dc % 2] is not None:
                        wt = wt + (tb_used[dc % 2],)
                    if dc < 2:
                        wt = wt + wv
                    if c == 0 and dc == 0:
                        wt = wt + (("ldg", 16),)
                E_("tensor", (lambda dc, g4, bank, xn_t: lambda eng: eng.matmul(
                    bank[:, 128 * g4:128 * g4 + 128],
                    xn_t[:, 1024 * g4 + 128 * dc:1024 * g4 + 128 * dc + 128],
                    identb[:], start=True, stop=True))(dc, g4, bank, xn_t),
                   waits=wt)
            tv = sch.tag_last("tensor", "t")
            tb_used[dc % 2] = ("s", E_("scalar", (lambda dc, bank, xf: lambda eng: eng.copy(
                xf[:, 512 * dc:512 * dc + 512], bank[:]))(dc, bank, xf),
                waits=(("t", tv),), inc="s"))
        xnt_used[c % 2] = ("t", sch.cnt["t"])
        v_evac = ("s", sch.cnt["s"])
        v_evacs[c] = v_evac
        gd_xn = None
        gd_xns[c] = None
        if c < 4:
            dstxn = rap(XNSCR, 512 * c, [[OWN, 128], [128 * OWN, 8], [1, 512]])
            gd_xns[c] = E_("gpsimd", (lambda xf, dstxn: lambda eng: eng.dma_start(
                out=dstxn, in_=xf[:]))(xf, dstxn),
                waits=(v_evac,), inc=f"xw{c % 2}", dma=True)

        for j4 in range(4):
            g = 4 * c + j4
            for dc in range(8):
                wt = (("s", bb_used[0]),) if (dc == 0 and bb_used[0] is not None) else ()
                if j4 == 0 and dc == 0:
                    wt = wt + (v_evac,)
                if c == 0 and j4 == 0 and dc == 0:
                    wt = wt + (("uvb", uvb_gd),)
                E_("tensor", (lambda j4, dc, xf: lambda eng: eng.matmul(
                    psg[4][:, 0:128], xf[:, 512 * dc + 128 * j4:512 * dc + 128 * j4 + 128],
                    uvb[dc][:, E:E + S], start=(dc == 0), stop=(dc == 7)))(j4, dc, xf),
                   waits=wt)
            tv = sch.tag_last("tensor", "t")
            bb_used[0] = E_("scalar", (lambda g: lambda eng: eng.activation(
                btm[:], psg[4][:, 0:128], AF.Silu, scale=alpha[:, g:g + 1]))(g),
                waits=(("t", tv),), inc="s")
            wt = (("s", bb_used[0]),)
            if j4 == 0 and b5_used[0] is not None:
                wt = wt + (b5_used[0],)
            E_("tensor", (lambda j4: lambda eng: eng.matmul(
                psg[5][:, 128 * j4:128 * j4 + 128], btm[:], identb[:],
                start=True, stop=True))(j4),
               waits=wt)
            E_("tensor", (lambda j4: lambda eng: eng.matmul(
                psg[6][0:64, 128 * j4:128 * j4 + 128], btm[:, 64:128], identb[:],
                start=True, stop=True))(j4))
            E_("tensor", (lambda j4: lambda eng: eng.matmul(
                psg[6][64:128, 128 * j4:128 * j4 + 128], btm[:, 0:64], identb[:],
                start=True, stop=True))(j4))
        tv = sch.tag_last("tensor", "t")
        jcols = slice(512 * c, 512 * c + 512)
        E_("vector", lambda eng: eng.tensor_scalar(
            tk[:], psg[5][:], gb[:, 2:3], gb[:, 3:4], ALU.mult, ALU.add),
            waits=(("t", tv),))
        E_("vector", lambda eng: eng.tensor_scalar(
            tks[:], psg[6][:], gb[:, 6:7], gb[:, 7:8], ALU.mult, ALU.add))
        if c < 4:
            E_("vector", lambda eng: eng.tensor_scalar(
                tq[:], psg[5][:], gbq[:, 0:1], gbq[:, 1:2], ALU.mult, ALU.add))
            E_("vector", lambda eng: eng.tensor_scalar(
                tqs[:], psg[6][:], gbq[:, 2:3], gbq[:, 3:4], ALU.mult, ALU.add))
        b5_used[0] = ("v", sch.cnt["v"] + 1)
        cs = cosT[:, jcols]
        sn = sinT[:, jcols]

        def rope(dst, t_, ts_):
            E_("vector", (lambda: lambda eng: eng.tensor_mul(
                rt[0][0:64, :], t_[0:64, :], cs[0:64, :]))(),
               waits=((("trg", 32),) if c == 0 else ()), pre_drain=True)
            E_("vector", (lambda: lambda eng: eng.tensor_mul(
                rt[1][0:64, :], ts_[0:64, :], sn[0:64, :]))())
            E_("vector", (lambda: lambda eng: eng.tensor_sub(
                dst[0:64, :], rt[0][0:64, :], rt[1][0:64, :]))(), pre_drain=True)
            E_("vector", (lambda: lambda eng: eng.tensor_mul(
                rt[0][64:128, :], t_[64:128, :], cs[64:128, :]))())
            E_("vector", (lambda: lambda eng: eng.tensor_mul(
                rt[1][64:128, :], ts_[64:128, :], sn[64:128, :]))())
            return E_("vector", (lambda: lambda eng: eng.tensor_add(
                dst[64:128, :], rt[0][64:128, :], rt[1][64:128, :]))(), inc="v",
                pre_drain=True)

        rope(kbf[:, jcols], tk, tks)
        if c < 4:
            rope(qbf[:, jcols], tq, tqs)
        rope_v[0] = sch.cnt["v"]


    def p1_compute2(c):
        fslot = c % 2
        xf = xnf[fslot]
        v_evac = v_evacs[c]
        gd_xn = gd_xns[c]
        vslot = c % 3
        for j4 in range(4):
            g = 4 * c + j4
            for eb in range(4):
                bi = (j4 * 4 + eb) % 2
                bank = psg[2 + bi]
                for dc in range(8):
                    wt = ()
                    if dc == 0:
                        wt = (("s", vb_used[bi]),)
                        if j4 == 0 and eb == 0:
                            wt = wt + (v_evac, ("uvb", uvb_gd))
                    E_("tensor", (lambda j4, eb, dc, bank, xf: lambda eng: eng.matmul(
                        bank[:], xf[:, 512 * dc + 128 * j4:512 * dc + 128 * j4 + 128],
                        uvb[dc][:, 512 * eb:512 * eb + 512],
                        start=(dc == 0), stop=(dc == 7)))(j4, eb, dc, bank, xf),
                       waits=wt)
                tv = sch.tag_last("tensor", "t")
                wv2 = (("t", tv),)
                if eb == 0 and vout_used[vslot] is not None:
                    wv2 = wv2 + ((f"vo{vslot}", vout_used[vslot]),)
                vb_used[bi] = E_("scalar", (lambda bank, vslot, eb, g: lambda eng: eng.activation(
                    vout[vslot][:, 512 * eb:512 * eb + 512], bank[:], AF.Silu,
                    scale=alpha[:, g:g + 1]))(bank, vslot, eb, g),
                    waits=wv2, inc="s")
            sv = sch.cnt["s"]
            vout_used[vslot] = E_("gpsimd", (lambda vslot, g: lambda eng: eng.dma_start(
                out=VSCR[128 * g:128 * g + 128, :], in_=vout[vslot][:]))(vslot, g),
                waits=(("s", sv),), inc=f"vo{vslot}", dma=True)
            vslot = (vslot + 1) % 3

        xnf_used[fslot] = ((f"xw{c % 2}", gd_xn),) if gd_xn else ()
    p1_xdma(0)
    p1_xdma(1)
    p1_stats_a(0)
    p1_stats_b(0)
    for c in range(NCH):
        if c + 2 < NCH:
            p1_xdma(c + 2)
        if c + 1 < NCH:
            p1_stats_a(c + 1)
        p1_compute1(c)
        if c + 1 < NCH:
            p1_stats_b(c + 1)
        p1_compute2(c)

    E_("vector", lambda eng: eng.memset(ones_f[:], 1.0))
    vv = E_("vector", lambda eng: eng.memset(onesr[:], 1.0), inc="v")
    for col in range(2):
        E_("tensor", (lambda col: lambda eng: eng.matmul(
            psg[6][0:1, col:col + 1], wacc[:, col:col + 1], ones_f[:],
            start=True, stop=True))(col),
           waits=((("v", vv), ("v", wacc_v[0])) if col == 0 else ()))
    tv = sch.tag_last("tensor", "t")
    E_("vector", lambda eng: eng.tensor_scalar_mul(
        stat[0:1, 36:38], psg[6][0:1, 0:2], 1.0 / (D * FF)), waits=(("t", tv),))
    E_("vector", lambda eng: eng.tensor_scalar_max(stat[0:1, 36:38], stat[0:1, 36:38], 1e-8),
       pre_drain=True)
    vv = E_("vector", lambda eng: eng.reciprocal(stat[0:1, 36:38], stat[0:1, 36:38]), inc="v",
            pre_drain=True)
    E_("tensor", lambda eng: eng.matmul(psg[6][:, 4:6], onesr[:], stat[0:1, 36:38],
                                        start=True, stop=True),
       waits=(("v", vv),))
    tv = sch.tag_last("tensor", "t")
    E_("vector", lambda eng: eng.tensor_copy(wsc[:], psg[6][:, 4:6]),
       waits=(("t", tv),), inc="v")

    if dumps:
        sch.barrier()
    end_p1 = (("t", sch.cnt["t"]), ("v", sch.cnt["v"]), ("s", sch.cnt["s"]),
              ("xw0", sch.cnt["xw0"]), ("xw1", sch.cnt["xw1"]),
              ("vo0", sch.cnt["vo0"]), ("vo1", sch.cnt["vo1"]),
              ("vo2", sch.cnt["vo2"]), ("uvc", sch.cnt["uvc"]))
    free_from(mark_p1)

    if dumps:
        dscr = sb("dscr", [128, 4096], F32)
        dbb = sb("dbb", [128, 4096], BF16)
        dmw_last = [None]

        def dump_pair(rd_fn, wr_fn, w=4096):
            gv = E_("gpsimd", rd_fn, waits=((("dmw", dmw_last[0]),) if dmw_last[0] else ()),
                    inc="dmr", dma=True)
            vv = E_("vector", (lambda w: lambda eng: eng.tensor_copy(
                dscr[:, 0:w], dbb[:, 0:w]))(w), waits=(("dmr", gv),), inc="v")
            dmw_last[0] = E_("gpsimd", wr_fn, waits=(("v", vv),), inc="dmw", dma=True)

        if "k" in dumps:
            vv = E_("vector", lambda eng: eng.tensor_copy(dscr[:, 0:L], kbf[:]), inc="v")
            dmw_last[0] = E_("gpsimd", lambda eng: eng.dma_start(out=dumps["k"], in_=dscr[:, 0:L]),
               waits=(("v", vv),), inc="dmw", dma=True)
        if "q" in dumps:
            vv = E_("vector", lambda eng: eng.tensor_copy(dscr[:, 0:OWN], qbf[:]),
                    waits=(("dmw", dmw_last[0]),) if dmw_last[0] else (), inc="v")
            dmw_last[0] = E_("gpsimd", lambda eng: eng.dma_start(out=dumps["q"], in_=dscr[:, 0:OWN]),
               waits=(("v", vv),), inc="dmw", dma=True)
        if "alpha" in dumps:
            E_("gpsimd", lambda eng: eng.dma_start(out=dumps["alpha"], in_=alpha[:]),
               inc="dmw2", dma=True)
        if "xn" in dumps:
            for it in range(ITC):
                src = rap(XNSCR, 512 * it, [[OWN, 128], [128 * OWN, 8], [1, 512]])
                dst = rap(dumps["xn"], 512 * it, [[OWN, 128], [128 * OWN, 8], [1, 512]])
                dump_pair((lambda src: lambda eng: eng.dma_start(out=dbb[:], in_=src))(src),
                          (lambda dst: lambda eng: eng.dma_start(out=dst, in_=dscr[:]))(dst))
        if "v" in dumps:
            for g in range(NG):
                dump_pair((lambda g: lambda eng: eng.dma_start(
                              out=dbb[:, 0:E], in_=VSCR[128 * g:128 * g + 128, :]))(g),
                          (lambda g: lambda eng: eng.dma_start(
                              out=dumps["v"][128 * g:128 * g + 128, :], in_=dscr[:, 0:E]))(g), w=E)
        sch.barrier()
        free_from(mark_p1)

    mark_c = len(guards)
    kts = sb("kts", [128, 32 * 512], BF16)
    usb = sb("usb", [128, 4 * 2048], BF16)
    gtm = usb
    gfm = sb("gfm", [128, 16 * 512], BF16)
    vwin = [sb(f"vwin{i}", [128, 16 * 512], BF16) for i in range(3)]
    btile = [sb(f"btile{i}", [128, 512], F32) for i in range(2)]
    ttile = [sb(f"ttile{i}", [128, 512], F32) for i in range(2)]
    xnc = [sb(f"xnc{i}", [128, 8 * 512], BF16) for i in range(2)]
    uvu = [sb(f"uvu{i}", [128, 8 * 512], BF16) for i in range(2)]
    owin = [sb(f"owin{i}", [128, 16 * 128], BF16) for i in range(2)]
    sout = sb("sout", [128, 8 * 512], BF16)
    wtern = [sb("wtern0", [128, 2048], F32)] * 2
    wsAB = [sb(f"wsAB{i}", [128, 4096], BF16) for i in range(2)]

    tern_items = ([("w1", i, j) for i in range(8) for j in range(2)]
                  + [("w2", i, 0) for i in range(16)])
    tern = {"si": 0, "in_used": [None, None], "ab_used": [None, None],
            "tw": [None, None]}

    def tern_step():
        if tern["si"] >= len(tern_items):
            return
        mat, i, j = tern_items[tern["si"]]
        tern["si"] += 1
        if mat == "w1":
            src = W1T[128 * i:128 * i + 128, 2048 * j:2048 * j + 2048]
            dst = W1SCR[128 * i:128 * i + 128, 2048 * j:2048 * j + 2048]
            col = 0
        else:
            src = rap(W2T, 256 * i * D, [[D, 128], [128 * D, 2], [1, D]])
            dst = rap(W2SCR, 256 * i * D, [[D, 128], [128 * D, 2], [1, D]])
            col = 1
        ts_ = tern["si"] % 2
        wt_ = wtern[ts_]
        wab = wsAB[ts_]
        prev = tern["in_used"][1 - ts_]
        dv = E_("sync", (lambda s_, wt_: lambda eng: eng.dma_start(out=wt_[:], in_=s_))(src, wt_),
                waits=((("s", prev),) if prev else ()),
                inc=f"tn{ts_}", dma=True)
        E_("scalar", (lambda col, wt_, wab: lambda eng: eng.activation(
            wab[:, 0:2048], wt_[:], AF.Sign, bias=-0.5, scale=wsc[:, col:col + 1]))(col, wt_, wab),
           waits=((f"tn{ts_}", dv),) + ((("v", tern["ab_used"][ts_]),) if tern["ab_used"][ts_] else ())
                 + (((f"tw{ts_}", tern["tw"][ts_]),) if tern["tw"][ts_] else ()))
        tern["in_used"][ts_] = E_("scalar", (lambda col, wt_, wab: lambda eng: eng.activation(
            wab[:, 2048:4096], wt_[:], AF.Sign, bias=0.5, scale=wsc[:, col:col + 1]))(col, wt_, wab),
            inc="s")
        tern["ab_used"][ts_] = E_("vector", (lambda wab: lambda eng: eng.tensor_add(
            wab[:, 0:2048], wab[:, 0:2048], wab[:, 2048:4096]))(wab),
            waits=(("s", tern["in_used"][ts_]),)
                  + (((f"tw{ts_}", tern["tw"][ts_]),) if tern["tw"][ts_] else ()), inc="v")
        tern["tw"][ts_] = E_("gpsimd", (lambda dst, wab: lambda eng: eng.dma_start(
            out=dst, in_=wab[:, 0:2048]))(dst, wab),
           waits=(("v", tern["ab_used"][ts_]),), inc=f"tw{ts_}", dma=True)

    cst = {"xnc_used": None, "kts_used": None, "usb_used": None,
           "uvu_used": [None, None], "vwin_used": [None, None, None],
           "btile_used": [None] * 2, "ttile_used": [None] * 2,
           "sout_used": None, "xnc_used2": [None, None],
           "sc_bank": [None, None], "u_bank": [None, None],
           "ou_bank": [None] * 4, "tr_bank": None, "sp_bank": None,
           "owin_used": [None, None]}

    kt_store = {}

    def emit_scores(it_s):
        kt_v = [None] * 32
        icols_s = slice(512 * it_s, 512 * it_s + 512)
        for jt in range(32):
            bank = psg[jt % 2]
            wt = (("v", cst["sc_bank"][jt % 2]),)
            if jt == 0 and it_s == 0:
                wt = wt + end_p1 + (("t", cst["kts_used"]),)
            tv = E_("tensor", (lambda jt, bank, icols_s: lambda eng: eng.matmul(
                bank[:], kbf[:, 128 * jt:128 * jt + 128], qbf[:, icols_s],
                start=True, stop=True))(jt, bank, icols_s),
                waits=wt, inc="t")
            tslot = jt % 2
            variable = (16 <= jt <= 4 * it_s + 19)
            if variable:
                bslot = jt % 2
                wb_ = (("v", cst["btile_used"][bslot]),)
                if it_s == 0 and jt == 16:
                    wb_ = wb_ + end_p1
                b_dv = E_("sync", (lambda jt, bslot, icols_s: lambda eng: eng.dma_start(
                    out=btile[bslot][:], in_=BIAST[128 * jt:128 * jt + 128, icols_s]))(jt, bslot, icols_s),
                    waits=wb_, inc=f"bt{bslot}", dma=True)
                E_("vector", (lambda bslot, tslot, bank: lambda eng: eng.scalar_tensor_tensor(
                    ttile[tslot][:], btile[bslot][:], isq[:, 0:1], bank[:],
                    ALU.mult, ALU.add))(bslot, tslot, bank),
                   waits=(("t", tv), (f"bt{bslot}", b_dv))
                         + ((cst["ttile_used"][tslot],) if cst["ttile_used"][tslot] else ()))
            else:
                E_("vector", (lambda tslot, bank: lambda eng: eng.tensor_scalar_add(
                    ttile[tslot][:], bank[:], wisq[:, 0:1]))(tslot, bank),
                   waits=(("t", tv),)
                         + ((cst["ttile_used"][tslot],) if cst["ttile_used"][tslot] else ()))
            vv = E_("vector", (lambda jt, tslot: lambda eng: eng.tensor_scalar_max(
                ttile[tslot][:], ttile[tslot][:], 0.0))(jt, tslot), inc="v")
            sq = E_("scalar", (lambda jt, tslot: lambda eng: eng.activation(
                kts[:, 512 * jt:512 * jt + 512], ttile[tslot][:], AF.Square))(jt, tslot),
                waits=(("v", vv),), inc="s")
            kt_v[jt] = ("s", sq)
            if variable:
                cst["btile_used"][jt % 2] = vv
            cst["sc_bank"][jt % 2] = vv
            cst["ttile_used"][tslot] = ("s", sq)
        kt_store[it_s] = kt_v

    def phase_c_itile(it):
        icols = slice(512 * it, 512 * it + 512)
        xslot = it % 2
        xn = xnc[xslot]
        src = rap(XNSCR, 512 * it, [[OWN, 128], [128 * OWN, 8], [1, 512]])
        wx = (("v", cst["xnc_used2"][xslot]),)
        if it == 0:
            wx = wx + end_p1
        xn_dv = E_("sync", (lambda src, xn: lambda eng: eng.dma_start(out=xn[:], in_=src))(src, xn),
                   waits=wx, inc=f"xnc{xslot}", dma=True)

        for eb in range(4):
            uslot = eb % 2
            for dc in range(8):
                wt = (("t", cst["uvu_used"][uslot]),) if dc == 0 else ()
                if dc == 0 and eb == 0 and cst["uvu_used"][0] is None:
                    wt = wt + end_p1
                E_("sync", (lambda uslot, dc, eb: lambda eng: eng.dma_start(
                    out=uvu[uslot][:, 512 * dc:512 * dc + 512],
                    in_=UVUSCR[128 * dc:128 * dc + 128, 512 * eb:512 * eb + 512]))(uslot, dc, eb),
                   waits=wt, inc=f"uu{uslot}", dma=True)
            u_dv = sch.cnt[f"uu{uslot}"]
            for isub in range(4):
                bank = psg[2 + (isub % 2)]
                for dc in range(8):
                    wt = ()
                    if dc == 0:
                        wt = ((f"uu{eb % 2}", u_dv), (f"xnc{xslot}", xn_dv))
                        if cst["u_bank"][isub % 2] is not None:
                            wt = wt + (cst["u_bank"][isub % 2],)
                        if eb == 0 and isub == 0:
                            wt = wt + (("v", cst["usb_used"]),)
                    E_("tensor", (lambda isub, dc, bank, uslot, xn: lambda eng: eng.matmul(
                        bank[:], xn[:, 512 * dc + 128 * isub:512 * dc + 128 * isub + 128],
                        uvu[uslot][:, 512 * dc:512 * dc + 512],
                        start=(dc == 0), stop=(dc == 7)))(isub, dc, bank, uslot, xn),
                       waits=wt)
                tv = sch.tag_last("tensor", "t")
                if isub == 3:
                    cst["uvu_used"][uslot] = tv
                g = 4 * it + isub
                cst["u_bank"][isub % 2] = ("s", E_("scalar",
                    (lambda bank, isub, eb, g: lambda eng: eng.activation(
                        usb[:, 2048 * isub + 512 * eb:2048 * isub + 512 * eb + 512],
                        bank[:], AF.Silu, scale=alpha[:, g:g + 1]))(bank, isub, eb, g),
                    waits=(("t", tv),), inc="s"))
            for _ in range(2):
                tern_step()
        u_done_s = sch.cnt["s"]
        for eb in range(4):
            for half in range(2):
                w2i = (2 * eb + half) % 3
                vw = vwin[w2i]
                wt = ((cst["vwin_used"][w2i],) if cst["vwin_used"][w2i] else ())
                vq = "sync"
                for q4 in range(2):
                    src = rap(VSCR, (2048 * half + 1024 * q4) * E + 512 * eb,
                              [[E, 128], [128 * E, 8], [1, 512]])
                    E_(vq, (lambda src, q4, vw: lambda eng: eng.dma_start(
                        out=vw[:, 4096 * q4:4096 * q4 + 4096], in_=src))(src, q4, vw),
                       waits=(wt if q4 == 0 else ()), inc=f"vw{w2i}", dma=True)
                v_dv = sch.cnt[f"vw{w2i}"]
                for isub in range(4):
                    bank = psg[4 + isub]
                    for jt16 in range(16):
                        jt = 16 * half + jt16
                        wt2 = ()
                        if jt16 == 0:
                            wt2 = ((f"vw{w2i}", v_dv),)
                            if half == 0:
                                wt2 = wt2 + (("v", cst["ou_bank"][isub]),)
                        if isub == 0 and eb == 0:
                            wt2 = wt2 + (kt_store[it][jt],)
                        E_("tensor", (lambda isub, jt, jt16, half, bank, vw: lambda eng: eng.matmul(
                            bank[:], kts[:, 512 * jt + 128 * isub:512 * jt + 128 * isub + 128],
                            vw[:, 512 * jt16:512 * jt16 + 512],
                            start=(half == 0 and jt16 == 0),
                            stop=(half == 1 and jt16 == 15)))(isub, jt, jt16, half, bank, vw),
                           waits=wt2)
                tv = sch.tag_last("tensor", "t")
                cst["vwin_used"][w2i] = ("t", tv)
                ou_last = tv
            for isub in range(4):
                bank = psg[4 + isub]
                cst["ou_bank"][isub] = E_("vector",
                    (lambda bank, isub, eb: lambda eng: eng.tensor_mul(
                        gtm[:, 2048 * isub + 512 * eb:2048 * isub + 512 * eb + 512],
                        bank[:], usb[:, 2048 * isub + 512 * eb:2048 * isub + 512 * eb + 512]))(bank, isub, eb),
                    waits=(("t", ou_last), ("s", u_done_s)), inc="v")
        gate_done_v = sch.cnt["v"]
        cst["usb_used"] = gate_done_v

        if it + 1 < ITC:
            emit_scores(it + 1)

        for ec in range(16):
            bank = psg[4 + (ec % 2)]
            for isub in range(4):
                wt = ()
                if isub == 0:
                    wt = (("v", gate_done_v), ("v", cst["tr_bank"]))
                E_("tensor", (lambda ec, isub, bank: lambda eng: eng.matmul(
                    bank[:, 128 * isub:128 * isub + 128],
                    gtm[:, 2048 * isub + 128 * ec:2048 * isub + 128 * ec + 128],
                    identb[:], start=True, stop=True))(ec, isub, bank),
                   waits=wt)
            tv = sch.tag_last("tensor", "t")
            cst["tr_bank"] = E_("vector", (lambda ec, bank: lambda eng: eng.tensor_copy(
                gfm[:, 512 * ec:512 * ec + 512], bank[:]))(ec, bank),
                waits=(("t", tv),), inc="v")
        gfm_done_v = sch.cnt["v"]

        for dc in range(8):
            oslot = dc % 2
            for h2 in range(2):
                src = rap(OWT, 1024 * h2 * D + 128 * dc,
                          [[D, 128], [128 * D, 8], [1, 128]])
                wt = (("t", cst["owin_used"][oslot]),) if h2 == 0 else ()
                E_("gpsimd", (lambda src, oslot, h2: lambda eng: eng.dma_start(
                    out=owin[oslot][:, 1024 * h2:1024 * h2 + 1024], in_=src))(src, oslot, h2),
                   waits=wt, inc=f"ow{oslot}", dma=True)
            o_dv = sch.cnt[f"ow{oslot}"]
            bank = psg[6 + (dc % 2)]
            for ec in range(16):
                wt = ()
                if ec == 0:
                    wt = (("v", cst["sp_bank"] if dc >= 2 else None),
                          ((f"ow{dc % 2}"), o_dv), ("v", gfm_done_v))
                E_("tensor", (lambda dc, ec, oslot, bank: lambda eng: eng.matmul(
                    bank[:], owin[oslot][:, 128 * ec:128 * ec + 128],
                    gfm[:, 512 * ec:512 * ec + 512],
                    start=(ec == 0), stop=(ec == 15)))(dc, ec, oslot, bank),
                   waits=wt)
            tv = sch.tag_last("tensor", "t")
            cst["owin_used"][oslot] = tv
            wt = (("t", tv),)
            if dc == 0 and cst["sout_used"] is not None:
                wt = wt + (("so", cst["sout_used"]),)
            cst["sp_bank"] = E_("vector", (lambda dc, bank, xn: lambda eng: eng.scalar_tensor_tensor(
                sout[:, 512 * dc:512 * dc + 512], xn[:, 512 * dc:512 * dc + 512],
                rs[:, dc:dc + 1], bank[:], ALU.mult, ALU.add))(dc, bank, xn),
                waits=wt, inc="v")
        s_done_v = sch.cnt["v"]
        cst["xnc_used2"][xslot] = s_done_v
        cst["kts_used"] = sch.cnt["t"]
        cst["marks"] = {"t": sch.cnt["t"], "gfm_v": gfm_done_v, "us": u_done_s}
        cst["ou_bank"][0] = gfm_done_v
        cst["ou_bank"][1] = gfm_done_v
        cst["ou_bank"][2] = s_done_v
        cst["ou_bank"][3] = s_done_v
        dst = rap(SSCR, 512 * it, [[OWN, 128], [128 * OWN, 8], [1, 512]])
        cst["sout_used"] = E_("gpsimd", (lambda dst: lambda eng: eng.dma_start(
            out=dst, in_=sout[:]))(dst), waits=(("v", s_done_v),), inc="so", dma=True)

    if "c" in phases:
        emit_scores(0)
        for it in range(ITC):
            phase_c_itile(it)
        while tern["si"] < len(tern_items):
            tern_step()

    if dumps or "c" not in phases:
        sch.barrier()
    end_c = (("t", sch.cnt["t"]), ("v", sch.cnt["v"]), ("s", sch.cnt["s"]),
             ("tw0", sch.cnt["tw0"]), ("so", sch.cnt["so"]),
             ("tw1", sch.cnt["tw1"]))
    free_from(mark_c)

    if dumps and ("s" in dumps or "w1t" in dumps or "w2t" in dumps):
        dscr2 = sb("dscr2", [128, 4096], F32)
        dbb2 = sb("dbb2", [128, 4096], BF16)
        d2_last = [None]

        def dump2(rd_fn, wr_fn, mul):
            gv = E_("gpsimd", rd_fn, waits=((("dw2", d2_last[0]),) if d2_last[0] else ()),
                    inc="dr2", dma=True)
            if mul:
                vv = E_("vector", lambda eng: eng.tensor_scalar_mul(dscr2[:], dbb2[:], 0.5),
                        waits=(("dr2", gv),), inc="v")
            else:
                vv = E_("vector", lambda eng: eng.tensor_copy(dscr2[:], dbb2[:]),
                        waits=(("dr2", gv),), inc="v")
            d2_last[0] = E_("gpsimd", wr_fn, waits=(("v", vv),), inc="dw2", dma=True)

        if "s" in dumps:
            for it in range(ITC):
                src = rap(SSCR, 512 * it, [[OWN, 128], [128 * OWN, 8], [1, 512]])
                dst = rap(dumps["s"], 512 * it, [[OWN, 128], [128 * OWN, 8], [1, 512]])
                dump2((lambda src: lambda eng: eng.dma_start(out=dbb2[:], in_=src))(src),
                      (lambda dst: lambda eng: eng.dma_start(out=dst, in_=dscr2[:]))(dst), False)
        if "w1t" in dumps:
            for i in range(8):
                dump2((lambda i: lambda eng: eng.dma_start(
                          out=dbb2[:], in_=W1SCR[128 * i:128 * i + 128, :]))(i),
                      (lambda i: lambda eng: eng.dma_start(
                          out=dumps["w1t"][128 * i:128 * i + 128, :], in_=dscr2[:]))(i), True)
        if "w2t" in dumps:
            for i in range(8):
                src = rap(W2SCR, 512 * i * D, [[D, 128], [128 * D, 4], [1, D]])
                dst = rap(dumps["w2t"], 512 * i * D, [[D, 128], [128 * D, 4], [1, D]])
                dump2((lambda src: lambda eng: eng.dma_start(out=dbb2[:], in_=src))(src),
                      (lambda dst: lambda eng: eng.dma_start(out=dst, in_=dscr2[:]))(dst), True)
        sch.barrier()
        free_from(mark_c)

    w1sb = [sb(f"w1sb{i}", [128, FF], BF16) for i in range(8)]
    hsb = sb("hsb", [128, 32 * 512], BF16)
    w2win = [sb(f"w2win{i}", [128, 32 * 128], BF16) for i in range(2)]
    xnd = [sb(f"xnd{i}", [128, 8 * 512], BF16) for i in range(2)]
    std = sb("std", [128, 8 * 512], BF16)
    gsb = sb("gsb", [128, 8 * 512], F32)
    sgsb = sb("sgsb", [128, 8 * 512], F32)
    xin = [sb(f"xin{i}", [128, 1024], F32) for i in range(2)]
    osb = [sb(f"osb{i}", [128, 1024], F32) for i in range(2)]

    mk = cst["marks"] if "c" in phases else {"t": 0, "gfm_v": 0, "us": 0}
    w1_early = (("t", mk["t"]), ("v", mk["gfm_v"]), ("s", mk["us"]),
                ("tw0", min(8 * 16, sch.cnt["tw0"])),
                ("tw1", min(8 * 16, sch.cnt["tw1"])))
    if "d" in phases:
        for dc in range(8):
            E_("sync", (lambda dc: lambda eng: eng.dma_start(
                out=w1sb[dc][:], in_=W1SCR[128 * dc:128 * dc + 128, :]))(dc),
               waits=(w1_early if dc == 0 else ()), inc="w1l", dma=True)
    w1_dv = sch.cnt["w1l"]

    dst_ = {"xnd_used": [None, None], "std_used": None, "hsb_used": None,
            "gsb_used": None, "sgsb_used": None,
            "h_bank": [None, None], "o_bank": [None, None],
            "w2_used": [None, None], "tr_bank": [None, None],
            "xin_used": [None, None], "osb_used": [None, None]}

    def phase_d_itile(it):
        icols = slice(512 * it, 512 * it + 512)
        xdslot = it % 2
        xnd_ = xnd[xdslot]
        src = rap(XNSCR, 512 * it, [[OWN, 128], [128 * OWN, 8], [1, 512]])
        wxd = (("t", dst_["xnd_used"][xdslot]),)
        if it == 0:
            wxd = wxd + (end_c[0], end_c[1])
        xn_dv = E_("sync", (lambda src, xnd_: lambda eng: eng.dma_start(out=xnd_[:], in_=src))(src, xnd_),
                   waits=wxd, inc=f"xnd{xdslot}", dma=True)
        src = rap(SSCR, 512 * it, [[OWN, 128], [128 * OWN, 8], [1, 512]])
        wsd = (("v", dst_["std_used"]),)
        if it == 0:
            wsd = wsd + (end_c[4],)
        s_dv = E_("sync", (lambda src: lambda eng: eng.dma_start(out=std[:], in_=src))(src),
                  waits=wsd, inc="ssd", dma=True)

        for fc in range(32):
            bank = psg[fc % 2]
            for dc in range(8):
                wt = ()
                if dc == 0:
                    wt = (("s", dst_["h_bank"][fc % 2]), (f"xnd{xdslot}", xn_dv),
                          ("w1l", w1_dv))
                E_("tensor", (lambda fc, dc, bank, xnd_: lambda eng: eng.matmul(
                    bank[:], w1sb[dc][:, 128 * fc:128 * fc + 128],
                    xnd_[:, 512 * dc:512 * dc + 512],
                    start=(dc == 0), stop=(dc == 7)))(fc, dc, bank, xnd_),
                   waits=wt)
            tv = sch.tag_last("tensor", "t")
            wt = (("t", tv),)
            if fc == 0:
                wt = wt + (("t", dst_["hsb_used"]),)
            dst_["h_bank"][fc % 2] = E_("scalar", (lambda fc, bank: lambda eng: eng.activation(
                hsb[:, 512 * fc:512 * fc + 512], bank[:], AF.Gelu,
                bias=b1s[:, fc:fc + 1], scale=0.5))(fc, bank),
                waits=wt, inc="s")
        h_done_s = sch.cnt["s"]

        for dc in range(8):
            wslot = dc % 2
            for h2 in range(2):
                src = rap(W2SCR, 2048 * h2 * D + 128 * dc,
                          [[D, 128], [128 * D, 16], [1, 128]])
                wt = (("t", dst_["w2_used"][wslot]),) if h2 == 0 else ()
                if it == 0 and dc == 0 and h2 == 0:
                    wt = wt + (end_c[3], end_c[5])
                E_("gpsimd", (lambda src, wslot, h2: lambda eng: eng.dma_start(
                    out=w2win[wslot][:, 2048 * h2:2048 * h2 + 2048], in_=src))(src, wslot, h2),
                   waits=wt, inc=f"w2{wslot}", dma=True)
            w2_dv = sch.cnt[f"w2{wslot}"]
            bank = psg[2 + (dc % 2)]
            for fc in range(32):
                wt = ()
                if fc == 0:
                    wt = (("s", dst_["o_bank"][dc % 2]), (f"w2{dc % 2}", w2_dv), ("s", h_done_s))
                    if dc == 0:
                        wt = wt + (("v", dst_["gsb_used"]),)
                E_("tensor", (lambda dc, fc, wslot, bank: lambda eng: eng.matmul(
                    bank[:], w2win[wslot][:, 128 * fc:128 * fc + 128],
                    hsb[:, 512 * fc:512 * fc + 512],
                    start=(fc == 0), stop=(fc == 31)))(dc, fc, wslot, bank),
                   waits=wt)
            tv = sch.tag_last("tensor", "t")
            dst_["w2_used"][wslot] = tv
            wgs = (("t", tv),)
            if it == 0 and dc == 0:
                wgs = wgs + (end_c[4],)
            dst_["o_bank"][dc % 2] = E_("scalar", (lambda dc, bank: lambda eng: eng.activation(
                gsb[:, 512 * dc:512 * dc + 512], bank[:], AF.Silu,
                bias=b2s[:, dc:dc + 1], scale=0.5))(dc, bank),
                waits=wgs, inc="s")
        g_done_s = sch.cnt["s"]
        dst_["hsb_used"] = sch.cnt["t"]

        for dc in range(8):
            wt = ()
            if dc == 0:
                wt = (("s", g_done_s), ("ssd", s_dv), ("t", dst_["sgsb_used"]))
                if it == 0:
                    wt = wt + (end_c[3], end_c[4], end_c[5])
            E_("vector", (lambda dc: lambda eng: eng.tensor_mul(
                sgsb[:, 512 * dc:512 * dc + 512], gsb[:, 512 * dc:512 * dc + 512],
                std[:, 512 * dc:512 * dc + 512]))(dc), waits=wt)
        sg_done_v = sch.tag_last("vector", "v")
        dst_["std_used"] = sg_done_v
        dst_["gsb_used"] = sg_done_v

        for isub in range(4):
            i0 = 512 * it + 128 * isub
            xslot = isub % 2
            wxi = (("v", dst_["xin_used"][xslot]),)
            if it == 0 and isub == 0:
                wxi = wxi + (end_c[3], end_c[5])
            x_dv = E_("sync", (lambda i0, xslot: lambda eng: eng.dma_start(
                out=xin[xslot][:], in_=X[i0:i0 + 128, :]))(i0, xslot),
                waits=wxi, inc=f"xi{xslot}", dma=True)
            tb = isub % 2
            bankA = psg[4 + 2 * tb]
            bankB = psg[5 + 2 * tb]
            for dc in range(8):
                bank = bankA if dc < 4 else bankB
                off = 128 * (dc % 4)
                wt = ()
                if dc == 0:
                    wt = (("v", sg_done_v), ("v", dst_["tr_bank"][tb]))
                E_("tensor", (lambda dc, isub, bank, off: lambda eng: eng.matmul(
                    bank[:, off:off + 128],
                    sgsb[:, 512 * dc + 128 * isub:512 * dc + 128 * isub + 128],
                    identf[:], start=True, stop=True))(dc, isub, bank, off),
                   waits=wt)
            tv = sch.tag_last("tensor", "t")
            oslot = isub % 2
            E_("vector", (lambda bankA, oslot, xslot: lambda eng: eng.tensor_add(
                osb[oslot][:, 0:512], bankA[:], xin[xslot][:, 0:512]))(bankA, oslot, xslot),
               waits=(("t", tv), (f"xi{xslot}", x_dv))
                     + (((f"ou{oslot}", dst_["osb_used"][oslot]),)
                        if dst_["osb_used"][oslot] else ()))
            vv = E_("vector", (lambda bankB, oslot, xslot: lambda eng: eng.tensor_add(
                osb[oslot][:, 512:1024], bankB[:], xin[xslot][:, 512:1024]))(bankB, oslot, xslot),
               inc="v")
            dst_["tr_bank"][tb] = vv
            dst_["xin_used"][xslot] = vv
            dst_["osb_used"][oslot] = E_("sync", (lambda i0, oslot: lambda eng: eng.dma_start(
                out=OUT[i0:i0 + 128, :], in_=osb[oslot][:]))(i0, oslot),
                waits=(("v", vv),), inc=f"ou{oslot}", dma=True)
        dst_["xnd_used"][xdslot] = sch.cnt["t"]
        dst_["sgsb_used"] = sch.cnt["t"]

    if "d" in phases:
        for it in range(ITC):
            phase_d_itile(it)

    E_("sync", None, waits=(("ou0", sch.cnt["ou0"]), ("ou1", sch.cnt["ou1"])))

    from contextlib import ExitStack
    names = sorted(sch.cnt.keys())
    with ExitStack() as stack:
        SEM = {nm: stack.enter_context(nc.semaphore("sem_" + nm)) for nm in names}
        block = stack.enter_context(nc.Block())

        def replay(eng, name):
            for fn, waits, inc, n in sch.ops[name]:
                for snm, val in waits:
                    eng.wait_ge(SEM[snm], val)
                if fn is None:
                    continue
                inst = fn(eng)
                if inc is not None:
                    inst.then_inc(SEM[inc], n)

        @block.sync
        def _(eng):
            replay(eng, "sync")

        @block.gpsimd
        def _(eng):
            replay(eng, "gpsimd")

        @block.tensor
        def _(eng):
            replay(eng, "tensor")

        @block.vector
        def _(eng):
            replay(eng, "vector")

        @block.scalar
        def _(eng):
            replay(eng, "scalar")

    free_from(0)
    return nc


def prepare_inputs(L0_attn_x, L0_attn_g, L0_attn_uv_w, L0_attn_gamma,
                   L0_attn_beta, L0_attn_w_rel, L0_attn_o_w, L0_attn_res_scale,
                   L0_ffn_W1, L0_ffn_b1, L0_ffn_W2, L0_ffn_b2):
    f = np.float32
    x = np.asarray(L0_attn_x, f)
    uvwT = np.ascontiguousarray(np.asarray(L0_attn_uv_w, f).T)
    owT = np.ascontiguousarray(np.asarray(L0_attn_o_w, f).T)
    w1T = np.ascontiguousarray(np.asarray(L0_ffn_W1, f).T)
    w2T = np.ascontiguousarray(np.asarray(L0_ffn_W2, f).T)
    gamma = np.asarray(L0_attn_gamma, f)
    beta = np.asarray(L0_attn_beta, f)
    w_rel = np.asarray(L0_attn_w_rel, f)
    gsh = np.roll(gamma, -64, axis=1)
    bsh = np.roll(beta, -64, axis=1)
    gb = np.stack([gamma[0], beta[0], gamma[1], beta[1],
                   gsh[0], bsh[0], gsh[1], bsh[1]], axis=1).astype(f)
    gb = np.ascontiguousarray(gb)
    grep = np.broadcast_to(np.asarray(L0_attn_g, f).reshape(-1)[0:1].reshape(1, 1),
                           (128, 1)).copy()
    rs = np.ascontiguousarray(np.asarray(L0_attn_res_scale, f).reshape(8, 128).T)
    b1 = np.ascontiguousarray(np.asarray(L0_ffn_b1, f).reshape(32, 128).T)
    b2 = np.ascontiguousarray(np.asarray(L0_ffn_b2, f).reshape(8, 128).T)
    ident = np.eye(128, dtype=f)
    wrelc = np.broadcast_to(w_rel[2046:2047].reshape(1, 1), (128, 1)).astype(f).copy()
    consts = np.zeros((128, 8), f)
    consts[:, 0] = 128.0
    consts[:, 1] = 1e-5
    consts[:, 2] = 0.5
    consts[:, 3] = -0.5

    sin, cos = _trig_tables()

    per_h = {}
    for h in (0, 1):
        rot = np.r_[OWN * h:OWN * h + OWN,
                    (0 if h else OWN):(OWN if h else L)].astype(np.int64)
        idx = rot[:, None] - (OWN * h + np.arange(OWN))[None, :] + (L - 1)
        biasT = np.ascontiguousarray(
            np.take(w_rel, np.clip(idx, 0, w_rel.shape[0] - 1)).astype(f))
        cosT = np.ascontiguousarray(np.concatenate([cos[rot].T, cos[rot].T], axis=0))
        sinT = np.ascontiguousarray(np.concatenate([sin[rot].T, sin[rot].T], axis=0))
        per_h[h] = (rot, biasT, cosT, sinT)

    in_maps = []
    for c in range(8):
        b, h = c // 2, c % 2
        rot, biasT, cosT, sinT = per_h[h]
        xc = np.ascontiguousarray(x[b][rot])
        in_maps.append(dict(x=xc, uvwT=uvwT, owT=owT, w1T=w1T, w2T=w2T,
                            biasT=biasT, cosT=cosT, sinT=sinT, gb=gb, grep=grep,
                            rs=rs, b1=b1, b2=b2, ident=ident, consts=consts,
                            wrelc=wrelc))
    return in_maps


def kernel(**inputs):
    in_maps = prepare_inputs(**inputs)
    nc = build_program()
    res = run_bass_kernel_spmd(nc, in_maps, core_ids=list(range(8)))
    out = np.empty((B, L, D), np.float32)
    for c in range(8):
        b, h = c // 2, c % 2
        out[b, OWN * h:OWN * h + OWN] = res.results[c]["out"]
    return out


# version 21
# speedup vs baseline: 1.0509x; 1.0509x over previous
import numpy as np
import ml_dtypes
import concourse.bass as bass
import concourse.mybir as mybir
from concourse.ap import AP
from concourse.bass_utils import run_bass_kernel_spmd

F32 = mybir.dt.float32
BF16 = mybir.dt.bfloat16
AF = mybir.ActivationFunctionType
ALU = mybir.AluOpType
AX = mybir.AxisListType.X

B, L, D = 4, 4096, 1024
E, S, FF = 2048, 128, 4096
OWN = 2048
NCH = 8
NG = 32
ITC = 4
EPS = 1e-5

ENGINES = ("sync", "gpsimd", "tensor", "vector", "scalar")


class Sched:

    def __init__(self):
        from collections import defaultdict
        self.ops = {e: [] for e in ENGINES}
        self.cnt = defaultdict(int)
        self.dma_eng = {}

    def emit(self, eng, fn, waits=(), inc=None, dma=False, pre_drain=False):
        if fn is not None and (eng in ("vector", "scalar")
                               or (eng == "gpsimd" and not dma)):
            fn = (lambda f: lambda e_: (e_.drain(), f(e_))[1])(fn)
        val = None
        if inc is not None:
            self.cnt[inc] += 16 if dma else 1
            val = self.cnt[inc]
            if dma:
                self.dma_eng[inc] = eng
        for (s_, v_) in waits:
            assert v_ is None or isinstance(v_, int), (eng, s_, v_)
        ws = tuple((s_, v_) for (s_, v_) in waits if v_ is not None)
        self.ops[eng].append((fn, ws, inc, 16 if dma else 1))
        return val

    def tag_last(self, eng, inc, dma=False):
        self.cnt[inc] += 16 if dma else 1
        val = self.cnt[inc]
        fn, ws, inc0, n0 = self.ops[eng][-1]
        assert inc0 is None, inc0
        self.ops[eng][-1] = (fn, ws, inc, 16 if dma else 1)
        return val

    def barrier(self):
        for nm, eng in sorted(self.dma_eng.items()):
            self.ops[eng].append((None, ((nm, self.cnt[nm]),), None, 1))
        self.cnt["ph"] += len(ENGINES)
        target = self.cnt["ph"]
        for e in ENGINES:
            self.ops[e].append((lambda eng: eng.nop(), (), "ph", 1))
        for e in ENGINES:
            self.ops[e].append((None, (("ph", target),), None, 1))


def _trig_tables():
    try:
        import jax
        import jax.numpy as jnp
        cpu = jax.devices("cpu")[0]
        with jax.default_device(cpu):
            pos = jnp.arange(L, dtype=jnp.float32)[:, None]
            inv_freq = 10000.0 ** (jnp.arange(64, dtype=jnp.float32) / 64)
            t = pos * inv_freq[None, :]
            sin = np.asarray(jnp.sin(t))
            cos = np.asarray(jnp.cos(t))
    except Exception:
        inv = (10000.0 ** (np.arange(64, dtype=np.float64) / 64.0)).astype(np.float32)
        t = (np.arange(L, dtype=np.float32)[:, None] * inv[None, :]).astype(np.float32)
        sin = np.sin(t.astype(np.float64)).astype(np.float32)
        cos = np.cos(t.astype(np.float64)).astype(np.float32)
    return sin, cos



def rap(base, off, dims):
    return AP(base.tensor, off, [list(d) for d in dims])

def build_program(dump=(), phases=("p1", "c", "d")):
    nc = bass.Bass("TRN2", target_bir_lowering=False)

    X = nc.dram_tensor("x", [L, D], F32, kind="ExternalInput").ap()
    UVWT = nc.dram_tensor("uvwT", [D, 2 * E + S], F32, kind="ExternalInput").ap()
    OWT = nc.dram_tensor("owT", [E, D], F32, kind="ExternalInput").ap()
    W1T = nc.dram_tensor("w1T", [D, FF], F32, kind="ExternalInput").ap()
    W2T = nc.dram_tensor("w2T", [FF, D], F32, kind="ExternalInput").ap()
    BIAST = nc.dram_tensor("biasT", [L, OWN], F32, kind="ExternalInput").ap()
    COST = nc.dram_tensor("cosT", [128, L], F32, kind="ExternalInput").ap()
    SINT = nc.dram_tensor("sinT", [128, L], F32, kind="ExternalInput").ap()
    GB = nc.dram_tensor("gb", [128, 8], F32, kind="ExternalInput").ap()
    GREP = nc.dram_tensor("grep", [128, 1], F32, kind="ExternalInput").ap()
    RS = nc.dram_tensor("rs", [128, 8], F32, kind="ExternalInput").ap()
    B1 = nc.dram_tensor("b1", [128, 32], F32, kind="ExternalInput").ap()
    B2 = nc.dram_tensor("b2", [128, 8], F32, kind="ExternalInput").ap()
    IDENT = nc.dram_tensor("ident", [128, 128], F32, kind="ExternalInput").ap()
    CONSTS = nc.dram_tensor("consts", [128, 8], F32, kind="ExternalInput").ap()
    WRELC = nc.dram_tensor("wrelc", [128, 1], F32, kind="ExternalInput").ap()
    OUT = nc.dram_tensor("out", [OWN, D], F32, kind="ExternalOutput").ap()

    VSCR = nc.dram_tensor("v_scr", [L, E], BF16).ap()
    XNSCR = nc.dram_tensor("xn_scr", [D, OWN], BF16).ap()
    SSCR = nc.dram_tensor("s_scr", [D, OWN], BF16).ap()
    W1SCR = nc.dram_tensor("w1_scr", [D, FF], BF16).ap()
    UVUSCR = nc.dram_tensor("uvu_scr", [D, E], BF16).ap()
    W2SCR = nc.dram_tensor("w2_scr", [FF, D], BF16).ap()

    dumps = {}
    if dump:
        shapes = {"xn": [D, OWN], "v": [L, E], "k": [128, L], "q": [128, OWN],
                  "alpha": [128, 32], "s": [D, OWN], "w1t": [D, FF],
                  "w2t": [FF, D]}
        for name in dump:
            dumps[name] = nc.dram_tensor("dump_" + name, shapes[name], F32,
                                         kind="ExternalOutput").ap()

    sch = Sched()
    E_ = sch.emit
    guards = []

    def sb(name, shape, dtype):
        g = nc.sbuf_tensor("sb_" + name, shape, dtype)
        t = g.__enter__()
        guards.append(g)
        return t

    def free_from(mark):
        while len(guards) > mark:
            guards.pop().__exit__(None, None, None)

    consts = sb("consts", [128, 8], F32)
    gb = sb("gbv", [128, 8], F32)
    gbq = sb("gbq", [128, 4], F32)
    grep = sb("grep", [128, 1], F32)
    isq = sb("isq", [128, 1], F32)
    rs = sb("rs", [128, 8], F32)
    b1s = sb("b1s", [128, 32], F32)
    b2s = sb("b2s", [128, 8], F32)
    identb = sb("identb", [128, 128], BF16)
    identf = sb("identf", [128, 128], F32)
    alpha = sb("alpha", [128, NG], F32)
    kbf = sb("kbf", [128, L], BF16)
    qbf = sb("qbf", [128, OWN], BF16)
    wacc = sb("wacc", [128, 2], F32)
    wsc = sb("wsc", [128, 2], F32)
    wisq = sb("wisq", [128, 1], F32)

    nc.const_aps.register(F32, 0.5, consts[:, 2:3])
    nc.const_aps.register(F32, -0.5, consts[:, 3:4])

    ps = [nc.psum_tensor(f"ps{i}", [128, 512], F32) for i in range(8)]
    psg = []
    for p in ps:
        psg.append(p.__enter__())
        guards.append(p)

    for apdst, apsrc in ((consts[:], CONSTS), (gb[:], GB), (grep[:], GREP),
                         (rs[:], RS), (b1s[:], B1), (b2s[:], B2),
                         (identf[:], IDENT)):
        E_("sync", (lambda d_, s_: lambda eng: eng.dma_start(out=d_, in_=s_))(apdst, apsrc),
           inc="ld0", dma=True)
    d0 = sch.cnt["ld0"]
    E_("gpsimd", lambda eng: eng.dma_start(out=identb[:], in_=IDENT), inc="ldg", dma=True)
    E_("scalar", lambda eng: eng.activation(isq[:], consts[:, 0:1], AF.Ln),
       waits=(("ld0", d0),))
    E_("scalar", lambda eng: eng.drain())
    sv = E_("scalar", lambda eng: eng.activation(isq[:], isq[:], AF.Exp, scale=-0.5),
            inc="s")
    E_("sync", lambda eng: eng.dma_start(out=wisq[:], in_=WRELC), inc="ldw", dma=True)
    E_("vector", lambda eng: eng.tensor_scalar_mul(gbq[:, 0:2], gb[:, 0:2], isq[:, 0:1]),
       waits=(("s", sv),))
    E_("vector", lambda eng: eng.tensor_mul(wisq[:], wisq[:], isq[:, 0:1]),
       waits=(("ldw", sch.cnt["ldw"]),))
    p0_v = E_("vector", lambda eng: eng.tensor_scalar_mul(gbq[:, 2:4], gb[:, 4:6], isq[:, 0:1]),
              inc="v")

    mark_p1 = len(guards)
    cosT = sb("cosT", [128, L], F32)
    sinT = sb("sinT", [128, L], F32)
    E_("sync", lambda eng: eng.dma_start(out=cosT[:], in_=COST), inc="trg", dma=True)
    E_("sync", lambda eng: eng.dma_start(out=sinT[:], in_=SINT), inc="trg", dma=True)
    uvb = [sb(f"uvb{i}", [128, E + S], BF16) for i in range(8)]
    xtm = [sb(f"xtm{i}", [128, L], F32) for i in range(3)]
    sqscr = sb("sqscr", [128, 1024], F32)
    xnt = [sb(f"xnt{i}", [128, L], BF16) for i in range(2)]
    xnt_used = [None, None]
    xnf = [sb(f"xnf{i}", [128, L], BF16) for i in range(2)]
    vout = [sb(f"vout{i}", [128, E], BF16) for i in range(3)]
    tq = sb("tq", [128, 512], F32)
    tk = sb("tk", [128, 512], F32)
    rt = [sb(f"rt{i}", [128, 512], F32) for i in range(2)]
    tqs = sb("tqs", [128, 512], F32)
    tks = sb("tks", [128, 512], F32)
    wst = [sb(f"wst{i}", [128, 2048], F32) for i in range(2)]
    stat = sb("stat", [128, 40], F32)
    btm = sb("btm", [128, 128], BF16)
    ones_f = sb("ones_f", [128, 1], F32)
    onesr = sb("onesr", [1, 128], F32)

    for dc in range(8):
        E_("gpsimd", (lambda dc: lambda eng: eng.dma_start(
            out=uvb[dc][:], in_=UVWT[128 * dc:128 * dc + 128, E:]))(dc),
           inc="uvb", dma=True)
    uvb_gd = sch.cnt["uvb"]
    for dc in range(8):
        E_("gpsimd", (lambda dc: lambda eng: eng.dma_start(
            out=UVUSCR[128 * dc:128 * dc + 128, :],
            in_=UVWT[128 * dc:128 * dc + 128, 0:E]))(dc),
           inc="uvc", dma=True)

    E_("vector", lambda eng: eng.memset(wacc[:], 0.0), inc="v")

    wslot_used = [None, None]
    wacc_v = [None]

    def w_scale_stream(i):
        slot = i % 2
        if i < 16:
            src = W1T[128 * (i // 2):128 * (i // 2) + 128,
                      2048 * (i % 2):2048 * (i % 2) + 2048]
            col = 0
        else:
            j = i - 16
            src = rap(W2T, 256 * j * D, [[D, 128], [128 * D, 2], [1, D]])
            col = 1
        dv = E_("gpsimd", (lambda s_, sl: lambda eng: eng.dma_start(out=wst[sl][:], in_=s_))(src, slot),
                waits=((wslot_used[slot],) if wslot_used[slot] else ()), inc=f"ws{slot}", dma=True)
        E_("vector", (lambda sl: lambda eng: eng.reduce_sum(
            stat[:, 36:37], wst[sl][:], axis=AX, apply_absolute_value=True))(slot),
           waits=((f"ws{slot}", dv),))
        wacc_v[0] = E_("vector", (lambda c_: lambda eng: eng.tensor_add(
            wacc[:, c_:c_ + 1], wacc[:, c_:c_ + 1], stat[:, 36:37]))(col), inc="v")
        wslot_used[slot] = ("v", wacc_v[0])

    xtm_used = [None, None, None]
    xnf_used = [None, None]
    vout_used = [None, None, None]
    tb_used = [None, None]
    vb_used = [None, None]
    bb_used = [None]
    b5_used = [None]
    rope_v = [None]

    s_norms = {}
    dx_store = {}
    stats_sv = [None]

    def p1_xdma(c):
        slot = c % 3
        src = rap(X, 512 * c * D, [[D, 128], [128 * D, 4], [1, D]])
        dx_store[c] = E_("sync", (lambda s_, sl: lambda eng: eng.dma_start(
            out=xtm[sl][:], in_=s_))(src, slot),
            waits=((xtm_used[slot],) if xtm_used[slot] else ()), inc=f"x{slot}", dma=True)

    def p1_stats_a(c):
        slot = c % 3
        dx = dx_store[c]

        for g4 in range(4):
            xt = xtm[slot][:, 1024 * g4:1024 * g4 + 1024]
            E_("vector", (lambda xt, g4: lambda eng: eng.reduce_sum(
                stat[:, 10 + g4:11 + g4], xt, axis=AX))(xt, g4),
               waits=((f"x{slot}", dx),) if g4 == 0 else ())
        vv = E_("vector", lambda eng: eng.tensor_scalar_mul(
            stat[:, 0:4], stat[:, 10:14], -1.0 / 1024.0), inc="v")
        for g4 in range(4):
            xt = xtm[slot][:, 1024 * g4:1024 * g4 + 1024]
            E_("scalar", (lambda xt, g4: lambda eng: eng.activation(
                sqscr[:], xt, AF.Square, bias=stat[:, g4:g4 + 1],
                accum_out=stat[:, 4 + g4:5 + g4]))(xt, g4),
                waits=(("v", vv),) if g4 == 0 else ())
        stats_sv[0] = sch.tag_last("scalar", "s")

    def p1_stats_b(c):
        slot = c % 3
        sv = stats_sv[0]
        E_("vector", lambda eng: eng.tensor_scalar_mul(
            stat[:, 8:12], stat[:, 4:8], 1.0 / 1024.0), waits=(("s", sv),))
        E_("vector", lambda eng: eng.tensor_scalar_add(stat[:, 12:16], stat[:, 8:12], EPS))
        E_("vector", lambda eng: eng.reciprocal(stat[:, 16:20], stat[:, 12:16]))
        E_("vector", lambda eng: eng.tensor_mul(
            stat[:, 20:24], stat[:, 8:12], stat[:, 16:20]))
        E_("vector", lambda eng: eng.tensor_scalar(
            stat[:, 24:28], stat[:, 12:16], -0.5, 1.5, ALU.mult, ALU.add))
        for _ in range(3):
            E_("vector", lambda eng: eng.tensor_mul(stat[:, 16:20], stat[:, 24:28], stat[:, 24:28]))
            E_("vector", lambda eng: eng.tensor_mul(stat[:, 16:20], stat[:, 16:20], stat[:, 12:16]))
            E_("vector", lambda eng: eng.tensor_scalar(
                stat[:, 16:20], stat[:, 16:20], -0.5, 1.5, ALU.mult, ALU.add))
            E_("vector", lambda eng: eng.tensor_mul(stat[:, 24:28], stat[:, 24:28], stat[:, 16:20]))
        E_("vector", lambda eng: eng.tensor_scalar(
            stat[:, 28:32], stat[:, 20:24], -0.5, 1.5, ALU.mult, ALU.add))
        for _ in range(2):
            E_("vector", lambda eng: eng.tensor_mul(stat[:, 16:20], stat[:, 28:32], stat[:, 28:32]))
            E_("vector", lambda eng: eng.tensor_mul(stat[:, 16:20], stat[:, 16:20], stat[:, 20:24]))
            E_("vector", lambda eng: eng.tensor_scalar(
                stat[:, 16:20], stat[:, 16:20], -0.5, 1.5, ALU.mult, ALU.add))
            E_("vector", lambda eng: eng.tensor_mul(stat[:, 28:32], stat[:, 28:32], stat[:, 16:20]))
        E_("vector", lambda eng: eng.tensor_mul(stat[:, 28:32], stat[:, 28:32], stat[:, 20:24]))
        E_("vector", lambda eng: eng.tensor_scalar_max(stat[:, 28:32], stat[:, 28:32], EPS))
        E_("vector", lambda eng: eng.reciprocal(stat[:, 28:32], stat[:, 28:32]))
        E_("vector", lambda eng: eng.tensor_scalar_mul(
            alpha[:, 4 * c:4 * c + 4], stat[:, 28:32], grep[:, 0:1]))
        E_("vector", lambda eng: eng.tensor_mul(
            stat[:, 32:36], stat[:, 0:4], stat[:, 24:28]), inc="v")
        xn_t = xnt[c % 2]
        wxt = xnt_used[c % 2]
        for g4 in range(4):
            xt = xtm[slot][:, 1024 * g4:1024 * g4 + 1024]
            E_("vector", (lambda xt, g4, xn_t: lambda eng: eng.tensor_scalar(
                xn_t[:, 1024 * g4:1024 * g4 + 1024], xt, stat[:, 24 + g4:25 + g4],
                stat[:, 32 + g4:33 + g4], ALU.mult, ALU.add))(xt, g4, xn_t), inc="v",
                waits=((wxt,) if (g4 == 0 and wxt) else ()))
        s_norms[c] = ("v", sch.cnt["v"])
        xtm_used[slot] = s_norms[c]
        for i in (4 * c, 4 * c + 1, 4 * c + 2, 4 * c + 3):
            if i < 32:
                w_scale_stream(i)

    v_evacs = {}
    gd_xns = {}

    def p1_compute1(c):
        s_norm = s_norms[c]
        xn_t = xnt[c % 2]
        fslot = c % 2
        wv = xnf_used[fslot] or ()
        xf = xnf[fslot]
        for dc in range(8):
            bank = psg[dc % 2]
            for g4 in range(4):
                wt = ()
                if g4 == 0:
                    wt = (s_norm,)
                    if tb_used[dc % 2] is not None:
                        wt = wt + (tb_used[dc % 2],)
                    if dc < 2:
                        wt = wt + wv
                    if c == 0 and dc == 0:
                        wt = wt + (("ldg", 16),)
                E_("tensor", (lambda dc, g4, bank, xn_t: lambda eng: eng.matmul(
                    bank[:, 128 * g4:128 * g4 + 128],
                    xn_t[:, 1024 * g4 + 128 * dc:1024 * g4 + 128 * dc + 128],
                    identb[:], start=True, stop=True))(dc, g4, bank, xn_t),
                   waits=wt)
            tv = sch.tag_last("tensor", "t")
            tb_used[dc % 2] = ("s", E_("scalar", (lambda dc, bank, xf: lambda eng: eng.copy(
                xf[:, 512 * dc:512 * dc + 512], bank[:]))(dc, bank, xf),
                waits=(("t", tv),), inc="s"))
        xnt_used[c % 2] = ("t", sch.cnt["t"])
        v_evac = ("s", sch.cnt["s"])
        v_evacs[c] = v_evac
        gd_xn = None
        gd_xns[c] = None
        if c < 4:
            dstxn = rap(XNSCR, 512 * c, [[OWN, 128], [128 * OWN, 8], [1, 512]])
            gd_xns[c] = E_("gpsimd", (lambda xf, dstxn: lambda eng: eng.dma_start(
                out=dstxn, in_=xf[:]))(xf, dstxn),
                waits=(v_evac,), inc=f"xw{c % 2}", dma=True)

        for j4 in range(4):
            g = 4 * c + j4
            for dc in range(8):
                wt = (("s", bb_used[0]),) if (dc == 0 and bb_used[0] is not None) else ()
                if j4 == 0 and dc == 0:
                    wt = wt + (v_evac,)
                if c == 0 and j4 == 0 and dc == 0:
                    wt = wt + (("uvb", uvb_gd),)
                E_("tensor", (lambda j4, dc, xf: lambda eng: eng.matmul(
                    psg[4][:, 0:128], xf[:, 512 * dc + 128 * j4:512 * dc + 128 * j4 + 128],
                    uvb[dc][:, E:E + S], start=(dc == 0), stop=(dc == 7)))(j4, dc, xf),
                   waits=wt)
            tv = sch.tag_last("tensor", "t")
            bb_used[0] = E_("scalar", (lambda g: lambda eng: eng.activation(
                btm[:], psg[4][:, 0:128], AF.Silu, scale=alpha[:, g:g + 1]))(g),
                waits=(("t", tv),), inc="s")
            wt = (("s", bb_used[0]),)
            if j4 == 0 and b5_used[0] is not None:
                wt = wt + (b5_used[0],)
            E_("tensor", (lambda j4: lambda eng: eng.matmul(
                psg[5][:, 128 * j4:128 * j4 + 128], btm[:], identb[:],
                start=True, stop=True))(j4),
               waits=wt)
            E_("tensor", (lambda j4: lambda eng: eng.matmul(
                psg[6][0:64, 128 * j4:128 * j4 + 128], btm[:, 64:128], identb[:],
                start=True, stop=True))(j4))
            E_("tensor", (lambda j4: lambda eng: eng.matmul(
                psg[6][64:128, 128 * j4:128 * j4 + 128], btm[:, 0:64], identb[:],
                start=True, stop=True))(j4))
        tv = sch.tag_last("tensor", "t")
        jcols = slice(512 * c, 512 * c + 512)
        E_("vector", lambda eng: eng.tensor_scalar(
            tk[:], psg[5][:], gb[:, 2:3], gb[:, 3:4], ALU.mult, ALU.add),
            waits=(("t", tv),))
        E_("vector", lambda eng: eng.tensor_scalar(
            tks[:], psg[6][:], gb[:, 6:7], gb[:, 7:8], ALU.mult, ALU.add))
        if c < 4:
            E_("vector", lambda eng: eng.tensor_scalar(
                tq[:], psg[5][:], gbq[:, 0:1], gbq[:, 1:2], ALU.mult, ALU.add))
            E_("vector", lambda eng: eng.tensor_scalar(
                tqs[:], psg[6][:], gbq[:, 2:3], gbq[:, 3:4], ALU.mult, ALU.add))
        b5_used[0] = ("v", sch.cnt["v"] + 1)
        cs = cosT[:, jcols]
        sn = sinT[:, jcols]

        def rope(dst, t_, ts_):
            E_("vector", (lambda: lambda eng: eng.tensor_mul(
                rt[0][0:64, :], t_[0:64, :], cs[0:64, :]))(),
               waits=((("trg", 32),) if c == 0 else ()), pre_drain=True)
            E_("vector", (lambda: lambda eng: eng.tensor_mul(
                rt[1][0:64, :], ts_[0:64, :], sn[0:64, :]))())
            E_("vector", (lambda: lambda eng: eng.tensor_sub(
                dst[0:64, :], rt[0][0:64, :], rt[1][0:64, :]))(), pre_drain=True)
            E_("vector", (lambda: lambda eng: eng.tensor_mul(
                rt[0][64:128, :], t_[64:128, :], cs[64:128, :]))())
            E_("vector", (lambda: lambda eng: eng.tensor_mul(
                rt[1][64:128, :], ts_[64:128, :], sn[64:128, :]))())
            return E_("vector", (lambda: lambda eng: eng.tensor_add(
                dst[64:128, :], rt[0][64:128, :], rt[1][64:128, :]))(), inc="v",
                pre_drain=True)

        rope(kbf[:, jcols], tk, tks)
        if c < 4:
            rope(qbf[:, jcols], tq, tqs)
        rope_v[0] = sch.cnt["v"]


    def p1_compute2(c):
        fslot = c % 2
        xf = xnf[fslot]
        v_evac = v_evacs[c]
        gd_xn = gd_xns[c]
        vslot = c % 3
        for j4 in range(4):
            g = 4 * c + j4
            for eb in range(4):
                bi = (j4 * 4 + eb) % 2
                bank = psg[2 + bi]
                for dc in range(8):
                    wt = ()
                    if dc == 0:
                        wt = (("s", vb_used[bi]),)
                        if j4 == 0 and eb == 0:
                            wt = wt + (v_evac, ("uvb", uvb_gd))
                    E_("tensor", (lambda j4, eb, dc, bank, xf: lambda eng: eng.matmul(
                        bank[:], xf[:, 512 * dc + 128 * j4:512 * dc + 128 * j4 + 128],
                        uvb[dc][:, 512 * eb:512 * eb + 512],
                        start=(dc == 0), stop=(dc == 7)))(j4, eb, dc, bank, xf),
                       waits=wt)
                tv = sch.tag_last("tensor", "t")
                wv2 = (("t", tv),)
                if eb == 0 and vout_used[vslot] is not None:
                    wv2 = wv2 + ((f"vo{vslot}", vout_used[vslot]),)
                vb_used[bi] = E_("scalar", (lambda bank, vslot, eb, g: lambda eng: eng.activation(
                    vout[vslot][:, 512 * eb:512 * eb + 512], bank[:], AF.Silu,
                    scale=alpha[:, g:g + 1]))(bank, vslot, eb, g),
                    waits=wv2, inc="s")
            sv = sch.cnt["s"]
            vout_used[vslot] = E_("gpsimd", (lambda vslot, g: lambda eng: eng.dma_start(
                out=VSCR[128 * g:128 * g + 128, :], in_=vout[vslot][:]))(vslot, g),
                waits=(("s", sv),), inc=f"vo{vslot}", dma=True)
            vslot = (vslot + 1) % 3

        xnf_used[fslot] = ((f"xw{c % 2}", gd_xn),) if gd_xn else ()
    p1_xdma(0)
    p1_xdma(1)
    p1_stats_a(0)
    p1_stats_b(0)
    for c in range(NCH):
        if c + 2 < NCH:
            p1_xdma(c + 2)
        if c + 1 < NCH:
            p1_stats_a(c + 1)
        p1_compute1(c)
        if c + 1 < NCH:
            p1_stats_b(c + 1)
        p1_compute2(c)

    E_("vector", lambda eng: eng.memset(ones_f[:], 1.0))
    vv = E_("vector", lambda eng: eng.memset(onesr[:], 1.0), inc="v")
    for col in range(2):
        E_("tensor", (lambda col: lambda eng: eng.matmul(
            psg[6][0:1, col:col + 1], wacc[:, col:col + 1], ones_f[:],
            start=True, stop=True))(col),
           waits=((("v", vv), ("v", wacc_v[0])) if col == 0 else ()))
    tv = sch.tag_last("tensor", "t")
    E_("vector", lambda eng: eng.tensor_scalar_mul(
        stat[0:1, 36:38], psg[6][0:1, 0:2], 1.0 / (D * FF)), waits=(("t", tv),))
    E_("vector", lambda eng: eng.tensor_scalar_max(stat[0:1, 36:38], stat[0:1, 36:38], 1e-8),
       pre_drain=True)
    vv = E_("vector", lambda eng: eng.reciprocal(stat[0:1, 36:38], stat[0:1, 36:38]), inc="v",
            pre_drain=True)
    E_("tensor", lambda eng: eng.matmul(psg[6][:, 4:6], onesr[:], stat[0:1, 36:38],
                                        start=True, stop=True),
       waits=(("v", vv),))
    tv = sch.tag_last("tensor", "t")
    E_("vector", lambda eng: eng.tensor_copy(wsc[:], psg[6][:, 4:6]),
       waits=(("t", tv),), inc="v")

    if dumps:
        sch.barrier()
    end_p1 = (("t", sch.cnt["t"]), ("v", sch.cnt["v"]), ("s", sch.cnt["s"]),
              ("xw0", sch.cnt["xw0"]), ("xw1", sch.cnt["xw1"]),
              ("vo0", sch.cnt["vo0"]), ("vo1", sch.cnt["vo1"]),
              ("vo2", sch.cnt["vo2"]), ("uvc", sch.cnt["uvc"]))
    free_from(mark_p1)

    if dumps:
        dscr = sb("dscr", [128, 4096], F32)
        dbb = sb("dbb", [128, 4096], BF16)
        dmw_last = [None]

        def dump_pair(rd_fn, wr_fn, w=4096):
            gv = E_("gpsimd", rd_fn, waits=((("dmw", dmw_last[0]),) if dmw_last[0] else ()),
                    inc="dmr", dma=True)
            vv = E_("vector", (lambda w: lambda eng: eng.tensor_copy(
                dscr[:, 0:w], dbb[:, 0:w]))(w), waits=(("dmr", gv),), inc="v")
            dmw_last[0] = E_("gpsimd", wr_fn, waits=(("v", vv),), inc="dmw", dma=True)

        if "k" in dumps:
            vv = E_("vector", lambda eng: eng.tensor_copy(dscr[:, 0:L], kbf[:]), inc="v")
            dmw_last[0] = E_("gpsimd", lambda eng: eng.dma_start(out=dumps["k"], in_=dscr[:, 0:L]),
               waits=(("v", vv),), inc="dmw", dma=True)
        if "q" in dumps:
            vv = E_("vector", lambda eng: eng.tensor_copy(dscr[:, 0:OWN], qbf[:]),
                    waits=(("dmw", dmw_last[0]),) if dmw_last[0] else (), inc="v")
            dmw_last[0] = E_("gpsimd", lambda eng: eng.dma_start(out=dumps["q"], in_=dscr[:, 0:OWN]),
               waits=(("v", vv),), inc="dmw", dma=True)
        if "alpha" in dumps:
            E_("gpsimd", lambda eng: eng.dma_start(out=dumps["alpha"], in_=alpha[:]),
               inc="dmw2", dma=True)
        if "xn" in dumps:
            for it in range(ITC):
                src = rap(XNSCR, 512 * it, [[OWN, 128], [128 * OWN, 8], [1, 512]])
                dst = rap(dumps["xn"], 512 * it, [[OWN, 128], [128 * OWN, 8], [1, 512]])
                dump_pair((lambda src: lambda eng: eng.dma_start(out=dbb[:], in_=src))(src),
                          (lambda dst: lambda eng: eng.dma_start(out=dst, in_=dscr[:]))(dst))
        if "v" in dumps:
            for g in range(NG):
                dump_pair((lambda g: lambda eng: eng.dma_start(
                              out=dbb[:, 0:E], in_=VSCR[128 * g:128 * g + 128, :]))(g),
                          (lambda g: lambda eng: eng.dma_start(
                              out=dumps["v"][128 * g:128 * g + 128, :], in_=dscr[:, 0:E]))(g), w=E)
        sch.barrier()
        free_from(mark_p1)

    mark_c = len(guards)
    kts = sb("kts", [128, 32 * 512], BF16)
    usb = sb("usb", [128, 4 * 2048], BF16)
    gtm = usb
    gfm = sb("gfm", [128, 16 * 512], BF16)
    vwin = [sb(f"vwin{i}", [128, 16 * 512], BF16) for i in range(2)]
    btile = [sb(f"btile{i}", [128, 512], F32) for i in range(2)]
    ttile = [sb(f"ttile{i}", [128, 512], F32) for i in range(2)]
    xnc = [sb(f"xnc{i}", [128, 8 * 512], BF16) for i in range(2)]
    uvu = [sb(f"uvu{i}", [128, 8 * 512], BF16) for i in range(2)]
    owin = [sb(f"owin{i}", [128, 16 * 128], BF16) for i in range(2)]
    sout = sb("sout", [128, 8 * 512], BF16)
    wtern = [sb(f"wtern{i}", [128, 2048], F32) for i in range(2)]
    wsAB = [sb(f"wsAB{i}", [128, 4096], BF16) for i in range(2)]

    tern_items = ([("w1", i, j) for i in range(8) for j in range(2)]
                  + [("w2", i, 0) for i in range(16)])
    tern = {"si": 0, "in_used": [None, None], "ab_used": [None, None],
            "tw": [None, None]}

    def tern_step():
        if tern["si"] >= len(tern_items):
            return
        mat, i, j = tern_items[tern["si"]]
        tern["si"] += 1
        if mat == "w1":
            src = W1T[128 * i:128 * i + 128, 2048 * j:2048 * j + 2048]
            dst = W1SCR[128 * i:128 * i + 128, 2048 * j:2048 * j + 2048]
            col = 0
        else:
            src = rap(W2T, 256 * i * D, [[D, 128], [128 * D, 2], [1, D]])
            dst = rap(W2SCR, 256 * i * D, [[D, 128], [128 * D, 2], [1, D]])
            col = 1
        ts_ = tern["si"] % 2
        wt_ = wtern[ts_]
        wab = wsAB[ts_]
        dv = E_("sync", (lambda s_, wt_: lambda eng: eng.dma_start(out=wt_[:], in_=s_))(src, wt_),
                waits=((("s", tern["in_used"][ts_]),) if tern["in_used"][ts_] else ()),
                inc=f"tn{ts_}", dma=True)
        E_("scalar", (lambda col, wt_, wab: lambda eng: eng.activation(
            wab[:, 0:2048], wt_[:], AF.Sign, bias=-0.5, scale=wsc[:, col:col + 1]))(col, wt_, wab),
           waits=((f"tn{ts_}", dv),) + ((("v", tern["ab_used"][ts_]),) if tern["ab_used"][ts_] else ())
                 + (((f"tw{ts_}", tern["tw"][ts_]),) if tern["tw"][ts_] else ()))
        tern["in_used"][ts_] = E_("scalar", (lambda col, wt_, wab: lambda eng: eng.activation(
            wab[:, 2048:4096], wt_[:], AF.Sign, bias=0.5, scale=wsc[:, col:col + 1]))(col, wt_, wab),
            inc="s")
        tern["ab_used"][ts_] = E_("vector", (lambda wab: lambda eng: eng.tensor_add(
            wab[:, 0:2048], wab[:, 0:2048], wab[:, 2048:4096]))(wab),
            waits=(("s", tern["in_used"][ts_]),)
                  + (((f"tw{ts_}", tern["tw"][ts_]),) if tern["tw"][ts_] else ()), inc="v")
        tern["tw"][ts_] = E_("gpsimd", (lambda dst, wab: lambda eng: eng.dma_start(
            out=dst, in_=wab[:, 0:2048]))(dst, wab),
           waits=(("v", tern["ab_used"][ts_]),), inc=f"tw{ts_}", dma=True)

    cst = {"xnc_used": None, "kts_used": None, "usb_used": None,
           "uvu_used": [None, None], "vwin_used": [None, None],
           "btile_used": [None] * 2, "ttile_used": [None] * 2,
           "sout_used": None, "xnc_used2": [None, None],
           "sc_bank": [None, None], "u_bank": [None, None],
           "ou_bank": [None] * 4, "tr_bank": None, "sp_bank": None,
           "owin_used": [None, None]}

    kt_store = {}

    def emit_scores(it_s):
        kt_v = [None] * 32
        icols_s = slice(512 * it_s, 512 * it_s + 512)
        for jt in range(32):
            bank = psg[jt % 2]
            wt = (("v", cst["sc_bank"][jt % 2]),)
            if jt == 0 and it_s == 0:
                wt = wt + end_p1 + (("t", cst["kts_used"]),)
            tv = E_("tensor", (lambda jt, bank, icols_s: lambda eng: eng.matmul(
                bank[:], kbf[:, 128 * jt:128 * jt + 128], qbf[:, icols_s],
                start=True, stop=True))(jt, bank, icols_s),
                waits=wt, inc="t")
            tslot = jt % 2
            variable = (16 <= jt <= 4 * it_s + 19)
            if variable:
                bslot = jt % 2
                wb_ = (("v", cst["btile_used"][bslot]),)
                if it_s == 0 and jt == 16:
                    wb_ = wb_ + end_p1
                b_dv = E_("sync", (lambda jt, bslot, icols_s: lambda eng: eng.dma_start(
                    out=btile[bslot][:], in_=BIAST[128 * jt:128 * jt + 128, icols_s]))(jt, bslot, icols_s),
                    waits=wb_, inc=f"bt{bslot}", dma=True)
                E_("vector", (lambda bslot, tslot, bank: lambda eng: eng.scalar_tensor_tensor(
                    ttile[tslot][:], btile[bslot][:], isq[:, 0:1], bank[:],
                    ALU.mult, ALU.add))(bslot, tslot, bank),
                   waits=(("t", tv), (f"bt{bslot}", b_dv), ("v", cst["ttile_used"][tslot])))
            else:
                E_("vector", (lambda tslot, bank: lambda eng: eng.tensor_scalar_add(
                    ttile[tslot][:], bank[:], wisq[:, 0:1]))(tslot, bank),
                   waits=(("t", tv), ("v", cst["ttile_used"][tslot])))
            vv = E_("vector", (lambda jt, tslot: lambda eng: eng.scalar_tensor_tensor(
                kts[:, 512 * jt:512 * jt + 512], ttile[tslot][:], 0.0, ttile[tslot][:],
                ALU.max, ALU.mult))(jt, tslot), inc="v", pre_drain=True)
            if variable:
                cst["btile_used"][jt % 2] = vv
            cst["sc_bank"][jt % 2] = vv
            cst["ttile_used"][tslot] = vv
            kt_v[jt] = vv
        kt_store[it_s] = kt_v

    def phase_c_itile(it):
        icols = slice(512 * it, 512 * it + 512)
        xslot = it % 2
        xn = xnc[xslot]
        src = rap(XNSCR, 512 * it, [[OWN, 128], [128 * OWN, 8], [1, 512]])
        wx = (("v", cst["xnc_used2"][xslot]),)
        if it == 0:
            wx = wx + end_p1
        xn_dv = E_("sync", (lambda src, xn: lambda eng: eng.dma_start(out=xn[:], in_=src))(src, xn),
                   waits=wx, inc=f"xnc{xslot}", dma=True)

        for eb in range(4):
            uslot = eb % 2
            for dc in range(8):
                wt = (("t", cst["uvu_used"][uslot]),) if dc == 0 else ()
                if dc == 0 and eb == 0 and cst["uvu_used"][0] is None:
                    wt = wt + end_p1
                E_("sync", (lambda uslot, dc, eb: lambda eng: eng.dma_start(
                    out=uvu[uslot][:, 512 * dc:512 * dc + 512],
                    in_=UVUSCR[128 * dc:128 * dc + 128, 512 * eb:512 * eb + 512]))(uslot, dc, eb),
                   waits=wt, inc=f"uu{uslot}", dma=True)
            u_dv = sch.cnt[f"uu{uslot}"]
            for isub in range(4):
                bank = psg[2 + (isub % 2)]
                for dc in range(8):
                    wt = ()
                    if dc == 0:
                        wt = ((f"uu{eb % 2}", u_dv), (f"xnc{xslot}", xn_dv))
                        if cst["u_bank"][isub % 2] is not None:
                            wt = wt + (cst["u_bank"][isub % 2],)
                        if eb == 0 and isub == 0:
                            wt = wt + (("v", cst["usb_used"]),)
                    E_("tensor", (lambda isub, dc, bank, uslot, xn: lambda eng: eng.matmul(
                        bank[:], xn[:, 512 * dc + 128 * isub:512 * dc + 128 * isub + 128],
                        uvu[uslot][:, 512 * dc:512 * dc + 512],
                        start=(dc == 0), stop=(dc == 7)))(isub, dc, bank, uslot, xn),
                       waits=wt)
                tv = sch.tag_last("tensor", "t")
                if isub == 3:
                    cst["uvu_used"][uslot] = tv
                g = 4 * it + isub
                cst["u_bank"][isub % 2] = ("s", E_("scalar",
                    (lambda bank, isub, eb, g: lambda eng: eng.activation(
                        usb[:, 2048 * isub + 512 * eb:2048 * isub + 512 * eb + 512],
                        bank[:], AF.Silu, scale=alpha[:, g:g + 1]))(bank, isub, eb, g),
                    waits=(("t", tv),), inc="s"))
            for _ in range(2):
                tern_step()
        u_done_s = sch.cnt["s"]
        for eb in range(4):
            for half in range(2):
                w2i = (2 * eb + half) % 2
                vw = vwin[w2i]
                wt = ((cst["vwin_used"][w2i],) if cst["vwin_used"][w2i] else ())
                vq = "sync"
                for q4 in range(2):
                    src = rap(VSCR, (2048 * half + 1024 * q4) * E + 512 * eb,
                              [[E, 128], [128 * E, 8], [1, 512]])
                    E_(vq, (lambda src, q4, vw: lambda eng: eng.dma_start(
                        out=vw[:, 4096 * q4:4096 * q4 + 4096], in_=src))(src, q4, vw),
                       waits=(wt if q4 == 0 else ()), inc=f"vw{w2i}", dma=True)
                v_dv = sch.cnt[f"vw{w2i}"]
                for isub in range(4):
                    bank = psg[4 + isub]
                    for jt16 in range(16):
                        jt = 16 * half + jt16
                        wt2 = ()
                        if jt16 == 0:
                            wt2 = ((f"vw{w2i}", v_dv),)
                            if half == 0:
                                wt2 = wt2 + (("v", cst["ou_bank"][isub]),)
                        if isub == 0 and eb == 0:
                            wt2 = wt2 + (("v", kt_store[it][jt]),)
                        E_("tensor", (lambda isub, jt, jt16, half, bank, vw: lambda eng: eng.matmul(
                            bank[:], kts[:, 512 * jt + 128 * isub:512 * jt + 128 * isub + 128],
                            vw[:, 512 * jt16:512 * jt16 + 512],
                            start=(half == 0 and jt16 == 0),
                            stop=(half == 1 and jt16 == 15)))(isub, jt, jt16, half, bank, vw),
                           waits=wt2)
                tv = sch.tag_last("tensor", "t")
                cst["vwin_used"][w2i] = ("t", tv)
                ou_last = tv
            for isub in range(4):
                bank = psg[4 + isub]
                cst["ou_bank"][isub] = E_("vector",
                    (lambda bank, isub, eb: lambda eng: eng.tensor_mul(
                        gtm[:, 2048 * isub + 512 * eb:2048 * isub + 512 * eb + 512],
                        bank[:], usb[:, 2048 * isub + 512 * eb:2048 * isub + 512 * eb + 512]))(bank, isub, eb),
                    waits=(("t", ou_last), ("s", u_done_s)), inc="v")
        gate_done_v = sch.cnt["v"]
        cst["usb_used"] = gate_done_v

        if it + 1 < ITC:
            emit_scores(it + 1)

        for ec in range(16):
            bank = psg[4 + (ec % 2)]
            for isub in range(4):
                wt = ()
                if isub == 0:
                    wt = (("v", gate_done_v), ("v", cst["tr_bank"]))
                E_("tensor", (lambda ec, isub, bank: lambda eng: eng.matmul(
                    bank[:, 128 * isub:128 * isub + 128],
                    gtm[:, 2048 * isub + 128 * ec:2048 * isub + 128 * ec + 128],
                    identb[:], start=True, stop=True))(ec, isub, bank),
                   waits=wt)
            tv = sch.tag_last("tensor", "t")
            cst["tr_bank"] = E_("vector", (lambda ec, bank: lambda eng: eng.tensor_copy(
                gfm[:, 512 * ec:512 * ec + 512], bank[:]))(ec, bank),
                waits=(("t", tv),), inc="v")
        gfm_done_v = sch.cnt["v"]

        for dc in range(8):
            oslot = dc % 2
            for h2 in range(2):
                src = rap(OWT, 1024 * h2 * D + 128 * dc,
                          [[D, 128], [128 * D, 8], [1, 128]])
                wt = (("t", cst["owin_used"][oslot]),) if h2 == 0 else ()
                E_("gpsimd", (lambda src, oslot, h2: lambda eng: eng.dma_start(
                    out=owin[oslot][:, 1024 * h2:1024 * h2 + 1024], in_=src))(src, oslot, h2),
                   waits=wt, inc=f"ow{oslot}", dma=True)
            o_dv = sch.cnt[f"ow{oslot}"]
            bank = psg[6 + (dc % 2)]
            for ec in range(16):
                wt = ()
                if ec == 0:
                    wt = (("v", cst["sp_bank"] if dc >= 2 else None),
                          ((f"ow{dc % 2}"), o_dv), ("v", gfm_done_v))
                E_("tensor", (lambda dc, ec, oslot, bank: lambda eng: eng.matmul(
                    bank[:], owin[oslot][:, 128 * ec:128 * ec + 128],
                    gfm[:, 512 * ec:512 * ec + 512],
                    start=(ec == 0), stop=(ec == 15)))(dc, ec, oslot, bank),
                   waits=wt)
            tv = sch.tag_last("tensor", "t")
            cst["owin_used"][oslot] = tv
            wt = (("t", tv),)
            if dc == 0 and cst["sout_used"] is not None:
                wt = wt + (("so", cst["sout_used"]),)
            cst["sp_bank"] = E_("vector", (lambda dc, bank, xn: lambda eng: eng.scalar_tensor_tensor(
                sout[:, 512 * dc:512 * dc + 512], xn[:, 512 * dc:512 * dc + 512],
                rs[:, dc:dc + 1], bank[:], ALU.mult, ALU.add))(dc, bank, xn),
                waits=wt, inc="v")
        s_done_v = sch.cnt["v"]
        cst["xnc_used2"][xslot] = s_done_v
        cst["kts_used"] = sch.cnt["t"]
        cst["marks"] = {"t": sch.cnt["t"], "gfm_v": gfm_done_v, "us": u_done_s}
        cst["ou_bank"][0] = gfm_done_v
        cst["ou_bank"][1] = gfm_done_v
        cst["ou_bank"][2] = s_done_v
        cst["ou_bank"][3] = s_done_v
        dst = rap(SSCR, 512 * it, [[OWN, 128], [128 * OWN, 8], [1, 512]])
        cst["sout_used"] = E_("gpsimd", (lambda dst: lambda eng: eng.dma_start(
            out=dst, in_=sout[:]))(dst), waits=(("v", s_done_v),), inc="so", dma=True)

    if "c" in phases:
        emit_scores(0)
        for it in range(ITC):
            phase_c_itile(it)
        while tern["si"] < len(tern_items):
            tern_step()

    if dumps or "c" not in phases:
        sch.barrier()
    end_c = (("t", sch.cnt["t"]), ("v", sch.cnt["v"]), ("s", sch.cnt["s"]),
             ("tw0", sch.cnt["tw0"]), ("so", sch.cnt["so"]),
             ("tw1", sch.cnt["tw1"]))
    free_from(mark_c)

    if dumps and ("s" in dumps or "w1t" in dumps or "w2t" in dumps):
        dscr2 = sb("dscr2", [128, 4096], F32)
        dbb2 = sb("dbb2", [128, 4096], BF16)
        d2_last = [None]

        def dump2(rd_fn, wr_fn, mul):
            gv = E_("gpsimd", rd_fn, waits=((("dw2", d2_last[0]),) if d2_last[0] else ()),
                    inc="dr2", dma=True)
            if mul:
                vv = E_("vector", lambda eng: eng.tensor_scalar_mul(dscr2[:], dbb2[:], 0.5),
                        waits=(("dr2", gv),), inc="v")
            else:
                vv = E_("vector", lambda eng: eng.tensor_copy(dscr2[:], dbb2[:]),
                        waits=(("dr2", gv),), inc="v")
            d2_last[0] = E_("gpsimd", wr_fn, waits=(("v", vv),), inc="dw2", dma=True)

        if "s" in dumps:
            for it in range(ITC):
                src = rap(SSCR, 512 * it, [[OWN, 128], [128 * OWN, 8], [1, 512]])
                dst = rap(dumps["s"], 512 * it, [[OWN, 128], [128 * OWN, 8], [1, 512]])
                dump2((lambda src: lambda eng: eng.dma_start(out=dbb2[:], in_=src))(src),
                      (lambda dst: lambda eng: eng.dma_start(out=dst, in_=dscr2[:]))(dst), False)
        if "w1t" in dumps:
            for i in range(8):
                dump2((lambda i: lambda eng: eng.dma_start(
                          out=dbb2[:], in_=W1SCR[128 * i:128 * i + 128, :]))(i),
                      (lambda i: lambda eng: eng.dma_start(
                          out=dumps["w1t"][128 * i:128 * i + 128, :], in_=dscr2[:]))(i), True)
        if "w2t" in dumps:
            for i in range(8):
                src = rap(W2SCR, 512 * i * D, [[D, 128], [128 * D, 4], [1, D]])
                dst = rap(dumps["w2t"], 512 * i * D, [[D, 128], [128 * D, 4], [1, D]])
                dump2((lambda src: lambda eng: eng.dma_start(out=dbb2[:], in_=src))(src),
                      (lambda dst: lambda eng: eng.dma_start(out=dst, in_=dscr2[:]))(dst), True)
        sch.barrier()
        free_from(mark_c)

    w1sb = [sb(f"w1sb{i}", [128, FF], BF16) for i in range(8)]
    hsb = sb("hsb", [128, 32 * 512], BF16)
    w2win = [sb(f"w2win{i}", [128, 32 * 128], BF16) for i in range(2)]
    xnd = [sb(f"xnd{i}", [128, 8 * 512], BF16) for i in range(2)]
    std = sb("std", [128, 8 * 512], BF16)
    gsb = sb("gsb", [128, 8 * 512], F32)
    sgsb = sb("sgsb", [128, 8 * 512], F32)
    xin = [sb(f"xin{i}", [128, 1024], F32) for i in range(2)]
    osb = [sb(f"osb{i}", [128, 1024], F32) for i in range(2)]

    mk = cst["marks"] if "c" in phases else {"t": 0, "gfm_v": 0, "us": 0}
    w1_early = (("t", mk["t"]), ("v", mk["gfm_v"]), ("s", mk["us"]),
                ("tw0", min(8 * 16, sch.cnt["tw0"])),
                ("tw1", min(8 * 16, sch.cnt["tw1"])))
    if "d" in phases:
        for dc in range(8):
            E_("sync", (lambda dc: lambda eng: eng.dma_start(
                out=w1sb[dc][:], in_=W1SCR[128 * dc:128 * dc + 128, :]))(dc),
               waits=(w1_early if dc == 0 else ()), inc="w1l", dma=True)
    w1_dv = sch.cnt["w1l"]

    dst_ = {"xnd_used": [None, None], "std_used": None, "hsb_used": None,
            "gsb_used": None, "sgsb_used": None,
            "h_bank": [None, None], "o_bank": [None, None],
            "w2_used": [None, None], "tr_bank": [None, None],
            "xin_used": [None, None], "osb_used": [None, None]}

    def phase_d_itile(it):
        icols = slice(512 * it, 512 * it + 512)
        xdslot = it % 2
        xnd_ = xnd[xdslot]
        src = rap(XNSCR, 512 * it, [[OWN, 128], [128 * OWN, 8], [1, 512]])
        wxd = (("t", dst_["xnd_used"][xdslot]),)
        if it == 0:
            wxd = wxd + (end_c[0], end_c[1])
        xn_dv = E_("sync", (lambda src, xnd_: lambda eng: eng.dma_start(out=xnd_[:], in_=src))(src, xnd_),
                   waits=wxd, inc=f"xnd{xdslot}", dma=True)
        src = rap(SSCR, 512 * it, [[OWN, 128], [128 * OWN, 8], [1, 512]])
        wsd = (("v", dst_["std_used"]),)
        if it == 0:
            wsd = wsd + (end_c[4],)
        s_dv = E_("sync", (lambda src: lambda eng: eng.dma_start(out=std[:], in_=src))(src),
                  waits=wsd, inc="ssd", dma=True)

        for fc in range(32):
            bank = psg[fc % 2]
            for dc in range(8):
                wt = ()
                if dc == 0:
                    wt = (("s", dst_["h_bank"][fc % 2]), (f"xnd{xdslot}", xn_dv),
                          ("w1l", w1_dv))
                E_("tensor", (lambda fc, dc, bank, xnd_: lambda eng: eng.matmul(
                    bank[:], w1sb[dc][:, 128 * fc:128 * fc + 128],
                    xnd_[:, 512 * dc:512 * dc + 512],
                    start=(dc == 0), stop=(dc == 7)))(fc, dc, bank, xnd_),
                   waits=wt)
            tv = sch.tag_last("tensor", "t")
            wt = (("t", tv),)
            if fc == 0:
                wt = wt + (("t", dst_["hsb_used"]),)
            dst_["h_bank"][fc % 2] = E_("scalar", (lambda fc, bank: lambda eng: eng.activation(
                hsb[:, 512 * fc:512 * fc + 512], bank[:], AF.Gelu,
                bias=b1s[:, fc:fc + 1], scale=0.5))(fc, bank),
                waits=wt, inc="s")
        h_done_s = sch.cnt["s"]

        for dc in range(8):
            wslot = dc % 2
            for h2 in range(2):
                src = rap(W2SCR, 2048 * h2 * D + 128 * dc,
                          [[D, 128], [128 * D, 16], [1, 128]])
                wt = (("t", dst_["w2_used"][wslot]),) if h2 == 0 else ()
                if it == 0 and dc == 0 and h2 == 0:
                    wt = wt + (end_c[3], end_c[5])
                E_("gpsimd", (lambda src, wslot, h2: lambda eng: eng.dma_start(
                    out=w2win[wslot][:, 2048 * h2:2048 * h2 + 2048], in_=src))(src, wslot, h2),
                   waits=wt, inc=f"w2{wslot}", dma=True)
            w2_dv = sch.cnt[f"w2{wslot}"]
            bank = psg[2 + (dc % 2)]
            for fc in range(32):
                wt = ()
                if fc == 0:
                    wt = (("s", dst_["o_bank"][dc % 2]), (f"w2{dc % 2}", w2_dv), ("s", h_done_s))
                    if dc == 0:
                        wt = wt + (("v", dst_["gsb_used"]),)
                E_("tensor", (lambda dc, fc, wslot, bank: lambda eng: eng.matmul(
                    bank[:], w2win[wslot][:, 128 * fc:128 * fc + 128],
                    hsb[:, 512 * fc:512 * fc + 512],
                    start=(fc == 0), stop=(fc == 31)))(dc, fc, wslot, bank),
                   waits=wt)
            tv = sch.tag_last("tensor", "t")
            dst_["w2_used"][wslot] = tv
            wgs = (("t", tv),)
            if it == 0 and dc == 0:
                wgs = wgs + (end_c[4],)
            dst_["o_bank"][dc % 2] = E_("scalar", (lambda dc, bank: lambda eng: eng.activation(
                gsb[:, 512 * dc:512 * dc + 512], bank[:], AF.Silu,
                bias=b2s[:, dc:dc + 1], scale=0.5))(dc, bank),
                waits=wgs, inc="s")
        g_done_s = sch.cnt["s"]
        dst_["hsb_used"] = sch.cnt["t"]

        for dc in range(8):
            wt = ()
            if dc == 0:
                wt = (("s", g_done_s), ("ssd", s_dv), ("t", dst_["sgsb_used"]))
                if it == 0:
                    wt = wt + (end_c[3], end_c[4], end_c[5])
            E_("vector", (lambda dc: lambda eng: eng.tensor_mul(
                sgsb[:, 512 * dc:512 * dc + 512], gsb[:, 512 * dc:512 * dc + 512],
                std[:, 512 * dc:512 * dc + 512]))(dc), waits=wt)
        sg_done_v = sch.tag_last("vector", "v")
        dst_["std_used"] = sg_done_v
        dst_["gsb_used"] = sg_done_v

        for isub in range(4):
            i0 = 512 * it + 128 * isub
            xslot = isub % 2
            wxi = (("v", dst_["xin_used"][xslot]),)
            if it == 0 and isub == 0:
                wxi = wxi + (end_c[3], end_c[5])
            x_dv = E_("sync", (lambda i0, xslot: lambda eng: eng.dma_start(
                out=xin[xslot][:], in_=X[i0:i0 + 128, :]))(i0, xslot),
                waits=wxi, inc=f"xi{xslot}", dma=True)
            tb = isub % 2
            bankA = psg[4 + 2 * tb]
            bankB = psg[5 + 2 * tb]
            for dc in range(8):
                bank = bankA if dc < 4 else bankB
                off = 128 * (dc % 4)
                wt = ()
                if dc == 0:
                    wt = (("v", sg_done_v), ("v", dst_["tr_bank"][tb]))
                E_("tensor", (lambda dc, isub, bank, off: lambda eng: eng.matmul(
                    bank[:, off:off + 128],
                    sgsb[:, 512 * dc + 128 * isub:512 * dc + 128 * isub + 128],
                    identf[:], start=True, stop=True))(dc, isub, bank, off),
                   waits=wt)
            tv = sch.tag_last("tensor", "t")
            oslot = isub % 2
            E_("vector", (lambda bankA, oslot, xslot: lambda eng: eng.tensor_add(
                osb[oslot][:, 0:512], bankA[:], xin[xslot][:, 0:512]))(bankA, oslot, xslot),
               waits=(("t", tv), (f"xi{xslot}", x_dv))
                     + (((f"ou{oslot}", dst_["osb_used"][oslot]),)
                        if dst_["osb_used"][oslot] else ()))
            vv = E_("vector", (lambda bankB, oslot, xslot: lambda eng: eng.tensor_add(
                osb[oslot][:, 512:1024], bankB[:], xin[xslot][:, 512:1024]))(bankB, oslot, xslot),
               inc="v")
            dst_["tr_bank"][tb] = vv
            dst_["xin_used"][xslot] = vv
            dst_["osb_used"][oslot] = E_("sync", (lambda i0, oslot: lambda eng: eng.dma_start(
                out=OUT[i0:i0 + 128, :], in_=osb[oslot][:]))(i0, oslot),
                waits=(("v", vv),), inc=f"ou{oslot}", dma=True)
        dst_["xnd_used"][xdslot] = sch.cnt["t"]
        dst_["sgsb_used"] = sch.cnt["t"]

    if "d" in phases:
        for it in range(ITC):
            phase_d_itile(it)

    E_("sync", None, waits=(("ou0", sch.cnt["ou0"]), ("ou1", sch.cnt["ou1"])))

    from contextlib import ExitStack
    names = sorted(sch.cnt.keys())
    with ExitStack() as stack:
        SEM = {nm: stack.enter_context(nc.semaphore("sem_" + nm)) for nm in names}
        block = stack.enter_context(nc.Block())

        def replay(eng, name):
            for fn, waits, inc, n in sch.ops[name]:
                for snm, val in waits:
                    eng.wait_ge(SEM[snm], val)
                if fn is None:
                    continue
                inst = fn(eng)
                if inc is not None:
                    inst.then_inc(SEM[inc], n)

        @block.sync
        def _(eng):
            replay(eng, "sync")

        @block.gpsimd
        def _(eng):
            replay(eng, "gpsimd")

        @block.tensor
        def _(eng):
            replay(eng, "tensor")

        @block.vector
        def _(eng):
            replay(eng, "vector")

        @block.scalar
        def _(eng):
            replay(eng, "scalar")

    free_from(0)
    return nc


def prepare_inputs(L0_attn_x, L0_attn_g, L0_attn_uv_w, L0_attn_gamma,
                   L0_attn_beta, L0_attn_w_rel, L0_attn_o_w, L0_attn_res_scale,
                   L0_ffn_W1, L0_ffn_b1, L0_ffn_W2, L0_ffn_b2):
    f = np.float32
    x = np.asarray(L0_attn_x, f)
    uvwT = np.ascontiguousarray(np.asarray(L0_attn_uv_w, f).T)
    owT = np.ascontiguousarray(np.asarray(L0_attn_o_w, f).T)
    w1T = np.ascontiguousarray(np.asarray(L0_ffn_W1, f).T)
    w2T = np.ascontiguousarray(np.asarray(L0_ffn_W2, f).T)
    gamma = np.asarray(L0_attn_gamma, f)
    beta = np.asarray(L0_attn_beta, f)
    w_rel = np.asarray(L0_attn_w_rel, f)
    gsh = np.roll(gamma, -64, axis=1)
    bsh = np.roll(beta, -64, axis=1)
    gb = np.stack([gamma[0], beta[0], gamma[1], beta[1],
                   gsh[0], bsh[0], gsh[1], bsh[1]], axis=1).astype(f)
    gb = np.ascontiguousarray(gb)
    grep = np.broadcast_to(np.asarray(L0_attn_g, f).reshape(-1)[0:1].reshape(1, 1),
                           (128, 1)).copy()
    rs = np.ascontiguousarray(np.asarray(L0_attn_res_scale, f).reshape(8, 128).T)
    b1 = np.ascontiguousarray(np.asarray(L0_ffn_b1, f).reshape(32, 128).T)
    b2 = np.ascontiguousarray(np.asarray(L0_ffn_b2, f).reshape(8, 128).T)
    ident = np.eye(128, dtype=f)
    wrelc = np.broadcast_to(w_rel[2046:2047].reshape(1, 1), (128, 1)).astype(f).copy()
    consts = np.zeros((128, 8), f)
    consts[:, 0] = 128.0
    consts[:, 1] = 1e-5
    consts[:, 2] = 0.5
    consts[:, 3] = -0.5

    sin, cos = _trig_tables()

    per_h = {}
    for h in (0, 1):
        rot = np.r_[OWN * h:OWN * h + OWN,
                    (0 if h else OWN):(OWN if h else L)].astype(np.int64)
        idx = rot[:, None] - (OWN * h + np.arange(OWN))[None, :] + (L - 1)
        biasT = np.ascontiguousarray(
            np.take(w_rel, np.clip(idx, 0, w_rel.shape[0] - 1)).astype(f))
        cosT = np.ascontiguousarray(np.concatenate([cos[rot].T, cos[rot].T], axis=0))
        sinT = np.ascontiguousarray(np.concatenate([sin[rot].T, sin[rot].T], axis=0))
        per_h[h] = (rot, biasT, cosT, sinT)

    in_maps = []
    for c in range(8):
        b, h = c // 2, c % 2
        rot, biasT, cosT, sinT = per_h[h]
        xc = np.ascontiguousarray(x[b][rot])
        in_maps.append(dict(x=xc, uvwT=uvwT, owT=owT, w1T=w1T, w2T=w2T,
                            biasT=biasT, cosT=cosT, sinT=sinT, gb=gb, grep=grep,
                            rs=rs, b1=b1, b2=b2, ident=ident, consts=consts,
                            wrelc=wrelc))
    return in_maps


def kernel(**inputs):
    in_maps = prepare_inputs(**inputs)
    nc = build_program()
    res = run_bass_kernel_spmd(nc, in_maps, core_ids=list(range(8)))
    out = np.empty((B, L, D), np.float32)
    for c in range(8):
        b, h = c // 2, c % 2
        out[b, OWN * h:OWN * h + OWN] = res.results[c]["out"]
    return out


# version 23
# speedup vs baseline: 1.1269x; 1.0723x over previous
import numpy as np
import ml_dtypes
import concourse.bass as bass
import concourse.mybir as mybir
from concourse.ap import AP
from concourse.bass_utils import run_bass_kernel_spmd

F32 = mybir.dt.float32
BF16 = mybir.dt.bfloat16
AF = mybir.ActivationFunctionType
ALU = mybir.AluOpType
AX = mybir.AxisListType.X

B, L, D = 4, 4096, 1024
E, S, FF = 2048, 128, 4096
OWN = 2048
NCH = 8
NG = 32
ITC = 4
EPS = 1e-5

ENGINES = ("sync", "gpsimd", "tensor", "vector", "scalar")


class Sched:

    def __init__(self):
        from collections import defaultdict
        self.ops = {e: [] for e in ENGINES}
        self.cnt = defaultdict(int)
        self.dma_eng = {}

    def emit(self, eng, fn, waits=(), inc=None, dma=False, pre_drain=False):
        if fn is not None and (eng in ("vector", "scalar")
                               or (eng == "gpsimd" and not dma)):
            fn = (lambda f: lambda e_: (e_.drain(), f(e_))[1])(fn)
        val = None
        if inc is not None:
            self.cnt[inc] += 16 if dma else 1
            val = self.cnt[inc]
            if dma:
                self.dma_eng[inc] = eng
        for (s_, v_) in waits:
            assert v_ is None or isinstance(v_, int), (eng, s_, v_)
        ws = tuple((s_, v_) for (s_, v_) in waits if v_ is not None)
        self.ops[eng].append((fn, ws, inc, 16 if dma else 1))
        return val

    def tag_last(self, eng, inc, dma=False):
        self.cnt[inc] += 16 if dma else 1
        val = self.cnt[inc]
        fn, ws, inc0, n0 = self.ops[eng][-1]
        assert inc0 is None, inc0
        self.ops[eng][-1] = (fn, ws, inc, 16 if dma else 1)
        return val

    def barrier(self):
        for nm, eng in sorted(self.dma_eng.items()):
            self.ops[eng].append((None, ((nm, self.cnt[nm]),), None, 1))
        self.cnt["ph"] += len(ENGINES)
        target = self.cnt["ph"]
        for e in ENGINES:
            self.ops[e].append((lambda eng: eng.nop(), (), "ph", 1))
        for e in ENGINES:
            self.ops[e].append((None, (("ph", target),), None, 1))


def _trig_tables():
    try:
        import jax
        import jax.numpy as jnp
        cpu = jax.devices("cpu")[0]
        with jax.default_device(cpu):
            pos = jnp.arange(L, dtype=jnp.float32)[:, None]
            inv_freq = 10000.0 ** (jnp.arange(64, dtype=jnp.float32) / 64)
            t = pos * inv_freq[None, :]
            sin = np.asarray(jnp.sin(t))
            cos = np.asarray(jnp.cos(t))
    except Exception:
        inv = (10000.0 ** (np.arange(64, dtype=np.float64) / 64.0)).astype(np.float32)
        t = (np.arange(L, dtype=np.float32)[:, None] * inv[None, :]).astype(np.float32)
        sin = np.sin(t.astype(np.float64)).astype(np.float32)
        cos = np.cos(t.astype(np.float64)).astype(np.float32)
    return sin, cos



def rap(base, off, dims):
    return AP(base.tensor, off, [list(d) for d in dims])

def build_program(dump=(), phases=("p1", "c", "d")):
    nc = bass.Bass("TRN2", target_bir_lowering=False)

    X = nc.dram_tensor("x", [L, D], F32, kind="ExternalInput").ap()
    UVWT = nc.dram_tensor("uvwT", [D, 2 * E + S], F32, kind="ExternalInput").ap()
    OWT = nc.dram_tensor("owT", [E, D], F32, kind="ExternalInput").ap()
    W1T = nc.dram_tensor("w1T", [D, FF], F32, kind="ExternalInput").ap()
    W2T = nc.dram_tensor("w2T", [FF, D], F32, kind="ExternalInput").ap()
    BIAST = nc.dram_tensor("biasT", [L, OWN], F32, kind="ExternalInput").ap()
    COST = nc.dram_tensor("cosT", [128, L], F32, kind="ExternalInput").ap()
    SINT = nc.dram_tensor("sinT", [128, L], F32, kind="ExternalInput").ap()
    GB = nc.dram_tensor("gb", [128, 8], F32, kind="ExternalInput").ap()
    GREP = nc.dram_tensor("grep", [128, 1], F32, kind="ExternalInput").ap()
    RS = nc.dram_tensor("rs", [128, 8], F32, kind="ExternalInput").ap()
    B1 = nc.dram_tensor("b1", [128, 32], F32, kind="ExternalInput").ap()
    B2 = nc.dram_tensor("b2", [128, 8], F32, kind="ExternalInput").ap()
    IDENT = nc.dram_tensor("ident", [128, 128], F32, kind="ExternalInput").ap()
    CONSTS = nc.dram_tensor("consts", [128, 8], F32, kind="ExternalInput").ap()
    WRELC = nc.dram_tensor("wrelc", [128, 1], F32, kind="ExternalInput").ap()
    OUT = nc.dram_tensor("out", [OWN, D], F32, kind="ExternalOutput").ap()

    VSCR = nc.dram_tensor("v_scr", [L, E], BF16).ap()
    XNSCR = nc.dram_tensor("xn_scr", [D, OWN], BF16).ap()
    SSCR = nc.dram_tensor("s_scr", [D, OWN], BF16).ap()
    W1SCR = nc.dram_tensor("w1_scr", [D, FF], BF16).ap()
    UVUSCR = nc.dram_tensor("uvu_scr", [D, E], BF16).ap()
    W2SCR = nc.dram_tensor("w2_scr", [FF, D], BF16).ap()

    dumps = {}
    if dump:
        shapes = {"xn": [D, OWN], "v": [L, E], "k": [128, L], "q": [128, OWN],
                  "alpha": [128, 32], "s": [D, OWN], "w1t": [D, FF],
                  "w2t": [FF, D]}
        for name in dump:
            dumps[name] = nc.dram_tensor("dump_" + name, shapes[name], F32,
                                         kind="ExternalOutput").ap()

    sch = Sched()
    E_ = sch.emit
    guards = []

    def sb(name, shape, dtype):
        g = nc.sbuf_tensor("sb_" + name, shape, dtype)
        t = g.__enter__()
        guards.append(g)
        return t

    def free_from(mark):
        while len(guards) > mark:
            guards.pop().__exit__(None, None, None)

    consts = sb("consts", [128, 8], F32)
    gb = sb("gbv", [128, 8], F32)
    gbq = sb("gbq", [128, 4], F32)
    grep = sb("grep", [128, 1], F32)
    isq = sb("isq", [128, 1], F32)
    rs = sb("rs", [128, 8], F32)
    b1s = sb("b1s", [128, 32], F32)
    b2s = sb("b2s", [128, 8], F32)
    identb = sb("identb", [128, 128], BF16)
    identf = sb("identf", [128, 128], F32)
    alpha = sb("alpha", [128, NG], F32)
    kbf = sb("kbf", [128, L], BF16)
    qbf = sb("qbf", [128, OWN], BF16)
    wacc = sb("wacc", [128, 2], F32)
    wsc = sb("wsc", [128, 2], F32)
    wisq = sb("wisq", [128, 1], F32)

    nc.const_aps.register(F32, 0.5, consts[:, 2:3])
    nc.const_aps.register(F32, -0.5, consts[:, 3:4])

    ps = [nc.psum_tensor(f"ps{i}", [128, 512], F32) for i in range(8)]
    psg = []
    for p in ps:
        psg.append(p.__enter__())
        guards.append(p)

    for apdst, apsrc in ((consts[:], CONSTS), (gb[:], GB), (grep[:], GREP),
                         (rs[:], RS), (b1s[:], B1), (b2s[:], B2),
                         (identf[:], IDENT)):
        E_("sync", (lambda d_, s_: lambda eng: eng.dma_start(out=d_, in_=s_))(apdst, apsrc),
           inc="ld0", dma=True)
    d0 = sch.cnt["ld0"]
    E_("gpsimd", lambda eng: eng.dma_start(out=identb[:], in_=IDENT), inc="ldg", dma=True)
    E_("scalar", lambda eng: eng.activation(isq[:], consts[:, 0:1], AF.Ln),
       waits=(("ld0", d0),))
    E_("scalar", lambda eng: eng.drain())
    sv = E_("scalar", lambda eng: eng.activation(isq[:], isq[:], AF.Exp, scale=-0.5),
            inc="s")
    E_("sync", lambda eng: eng.dma_start(out=wisq[:], in_=WRELC), inc="ldw", dma=True)
    E_("vector", lambda eng: eng.tensor_scalar_mul(gbq[:, 0:2], gb[:, 0:2], isq[:, 0:1]),
       waits=(("s", sv),))
    E_("vector", lambda eng: eng.tensor_mul(wisq[:], wisq[:], isq[:, 0:1]),
       waits=(("ldw", sch.cnt["ldw"]),))
    p0_v = E_("vector", lambda eng: eng.tensor_scalar_mul(gbq[:, 2:4], gb[:, 4:6], isq[:, 0:1]),
              inc="v")

    mark_p1 = len(guards)
    cosT = sb("cosT", [128, L], F32)
    sinT = sb("sinT", [128, L], F32)
    E_("sync", lambda eng: eng.dma_start(out=cosT[:], in_=COST), inc="trg", dma=True)
    E_("sync", lambda eng: eng.dma_start(out=sinT[:], in_=SINT), inc="trg", dma=True)
    uvb = [sb(f"uvb{i}", [128, E + S], BF16) for i in range(8)]
    xtm = [sb(f"xtm{i}", [128, L], F32) for i in range(3)]
    sqscr = sb("sqscr", [128, 1024], F32)
    xnt = [sb(f"xnt{i}", [128, L], BF16) for i in range(2)]
    xnt_used = [None, None]
    xnf = [sb(f"xnf{i}", [128, L], BF16) for i in range(2)]
    vout = [sb(f"vout{i}", [128, E], BF16) for i in range(3)]
    tq = sb("tq", [128, 512], F32)
    tk = sb("tk", [128, 512], F32)
    rt = [sb(f"rt{i}", [128, 512], F32) for i in range(2)]
    tqs = sb("tqs", [128, 512], F32)
    tks = sb("tks", [128, 512], F32)
    wst = [sb(f"wst{i}", [128, 2048], F32) for i in range(2)]
    stat = sb("stat", [128, 40], F32)
    btm = sb("btm", [128, 128], BF16)
    ones_f = sb("ones_f", [128, 1], F32)
    onesr = sb("onesr", [1, 128], F32)

    for dc in range(8):
        E_("gpsimd", (lambda dc: lambda eng: eng.dma_start(
            out=uvb[dc][:], in_=UVWT[128 * dc:128 * dc + 128, E:]))(dc),
           inc="uvb", dma=True)
    uvb_gd = sch.cnt["uvb"]
    for dc in range(8):
        E_("gpsimd", (lambda dc: lambda eng: eng.dma_start(
            out=UVUSCR[128 * dc:128 * dc + 128, :],
            in_=UVWT[128 * dc:128 * dc + 128, 0:E]))(dc),
           inc="uvc", dma=True)

    E_("vector", lambda eng: eng.memset(wacc[:], 0.0), inc="v")

    wslot_used = [None, None]
    wacc_v = [None]

    def w_scale_stream(i):
        slot = i % 2
        if i < 16:
            src = W1T[128 * (i // 2):128 * (i // 2) + 128,
                      2048 * (i % 2):2048 * (i % 2) + 2048]
            col = 0
        else:
            j = i - 16
            src = rap(W2T, 256 * j * D, [[D, 128], [128 * D, 2], [1, D]])
            col = 1
        dv = E_("gpsimd", (lambda s_, sl: lambda eng: eng.dma_start(out=wst[sl][:], in_=s_))(src, slot),
                waits=((wslot_used[slot],) if wslot_used[slot] else ()), inc=f"ws{slot}", dma=True)
        E_("vector", (lambda sl: lambda eng: eng.reduce_sum(
            stat[:, 36:37], wst[sl][:], axis=AX, apply_absolute_value=True))(slot),
           waits=((f"ws{slot}", dv),))
        wacc_v[0] = E_("vector", (lambda c_: lambda eng: eng.tensor_add(
            wacc[:, c_:c_ + 1], wacc[:, c_:c_ + 1], stat[:, 36:37]))(col), inc="v")
        wslot_used[slot] = ("v", wacc_v[0])

    xtm_used = [None, None, None]
    xnf_used = [None, None]
    vout_used = [None, None, None]
    tb_used = [None, None]
    vb_used = [None, None]
    bb_used = [None]
    b5_used = [None]
    rope_v = [None]

    s_norms = {}
    dx_store = {}
    stats_sv = [None]

    def p1_xdma(c):
        slot = c % 3
        src = rap(X, 512 * c * D, [[D, 128], [128 * D, 4], [1, D]])
        dx_store[c] = E_("sync", (lambda s_, sl: lambda eng: eng.dma_start(
            out=xtm[sl][:], in_=s_))(src, slot),
            waits=((xtm_used[slot],) if xtm_used[slot] else ()), inc=f"x{slot}", dma=True)

    def p1_stats_a(c):
        slot = c % 3
        dx = dx_store[c]

        for g4 in range(4):
            xt = xtm[slot][:, 1024 * g4:1024 * g4 + 1024]
            E_("vector", (lambda xt, g4: lambda eng: eng.reduce_sum(
                stat[:, 10 + g4:11 + g4], xt, axis=AX))(xt, g4),
               waits=((f"x{slot}", dx),) if g4 == 0 else ())
        vv = E_("vector", lambda eng: eng.tensor_scalar_mul(
            stat[:, 0:4], stat[:, 10:14], -1.0 / 1024.0), inc="v")
        for g4 in range(4):
            xt = xtm[slot][:, 1024 * g4:1024 * g4 + 1024]
            E_("scalar", (lambda xt, g4: lambda eng: eng.activation(
                sqscr[:], xt, AF.Square, bias=stat[:, g4:g4 + 1],
                accum_out=stat[:, 4 + g4:5 + g4]))(xt, g4),
                waits=(("v", vv),) if g4 == 0 else ())
        stats_sv[0] = sch.tag_last("scalar", "s")

    def p1_stats_b(c):
        slot = c % 3
        sv = stats_sv[0]
        E_("vector", lambda eng: eng.tensor_scalar_mul(
            stat[:, 8:12], stat[:, 4:8], 1.0 / 1024.0), waits=(("s", sv),))
        E_("vector", lambda eng: eng.tensor_scalar_add(stat[:, 12:16], stat[:, 8:12], EPS))
        E_("vector", lambda eng: eng.reciprocal(stat[:, 16:20], stat[:, 12:16]))
        E_("vector", lambda eng: eng.tensor_mul(
            stat[:, 20:24], stat[:, 8:12], stat[:, 16:20]))
        E_("vector", lambda eng: eng.tensor_scalar(
            stat[:, 24:28], stat[:, 12:16], -0.5, 1.5, ALU.mult, ALU.add))
        for _ in range(3):
            E_("vector", lambda eng: eng.tensor_mul(stat[:, 16:20], stat[:, 24:28], stat[:, 24:28]))
            E_("vector", lambda eng: eng.tensor_mul(stat[:, 16:20], stat[:, 16:20], stat[:, 12:16]))
            E_("vector", lambda eng: eng.tensor_scalar(
                stat[:, 16:20], stat[:, 16:20], -0.5, 1.5, ALU.mult, ALU.add))
            E_("vector", lambda eng: eng.tensor_mul(stat[:, 24:28], stat[:, 24:28], stat[:, 16:20]))
        E_("vector", lambda eng: eng.tensor_scalar(
            stat[:, 28:32], stat[:, 20:24], -0.5, 1.5, ALU.mult, ALU.add))
        for _ in range(2):
            E_("vector", lambda eng: eng.tensor_mul(stat[:, 16:20], stat[:, 28:32], stat[:, 28:32]))
            E_("vector", lambda eng: eng.tensor_mul(stat[:, 16:20], stat[:, 16:20], stat[:, 20:24]))
            E_("vector", lambda eng: eng.tensor_scalar(
                stat[:, 16:20], stat[:, 16:20], -0.5, 1.5, ALU.mult, ALU.add))
            E_("vector", lambda eng: eng.tensor_mul(stat[:, 28:32], stat[:, 28:32], stat[:, 16:20]))
        E_("vector", lambda eng: eng.tensor_mul(stat[:, 28:32], stat[:, 28:32], stat[:, 20:24]))
        E_("vector", lambda eng: eng.tensor_scalar_max(stat[:, 28:32], stat[:, 28:32], EPS))
        E_("vector", lambda eng: eng.reciprocal(stat[:, 28:32], stat[:, 28:32]))
        E_("vector", lambda eng: eng.tensor_scalar_mul(
            alpha[:, 4 * c:4 * c + 4], stat[:, 28:32], grep[:, 0:1]))
        E_("vector", lambda eng: eng.tensor_mul(
            stat[:, 32:36], stat[:, 0:4], stat[:, 24:28]), inc="v")
        xn_t = xnt[c % 2]
        wxt = xnt_used[c % 2]
        for g4 in range(4):
            xt = xtm[slot][:, 1024 * g4:1024 * g4 + 1024]
            E_("vector", (lambda xt, g4, xn_t: lambda eng: eng.tensor_scalar(
                xn_t[:, 1024 * g4:1024 * g4 + 1024], xt, stat[:, 24 + g4:25 + g4],
                stat[:, 32 + g4:33 + g4], ALU.mult, ALU.add))(xt, g4, xn_t), inc="v",
                waits=((wxt,) if (g4 == 0 and wxt) else ()))
        s_norms[c] = ("v", sch.cnt["v"])
        xtm_used[slot] = s_norms[c]
        for i in (4 * c, 4 * c + 1, 4 * c + 2, 4 * c + 3):
            if i < 32:
                w_scale_stream(i)

    v_evacs = {}
    gd_xns = {}

    def p1_compute1(c):
        s_norm = s_norms[c]
        xn_t = xnt[c % 2]
        fslot = c % 2
        wv = xnf_used[fslot] or ()
        xf = xnf[fslot]
        for dc in range(8):
            bank = psg[dc % 2]
            for g4 in range(4):
                wt = ()
                if g4 == 0:
                    wt = (s_norm,)
                    if tb_used[dc % 2] is not None:
                        wt = wt + (tb_used[dc % 2],)
                    if dc < 2:
                        wt = wt + wv
                    if c == 0 and dc == 0:
                        wt = wt + (("ldg", 16),)
                E_("tensor", (lambda dc, g4, bank, xn_t: lambda eng: eng.matmul(
                    bank[:, 128 * g4:128 * g4 + 128],
                    xn_t[:, 1024 * g4 + 128 * dc:1024 * g4 + 128 * dc + 128],
                    identb[:], start=True, stop=True))(dc, g4, bank, xn_t),
                   waits=wt)
            tv = sch.tag_last("tensor", "t")
            tb_used[dc % 2] = ("s", E_("scalar", (lambda dc, bank, xf: lambda eng: eng.copy(
                xf[:, 512 * dc:512 * dc + 512], bank[:]))(dc, bank, xf),
                waits=(("t", tv),), inc="s"))
        xnt_used[c % 2] = ("t", sch.cnt["t"])
        v_evac = ("s", sch.cnt["s"])
        v_evacs[c] = v_evac
        gd_xn = None
        gd_xns[c] = None
        if c < 4:
            dstxn = rap(XNSCR, 512 * c, [[OWN, 128], [128 * OWN, 8], [1, 512]])
            gd_xns[c] = E_("gpsimd", (lambda xf, dstxn: lambda eng: eng.dma_start(
                out=dstxn, in_=xf[:]))(xf, dstxn),
                waits=(v_evac,), inc=f"xw{c % 2}", dma=True)

        for j4 in range(4):
            g = 4 * c + j4
            for dc in range(8):
                wt = (("s", bb_used[0]),) if (dc == 0 and bb_used[0] is not None) else ()
                if j4 == 0 and dc == 0:
                    wt = wt + (v_evac,)
                if c == 0 and j4 == 0 and dc == 0:
                    wt = wt + (("uvb", uvb_gd),)
                E_("tensor", (lambda j4, dc, xf: lambda eng: eng.matmul(
                    psg[4][:, 0:128], xf[:, 512 * dc + 128 * j4:512 * dc + 128 * j4 + 128],
                    uvb[dc][:, E:E + S], start=(dc == 0), stop=(dc == 7)))(j4, dc, xf),
                   waits=wt)
            tv = sch.tag_last("tensor", "t")
            bb_used[0] = E_("scalar", (lambda g: lambda eng: eng.activation(
                btm[:], psg[4][:, 0:128], AF.Silu, scale=alpha[:, g:g + 1]))(g),
                waits=(("t", tv),), inc="s")
            wt = (("s", bb_used[0]),)
            if j4 == 0 and b5_used[0] is not None:
                wt = wt + (b5_used[0],)
            E_("tensor", (lambda j4: lambda eng: eng.matmul(
                psg[5][:, 128 * j4:128 * j4 + 128], btm[:], identb[:],
                start=True, stop=True))(j4),
               waits=wt)
            E_("tensor", (lambda j4: lambda eng: eng.matmul(
                psg[6][0:64, 128 * j4:128 * j4 + 128], btm[:, 64:128], identb[:],
                start=True, stop=True))(j4))
            E_("tensor", (lambda j4: lambda eng: eng.matmul(
                psg[6][64:128, 128 * j4:128 * j4 + 128], btm[:, 0:64], identb[:],
                start=True, stop=True))(j4))
        tv = sch.tag_last("tensor", "t")
        jcols = slice(512 * c, 512 * c + 512)
        E_("vector", lambda eng: eng.tensor_scalar(
            tk[:], psg[5][:], gb[:, 2:3], gb[:, 3:4], ALU.mult, ALU.add),
            waits=(("t", tv),))
        E_("vector", lambda eng: eng.tensor_scalar(
            tks[:], psg[6][:], gb[:, 6:7], gb[:, 7:8], ALU.mult, ALU.add))
        if c < 4:
            E_("vector", lambda eng: eng.tensor_scalar(
                tq[:], psg[5][:], gbq[:, 0:1], gbq[:, 1:2], ALU.mult, ALU.add))
            E_("vector", lambda eng: eng.tensor_scalar(
                tqs[:], psg[6][:], gbq[:, 2:3], gbq[:, 3:4], ALU.mult, ALU.add))
        b5_used[0] = ("v", sch.cnt["v"] + 1)
        cs = cosT[:, jcols]
        sn = sinT[:, jcols]

        def rope(dst, t_, ts_):
            E_("vector", (lambda: lambda eng: eng.tensor_mul(
                rt[0][0:64, :], t_[0:64, :], cs[0:64, :]))(),
               waits=((("trg", 32),) if c == 0 else ()), pre_drain=True)
            E_("vector", (lambda: lambda eng: eng.tensor_mul(
                rt[1][0:64, :], ts_[0:64, :], sn[0:64, :]))())
            E_("vector", (lambda: lambda eng: eng.tensor_sub(
                dst[0:64, :], rt[0][0:64, :], rt[1][0:64, :]))(), pre_drain=True)
            E_("vector", (lambda: lambda eng: eng.tensor_mul(
                rt[0][64:128, :], t_[64:128, :], cs[64:128, :]))())
            E_("vector", (lambda: lambda eng: eng.tensor_mul(
                rt[1][64:128, :], ts_[64:128, :], sn[64:128, :]))())
            return E_("vector", (lambda: lambda eng: eng.tensor_add(
                dst[64:128, :], rt[0][64:128, :], rt[1][64:128, :]))(), inc="v",
                pre_drain=True)

        rope(kbf[:, jcols], tk, tks)
        if c < 4:
            rope(qbf[:, jcols], tq, tqs)
        rope_v[0] = sch.cnt["v"]


    def p1_compute2(c):
        fslot = c % 2
        xf = xnf[fslot]
        v_evac = v_evacs[c]
        gd_xn = gd_xns[c]
        vslot = c % 3
        for j4 in range(4):
            g = 4 * c + j4
            for eb in range(4):
                bi = (j4 * 4 + eb) % 2
                bank = psg[2 + bi]
                for dc in range(8):
                    wt = ()
                    if dc == 0:
                        wt = (("s", vb_used[bi]),)
                        if j4 == 0 and eb == 0:
                            wt = wt + (v_evac, ("uvb", uvb_gd))
                    E_("tensor", (lambda j4, eb, dc, bank, xf: lambda eng: eng.matmul(
                        bank[:], xf[:, 512 * dc + 128 * j4:512 * dc + 128 * j4 + 128],
                        uvb[dc][:, 512 * eb:512 * eb + 512],
                        start=(dc == 0), stop=(dc == 7)))(j4, eb, dc, bank, xf),
                       waits=wt)
                tv = sch.tag_last("tensor", "t")
                wv2 = (("t", tv),)
                if eb == 0 and vout_used[vslot] is not None:
                    wv2 = wv2 + ((f"vo{vslot}", vout_used[vslot]),)
                vb_used[bi] = E_("scalar", (lambda bank, vslot, eb, g: lambda eng: eng.activation(
                    vout[vslot][:, 512 * eb:512 * eb + 512], bank[:], AF.Silu,
                    scale=alpha[:, g:g + 1]))(bank, vslot, eb, g),
                    waits=wv2, inc="s")
            sv = sch.cnt["s"]
            vout_used[vslot] = E_("gpsimd", (lambda vslot, g: lambda eng: eng.dma_start(
                out=VSCR[128 * g:128 * g + 128, :], in_=vout[vslot][:]))(vslot, g),
                waits=(("s", sv),), inc=f"vo{vslot}", dma=True)
            vslot = (vslot + 1) % 3

        xnf_used[fslot] = ((f"xw{c % 2}", gd_xn),) if gd_xn else ()
    p1_xdma(0)
    p1_xdma(1)
    p1_stats_a(0)
    p1_stats_b(0)
    for c in range(NCH):
        if c + 2 < NCH:
            p1_xdma(c + 2)
        if c + 1 < NCH:
            p1_stats_a(c + 1)
        p1_compute1(c)
        if c + 1 < NCH:
            p1_stats_b(c + 1)
        p1_compute2(c)

    E_("vector", lambda eng: eng.memset(ones_f[:], 1.0))
    vv = E_("vector", lambda eng: eng.memset(onesr[:], 1.0), inc="v")
    for col in range(2):
        E_("tensor", (lambda col: lambda eng: eng.matmul(
            psg[6][0:1, col:col + 1], wacc[:, col:col + 1], ones_f[:],
            start=True, stop=True))(col),
           waits=((("v", vv), ("v", wacc_v[0])) if col == 0 else ()))
    tv = sch.tag_last("tensor", "t")
    E_("vector", lambda eng: eng.tensor_scalar_mul(
        stat[0:1, 36:38], psg[6][0:1, 0:2], 1.0 / (D * FF)), waits=(("t", tv),))
    E_("vector", lambda eng: eng.tensor_scalar_max(stat[0:1, 36:38], stat[0:1, 36:38], 1e-8),
       pre_drain=True)
    vv = E_("vector", lambda eng: eng.reciprocal(stat[0:1, 36:38], stat[0:1, 36:38]), inc="v",
            pre_drain=True)
    E_("tensor", lambda eng: eng.matmul(psg[6][:, 4:6], onesr[:], stat[0:1, 36:38],
                                        start=True, stop=True),
       waits=(("v", vv),))
    tv = sch.tag_last("tensor", "t")
    E_("vector", lambda eng: eng.tensor_copy(wsc[:], psg[6][:, 4:6]),
       waits=(("t", tv),), inc="v")

    if dumps:
        sch.barrier()
    end_p1 = (("t", sch.cnt["t"]), ("v", sch.cnt["v"]), ("s", sch.cnt["s"]),
              ("xw0", sch.cnt["xw0"]), ("xw1", sch.cnt["xw1"]),
              ("vo0", sch.cnt["vo0"]), ("vo1", sch.cnt["vo1"]),
              ("vo2", sch.cnt["vo2"]), ("uvc", sch.cnt["uvc"]))
    free_from(mark_p1)

    if dumps:
        dscr = sb("dscr", [128, 4096], F32)
        dbb = sb("dbb", [128, 4096], BF16)
        dmw_last = [None]

        def dump_pair(rd_fn, wr_fn, w=4096):
            gv = E_("gpsimd", rd_fn, waits=((("dmw", dmw_last[0]),) if dmw_last[0] else ()),
                    inc="dmr", dma=True)
            vv = E_("vector", (lambda w: lambda eng: eng.tensor_copy(
                dscr[:, 0:w], dbb[:, 0:w]))(w), waits=(("dmr", gv),), inc="v")
            dmw_last[0] = E_("gpsimd", wr_fn, waits=(("v", vv),), inc="dmw", dma=True)

        if "k" in dumps:
            vv = E_("vector", lambda eng: eng.tensor_copy(dscr[:, 0:L], kbf[:]), inc="v")
            dmw_last[0] = E_("gpsimd", lambda eng: eng.dma_start(out=dumps["k"], in_=dscr[:, 0:L]),
               waits=(("v", vv),), inc="dmw", dma=True)
        if "q" in dumps:
            vv = E_("vector", lambda eng: eng.tensor_copy(dscr[:, 0:OWN], qbf[:]),
                    waits=(("dmw", dmw_last[0]),) if dmw_last[0] else (), inc="v")
            dmw_last[0] = E_("gpsimd", lambda eng: eng.dma_start(out=dumps["q"], in_=dscr[:, 0:OWN]),
               waits=(("v", vv),), inc="dmw", dma=True)
        if "alpha" in dumps:
            E_("gpsimd", lambda eng: eng.dma_start(out=dumps["alpha"], in_=alpha[:]),
               inc="dmw2", dma=True)
        if "xn" in dumps:
            for it in range(ITC):
                src = rap(XNSCR, 512 * it, [[OWN, 128], [128 * OWN, 8], [1, 512]])
                dst = rap(dumps["xn"], 512 * it, [[OWN, 128], [128 * OWN, 8], [1, 512]])
                dump_pair((lambda src: lambda eng: eng.dma_start(out=dbb[:], in_=src))(src),
                          (lambda dst: lambda eng: eng.dma_start(out=dst, in_=dscr[:]))(dst))
        if "v" in dumps:
            for g in range(NG):
                dump_pair((lambda g: lambda eng: eng.dma_start(
                              out=dbb[:, 0:E], in_=VSCR[128 * g:128 * g + 128, :]))(g),
                          (lambda g: lambda eng: eng.dma_start(
                              out=dumps["v"][128 * g:128 * g + 128, :], in_=dscr[:, 0:E]))(g), w=E)
        sch.barrier()
        free_from(mark_p1)

    mark_c = len(guards)
    kts = sb("kts", [128, 32 * 512], BF16)
    usb = sb("usb", [128, 4 * 2048], BF16)
    gtm = usb
    gfm = sb("gfm", [128, 16 * 512], BF16)
    vwin = [sb(f"vwin{i}", [128, 16 * 512], BF16) for i in range(2)]
    btile = [sb(f"btile{i}", [128, 512], F32) for i in range(2)]
    ttile = [sb(f"ttile{i}", [128, 512], F32) for i in range(2)]
    xnc = [sb(f"xnc{i}", [128, 8 * 512], BF16) for i in range(2)]
    uvu = [sb(f"uvu{i}", [128, 8 * 512], BF16) for i in range(2)]
    owin = [sb(f"owin{i}", [128, 16 * 128], BF16) for i in range(2)]
    sout = sb("sout", [128, 8 * 512], BF16)
    wtern = [sb(f"wtern{i}", [128, 2048], F32) for i in range(2)]
    wsAB = [sb(f"wsAB{i}", [128, 4096], BF16) for i in range(2)]

    tern_items = ([("w1", i, j) for i in range(8) for j in range(2)]
                  + [("w2", i, 0) for i in range(16)])
    tern = {"si": 0, "in_used": [None, None], "ab_used": [None, None],
            "tw": [None, None]}

    def tern_step():
        if tern["si"] >= len(tern_items):
            return
        mat, i, j = tern_items[tern["si"]]
        tern["si"] += 1
        if mat == "w1":
            src = W1T[128 * i:128 * i + 128, 2048 * j:2048 * j + 2048]
            dst = W1SCR[128 * i:128 * i + 128, 2048 * j:2048 * j + 2048]
            col = 0
        else:
            src = rap(W2T, 256 * i * D, [[D, 128], [128 * D, 2], [1, D]])
            dst = rap(W2SCR, 256 * i * D, [[D, 128], [128 * D, 2], [1, D]])
            col = 1
        ts_ = tern["si"] % 2
        wt_ = wtern[ts_]
        wab = wsAB[ts_]
        dv = E_("sync", (lambda s_, wt_: lambda eng: eng.dma_start(out=wt_[:], in_=s_))(src, wt_),
                waits=((("s", tern["in_used"][ts_]),) if tern["in_used"][ts_] else ()),
                inc=f"tn{ts_}", dma=True)
        E_("scalar", (lambda col, wt_, wab: lambda eng: eng.activation(
            wab[:, 0:2048], wt_[:], AF.Sign, bias=-0.5, scale=wsc[:, col:col + 1]))(col, wt_, wab),
           waits=((f"tn{ts_}", dv),) + ((("v", tern["ab_used"][ts_]),) if tern["ab_used"][ts_] else ())
                 + (((f"tw{ts_}", tern["tw"][ts_]),) if tern["tw"][ts_] else ()))
        tern["in_used"][ts_] = E_("scalar", (lambda col, wt_, wab: lambda eng: eng.activation(
            wab[:, 2048:4096], wt_[:], AF.Sign, bias=0.5, scale=wsc[:, col:col + 1]))(col, wt_, wab),
            inc="s")
        tern["ab_used"][ts_] = E_("vector", (lambda wab: lambda eng: eng.tensor_add(
            wab[:, 0:2048], wab[:, 0:2048], wab[:, 2048:4096]))(wab),
            waits=(("s", tern["in_used"][ts_]),)
                  + (((f"tw{ts_}", tern["tw"][ts_]),) if tern["tw"][ts_] else ()), inc="v")
        tern["tw"][ts_] = E_("gpsimd", (lambda dst, wab: lambda eng: eng.dma_start(
            out=dst, in_=wab[:, 0:2048]))(dst, wab),
           waits=(("v", tern["ab_used"][ts_]),), inc=f"tw{ts_}", dma=True)

    cst = {"xnc_used": None, "kts_used": None, "usb_used": None,
           "uvu_used": [None, None], "vwin_used": [None, None],
           "btile_used": [None] * 2, "ttile_used": [None] * 2,
           "sout_used": None, "xnc_used2": [None, None],
           "sc_bank": [None, None], "u_bank": [None, None],
           "ou_bank": [None] * 4, "tr_bank": [None, None], "sp_bank": None,
           "owin_used": [None, None]}

    kt_store = {}

    def emit_scores(it_s):
        kt_v = [None] * 32
        icols_s = slice(512 * it_s, 512 * it_s + 512)
        for jt in range(32):
            bank = psg[jt % 2]
            wt = (("v", cst["sc_bank"][jt % 2]),)
            if jt == 0 and it_s == 0:
                wt = wt + end_p1 + (("t", cst["kts_used"]),)
            tv = E_("tensor", (lambda jt, bank, icols_s: lambda eng: eng.matmul(
                bank[:], kbf[:, 128 * jt:128 * jt + 128], qbf[:, icols_s],
                start=True, stop=True))(jt, bank, icols_s),
                waits=wt, inc="t")
            tslot = jt % 2
            variable = (16 <= jt <= 4 * it_s + 19)
            if variable:
                bslot = jt % 2
                wb_ = (("v", cst["btile_used"][bslot]),)
                if it_s == 0 and jt == 16:
                    wb_ = wb_ + end_p1
                b_dv = E_("sync", (lambda jt, bslot, icols_s: lambda eng: eng.dma_start(
                    out=btile[bslot][:], in_=BIAST[128 * jt:128 * jt + 128, icols_s]))(jt, bslot, icols_s),
                    waits=wb_, inc=f"bt{bslot}", dma=True)
                E_("vector", (lambda bslot, tslot, bank: lambda eng: eng.scalar_tensor_tensor(
                    ttile[tslot][:], btile[bslot][:], isq[:, 0:1], bank[:],
                    ALU.mult, ALU.add))(bslot, tslot, bank),
                   waits=(("t", tv), (f"bt{bslot}", b_dv), ("v", cst["ttile_used"][tslot])))
            else:
                E_("vector", (lambda tslot, bank: lambda eng: eng.tensor_scalar_add(
                    ttile[tslot][:], bank[:], wisq[:, 0:1]))(tslot, bank),
                   waits=(("t", tv), ("v", cst["ttile_used"][tslot])))
            vv = E_("vector", (lambda jt, tslot: lambda eng: eng.scalar_tensor_tensor(
                kts[:, 512 * jt:512 * jt + 512], ttile[tslot][:], 0.0, ttile[tslot][:],
                ALU.max, ALU.mult))(jt, tslot), inc="v", pre_drain=True)
            if variable:
                cst["btile_used"][jt % 2] = vv
            cst["sc_bank"][jt % 2] = vv
            cst["ttile_used"][tslot] = vv
            kt_v[jt] = vv
        kt_store[it_s] = kt_v

    def phase_c_itile(it):
        icols = slice(512 * it, 512 * it + 512)
        xslot = it % 2
        xn = xnc[xslot]
        src = rap(XNSCR, 512 * it, [[OWN, 128], [128 * OWN, 8], [1, 512]])
        wx = (("v", cst["xnc_used2"][xslot]),)
        if it == 0:
            wx = wx + end_p1
        xn_dv = E_("sync", (lambda src, xn: lambda eng: eng.dma_start(out=xn[:], in_=src))(src, xn),
                   waits=wx, inc=f"xnc{xslot}", dma=True)

        for eb in range(4):
            uslot = eb % 2
            for dc in range(8):
                wt = (("t", cst["uvu_used"][uslot]),) if dc == 0 else ()
                if dc == 0 and eb == 0 and cst["uvu_used"][0] is None:
                    wt = wt + end_p1
                E_("sync", (lambda uslot, dc, eb: lambda eng: eng.dma_start(
                    out=uvu[uslot][:, 512 * dc:512 * dc + 512],
                    in_=UVUSCR[128 * dc:128 * dc + 128, 512 * eb:512 * eb + 512]))(uslot, dc, eb),
                   waits=wt, inc=f"uu{uslot}", dma=True)
            u_dv = sch.cnt[f"uu{uslot}"]
            for isub in range(4):
                bank = psg[2 + (isub % 2)]
                for dc in range(8):
                    wt = ()
                    if dc == 0:
                        wt = ((f"uu{eb % 2}", u_dv), (f"xnc{xslot}", xn_dv))
                        if cst["u_bank"][isub % 2] is not None:
                            wt = wt + (cst["u_bank"][isub % 2],)
                        if eb == 0 and isub == 0:
                            wt = wt + (("v", cst["usb_used"]),)
                    E_("tensor", (lambda isub, dc, bank, uslot, xn: lambda eng: eng.matmul(
                        bank[:], xn[:, 512 * dc + 128 * isub:512 * dc + 128 * isub + 128],
                        uvu[uslot][:, 512 * dc:512 * dc + 512],
                        start=(dc == 0), stop=(dc == 7)))(isub, dc, bank, uslot, xn),
                       waits=wt)
                tv = sch.tag_last("tensor", "t")
                if isub == 3:
                    cst["uvu_used"][uslot] = tv
                g = 4 * it + isub
                cst["u_bank"][isub % 2] = ("s", E_("scalar",
                    (lambda bank, isub, eb, g: lambda eng: eng.activation(
                        usb[:, 2048 * isub + 512 * eb:2048 * isub + 512 * eb + 512],
                        bank[:], AF.Silu, scale=alpha[:, g:g + 1]))(bank, isub, eb, g),
                    waits=(("t", tv),), inc="s"))
            for _ in range(2):
                tern_step()
        u_done_s = sch.cnt["s"]
        for eb in range(4):
            for half in range(2):
                w2i = (2 * eb + half) % 2
                vw = vwin[w2i]
                wt = ((cst["vwin_used"][w2i],) if cst["vwin_used"][w2i] else ())
                vq = "sync"
                for q4 in range(2):
                    src = rap(VSCR, (2048 * half + 1024 * q4) * E + 512 * eb,
                              [[E, 128], [128 * E, 8], [1, 512]])
                    E_(vq, (lambda src, q4, vw: lambda eng: eng.dma_start(
                        out=vw[:, 4096 * q4:4096 * q4 + 4096], in_=src))(src, q4, vw),
                       waits=(wt if q4 == 0 else ()), inc=f"vw{w2i}", dma=True)
                v_dv = sch.cnt[f"vw{w2i}"]
                for isub in range(4):
                    bank = psg[4 + isub]
                    for jt16 in range(16):
                        jt = 16 * half + jt16
                        wt2 = ()
                        if jt16 == 0:
                            wt2 = ((f"vw{w2i}", v_dv),)
                            if half == 0 and cst["ou_bank"][isub] is not None:
                                wt2 = wt2 + (cst["ou_bank"][isub],)
                        if isub == 0 and eb == 0:
                            wt2 = wt2 + (("v", kt_store[it][jt]),)
                        E_("tensor", (lambda isub, jt, jt16, half, bank, vw: lambda eng: eng.matmul(
                            bank[:], kts[:, 512 * jt + 128 * isub:512 * jt + 128 * isub + 128],
                            vw[:, 512 * jt16:512 * jt16 + 512],
                            start=(half == 0 and jt16 == 0),
                            stop=(half == 1 and jt16 == 15)))(isub, jt, jt16, half, bank, vw),
                           waits=wt2)
                tv = sch.tag_last("tensor", "t")
                cst["vwin_used"][w2i] = ("t", tv)
                ou_last = tv
            for isub in range(4):
                bank = psg[4 + isub]
                cst["ou_bank"][isub] = ("v", E_("vector",
                    (lambda bank, isub, eb: lambda eng: eng.tensor_mul(
                        gtm[:, 2048 * isub + 512 * eb:2048 * isub + 512 * eb + 512],
                        bank[:], usb[:, 2048 * isub + 512 * eb:2048 * isub + 512 * eb + 512]))(bank, isub, eb),
                    waits=(("t", ou_last), ("s", u_done_s)), inc="v"))
        gate_done_v = sch.cnt["v"]
        cst["usb_used"] = gate_done_v

        if it + 1 < ITC:
            emit_scores(it + 1)

        for ec in range(16):
            bank = psg[4 + (ec % 2)]
            for isub in range(4):
                wt = ()
                if isub == 0:
                    wt = (("v", gate_done_v), ("s", cst["tr_bank"][ec % 2]))
                E_("tensor", (lambda ec, isub, bank: lambda eng: eng.matmul(
                    bank[:, 128 * isub:128 * isub + 128],
                    gtm[:, 2048 * isub + 128 * ec:2048 * isub + 128 * ec + 128],
                    identb[:], start=True, stop=True))(ec, isub, bank),
                   waits=wt)
            tv = sch.tag_last("tensor", "t")
            cst["tr_bank"][ec % 2] = E_("scalar", (lambda ec, bank: lambda eng: eng.copy(
                gfm[:, 512 * ec:512 * ec + 512], bank[:]))(ec, bank),
                waits=(("t", tv),), inc="s")
        gfm_done = sch.cnt["s"]

        for dc in range(8):
            oslot = dc % 2
            for h2 in range(2):
                src = rap(OWT, 1024 * h2 * D + 128 * dc,
                          [[D, 128], [128 * D, 8], [1, 128]])
                wt = (("t", cst["owin_used"][oslot]),) if h2 == 0 else ()
                E_("gpsimd", (lambda src, oslot, h2: lambda eng: eng.dma_start(
                    out=owin[oslot][:, 1024 * h2:1024 * h2 + 1024], in_=src))(src, oslot, h2),
                   waits=wt, inc=f"ow{oslot}", dma=True)
            o_dv = sch.cnt[f"ow{oslot}"]
            bank = psg[6 + (dc % 2)]
            for ec in range(16):
                wt = ()
                if ec == 0:
                    wt = (("v", cst["sp_bank"] if dc >= 2 else None),
                          ((f"ow{dc % 2}"), o_dv), ("s", gfm_done))
                E_("tensor", (lambda dc, ec, oslot, bank: lambda eng: eng.matmul(
                    bank[:], owin[oslot][:, 128 * ec:128 * ec + 128],
                    gfm[:, 512 * ec:512 * ec + 512],
                    start=(ec == 0), stop=(ec == 15)))(dc, ec, oslot, bank),
                   waits=wt)
            tv = sch.tag_last("tensor", "t")
            cst["owin_used"][oslot] = tv
            wt = (("t", tv),)
            if dc == 0 and cst["sout_used"] is not None:
                wt = wt + (("so", cst["sout_used"]),)
            cst["sp_bank"] = E_("vector", (lambda dc, bank, xn: lambda eng: eng.scalar_tensor_tensor(
                sout[:, 512 * dc:512 * dc + 512], xn[:, 512 * dc:512 * dc + 512],
                rs[:, dc:dc + 1], bank[:], ALU.mult, ALU.add))(dc, bank, xn),
                waits=wt, inc="v")
        s_done_v = sch.cnt["v"]
        cst["xnc_used2"][xslot] = s_done_v
        cst["kts_used"] = sch.cnt["t"]
        cst["marks"] = {"t": sch.cnt["t"], "gfm_s": gfm_done, "us": u_done_s}
        cst["ou_bank"][0] = ("s", gfm_done)
        cst["ou_bank"][1] = ("s", gfm_done)
        cst["ou_bank"][2] = ("v", s_done_v)
        cst["ou_bank"][3] = ("v", s_done_v)
        dst = rap(SSCR, 512 * it, [[OWN, 128], [128 * OWN, 8], [1, 512]])
        cst["sout_used"] = E_("gpsimd", (lambda dst: lambda eng: eng.dma_start(
            out=dst, in_=sout[:]))(dst), waits=(("v", s_done_v),), inc="so", dma=True)

    if "c" in phases:
        emit_scores(0)
        for it in range(ITC):
            phase_c_itile(it)
        while tern["si"] < len(tern_items):
            tern_step()

    if dumps or "c" not in phases:
        sch.barrier()
    end_c = (("t", sch.cnt["t"]), ("v", sch.cnt["v"]), ("s", sch.cnt["s"]),
             ("tw0", sch.cnt["tw0"]), ("so", sch.cnt["so"]),
             ("tw1", sch.cnt["tw1"]))
    free_from(mark_c)

    if dumps and ("s" in dumps or "w1t" in dumps or "w2t" in dumps):
        dscr2 = sb("dscr2", [128, 4096], F32)
        dbb2 = sb("dbb2", [128, 4096], BF16)
        d2_last = [None]

        def dump2(rd_fn, wr_fn, mul):
            gv = E_("gpsimd", rd_fn, waits=((("dw2", d2_last[0]),) if d2_last[0] else ()),
                    inc="dr2", dma=True)
            if mul:
                vv = E_("vector", lambda eng: eng.tensor_scalar_mul(dscr2[:], dbb2[:], 0.5),
                        waits=(("dr2", gv),), inc="v")
            else:
                vv = E_("vector", lambda eng: eng.tensor_copy(dscr2[:], dbb2[:]),
                        waits=(("dr2", gv),), inc="v")
            d2_last[0] = E_("gpsimd", wr_fn, waits=(("v", vv),), inc="dw2", dma=True)

        if "s" in dumps:
            for it in range(ITC):
                src = rap(SSCR, 512 * it, [[OWN, 128], [128 * OWN, 8], [1, 512]])
                dst = rap(dumps["s"], 512 * it, [[OWN, 128], [128 * OWN, 8], [1, 512]])
                dump2((lambda src: lambda eng: eng.dma_start(out=dbb2[:], in_=src))(src),
                      (lambda dst: lambda eng: eng.dma_start(out=dst, in_=dscr2[:]))(dst), False)
        if "w1t" in dumps:
            for i in range(8):
                dump2((lambda i: lambda eng: eng.dma_start(
                          out=dbb2[:], in_=W1SCR[128 * i:128 * i + 128, :]))(i),
                      (lambda i: lambda eng: eng.dma_start(
                          out=dumps["w1t"][128 * i:128 * i + 128, :], in_=dscr2[:]))(i), True)
        if "w2t" in dumps:
            for i in range(8):
                src = rap(W2SCR, 512 * i * D, [[D, 128], [128 * D, 4], [1, D]])
                dst = rap(dumps["w2t"], 512 * i * D, [[D, 128], [128 * D, 4], [1, D]])
                dump2((lambda src: lambda eng: eng.dma_start(out=dbb2[:], in_=src))(src),
                      (lambda dst: lambda eng: eng.dma_start(out=dst, in_=dscr2[:]))(dst), True)
        sch.barrier()
        free_from(mark_c)

    w1sb = [sb(f"w1sb{i}", [128, FF], BF16) for i in range(8)]
    hsb = sb("hsb", [128, 32 * 512], BF16)
    w2win = [sb(f"w2win{i}", [128, 32 * 128], BF16) for i in range(2)]
    xnd = [sb(f"xnd{i}", [128, 8 * 512], BF16) for i in range(2)]
    std = sb("std", [128, 8 * 512], BF16)
    gsb = sb("gsb", [128, 8 * 512], F32)
    sgsb = sb("sgsb", [128, 8 * 512], F32)
    xin = [sb(f"xin{i}", [128, 1024], F32) for i in range(2)]
    osb = [sb(f"osb{i}", [128, 1024], F32) for i in range(2)]

    mk = cst["marks"] if "c" in phases else {"t": 0, "gfm_s": 0, "us": 0}
    w1_early = (("t", mk["t"]), ("s", mk["gfm_s"]), ("s", mk["us"]),
                ("tw0", min(8 * 16, sch.cnt["tw0"])),
                ("tw1", min(8 * 16, sch.cnt["tw1"])))
    if "d" in phases:
        for dc in range(8):
            E_("sync", (lambda dc: lambda eng: eng.dma_start(
                out=w1sb[dc][:], in_=W1SCR[128 * dc:128 * dc + 128, :]))(dc),
               waits=(w1_early if dc == 0 else ()), inc="w1l", dma=True)
    w1_dv = sch.cnt["w1l"]

    dst_ = {"xnd_used": [None, None], "std_used": None, "hsb_used": None,
            "gsb_used": None, "sgsb_used": None,
            "h_bank": [None, None], "o_bank": [None, None],
            "w2_used": [None, None], "tr_bank": [None, None],
            "xin_used": [None, None], "osb_used": [None, None]}

    def phase_d_itile(it):
        icols = slice(512 * it, 512 * it + 512)
        xdslot = it % 2
        xnd_ = xnd[xdslot]
        src = rap(XNSCR, 512 * it, [[OWN, 128], [128 * OWN, 8], [1, 512]])
        wxd = (("t", dst_["xnd_used"][xdslot]),)
        if it == 0:
            wxd = wxd + (end_c[0], end_c[1])
        xn_dv = E_("sync", (lambda src, xnd_: lambda eng: eng.dma_start(out=xnd_[:], in_=src))(src, xnd_),
                   waits=wxd, inc=f"xnd{xdslot}", dma=True)
        src = rap(SSCR, 512 * it, [[OWN, 128], [128 * OWN, 8], [1, 512]])
        wsd = (("v", dst_["std_used"]),)
        if it == 0:
            wsd = wsd + (end_c[4],)
        s_dv = E_("sync", (lambda src: lambda eng: eng.dma_start(out=std[:], in_=src))(src),
                  waits=wsd, inc="ssd", dma=True)

        for fc in range(32):
            bank = psg[fc % 2]
            for dc in range(8):
                wt = ()
                if dc == 0:
                    wt = (("s", dst_["h_bank"][fc % 2]), (f"xnd{xdslot}", xn_dv),
                          ("w1l", w1_dv))
                E_("tensor", (lambda fc, dc, bank, xnd_: lambda eng: eng.matmul(
                    bank[:], w1sb[dc][:, 128 * fc:128 * fc + 128],
                    xnd_[:, 512 * dc:512 * dc + 512],
                    start=(dc == 0), stop=(dc == 7)))(fc, dc, bank, xnd_),
                   waits=wt)
            tv = sch.tag_last("tensor", "t")
            wt = (("t", tv),)
            if fc == 0:
                wt = wt + (("t", dst_["hsb_used"]),)
            dst_["h_bank"][fc % 2] = E_("scalar", (lambda fc, bank: lambda eng: eng.activation(
                hsb[:, 512 * fc:512 * fc + 512], bank[:], AF.Gelu,
                bias=b1s[:, fc:fc + 1], scale=0.5))(fc, bank),
                waits=wt, inc="s")
        h_done_s = sch.cnt["s"]

        for dc in range(8):
            wslot = dc % 2
            for h2 in range(2):
                src = rap(W2SCR, 2048 * h2 * D + 128 * dc,
                          [[D, 128], [128 * D, 16], [1, 128]])
                wt = (("t", dst_["w2_used"][wslot]),) if h2 == 0 else ()
                if it == 0 and dc == 0 and h2 == 0:
                    wt = wt + (end_c[3], end_c[5])
                E_("sync", (lambda src, wslot, h2: lambda eng: eng.dma_start(
                    out=w2win[wslot][:, 2048 * h2:2048 * h2 + 2048], in_=src))(src, wslot, h2),
                   waits=wt, inc=f"w2{wslot}", dma=True)
            w2_dv = sch.cnt[f"w2{wslot}"]
            bank = psg[2 + (dc % 2)]
            for fc in range(32):
                wt = ()
                if fc == 0:
                    wt = (("s", dst_["o_bank"][dc % 2]), (f"w2{dc % 2}", w2_dv), ("s", h_done_s))
                    if dc == 0:
                        wt = wt + (("v", dst_["gsb_used"]),)
                E_("tensor", (lambda dc, fc, wslot, bank: lambda eng: eng.matmul(
                    bank[:], w2win[wslot][:, 128 * fc:128 * fc + 128],
                    hsb[:, 512 * fc:512 * fc + 512],
                    start=(fc == 0), stop=(fc == 31)))(dc, fc, wslot, bank),
                   waits=wt)
            tv = sch.tag_last("tensor", "t")
            dst_["w2_used"][wslot] = tv
            wgs = (("t", tv),)
            if it == 0 and dc == 0:
                wgs = wgs + (end_c[4],)
            dst_["o_bank"][dc % 2] = E_("scalar", (lambda dc, bank: lambda eng: eng.activation(
                gsb[:, 512 * dc:512 * dc + 512], bank[:], AF.Silu,
                bias=b2s[:, dc:dc + 1], scale=0.5))(dc, bank),
                waits=wgs, inc="s")
        g_done_s = sch.cnt["s"]
        dst_["hsb_used"] = sch.cnt["t"]

        for dc in range(8):
            wt = ()
            if dc == 0:
                wt = (("s", g_done_s), ("ssd", s_dv), ("t", dst_["sgsb_used"]))
                if it == 0:
                    wt = wt + (end_c[3], end_c[4], end_c[5])
            E_("vector", (lambda dc: lambda eng: eng.tensor_mul(
                sgsb[:, 512 * dc:512 * dc + 512], gsb[:, 512 * dc:512 * dc + 512],
                std[:, 512 * dc:512 * dc + 512]))(dc), waits=wt)
        sg_done_v = sch.tag_last("vector", "v")
        dst_["std_used"] = sg_done_v
        dst_["gsb_used"] = sg_done_v

        for isub in range(4):
            i0 = 512 * it + 128 * isub
            xslot = isub % 2
            wxi = (("v", dst_["xin_used"][xslot]),)
            if it == 0 and isub == 0:
                wxi = wxi + (end_c[3], end_c[5])
            x_dv = E_("sync", (lambda i0, xslot: lambda eng: eng.dma_start(
                out=xin[xslot][:], in_=X[i0:i0 + 128, :]))(i0, xslot),
                waits=wxi, inc=f"xi{xslot}", dma=True)
            tb = isub % 2
            bankA = psg[4 + 2 * tb]
            bankB = psg[5 + 2 * tb]
            for dc in range(8):
                bank = bankA if dc < 4 else bankB
                off = 128 * (dc % 4)
                wt = ()
                if dc == 0:
                    wt = (("v", sg_done_v), ("v", dst_["tr_bank"][tb]))
                E_("tensor", (lambda dc, isub, bank, off: lambda eng: eng.matmul(
                    bank[:, off:off + 128],
                    sgsb[:, 512 * dc + 128 * isub:512 * dc + 128 * isub + 128],
                    identf[:], start=True, stop=True))(dc, isub, bank, off),
                   waits=wt)
            tv = sch.tag_last("tensor", "t")
            oslot = isub % 2
            E_("vector", (lambda bankA, oslot, xslot: lambda eng: eng.tensor_add(
                osb[oslot][:, 0:512], bankA[:], xin[xslot][:, 0:512]))(bankA, oslot, xslot),
               waits=(("t", tv), (f"xi{xslot}", x_dv))
                     + (((f"ou{oslot}", dst_["osb_used"][oslot]),)
                        if dst_["osb_used"][oslot] else ()))
            vv = E_("vector", (lambda bankB, oslot, xslot: lambda eng: eng.tensor_add(
                osb[oslot][:, 512:1024], bankB[:], xin[xslot][:, 512:1024]))(bankB, oslot, xslot),
               inc="v")
            dst_["tr_bank"][tb] = vv
            dst_["xin_used"][xslot] = vv
            dst_["osb_used"][oslot] = E_("gpsimd", (lambda i0, oslot: lambda eng: eng.dma_start(
                out=OUT[i0:i0 + 128, :], in_=osb[oslot][:]))(i0, oslot),
                waits=(("v", vv),), inc=f"ou{oslot}", dma=True)
        dst_["xnd_used"][xdslot] = sch.cnt["t"]
        dst_["sgsb_used"] = sch.cnt["t"]

    if "d" in phases:
        for it in range(ITC):
            phase_d_itile(it)

    E_("gpsimd", None, waits=(("ou0", sch.cnt["ou0"]), ("ou1", sch.cnt["ou1"])))

    from contextlib import ExitStack
    names = sorted(sch.cnt.keys())
    with ExitStack() as stack:
        SEM = {nm: stack.enter_context(nc.semaphore("sem_" + nm)) for nm in names}
        block = stack.enter_context(nc.Block())

        def replay(eng, name):
            for fn, waits, inc, n in sch.ops[name]:
                for snm, val in waits:
                    eng.wait_ge(SEM[snm], val)
                if fn is None:
                    continue
                inst = fn(eng)
                if inc is not None:
                    inst.then_inc(SEM[inc], n)

        @block.sync
        def _(eng):
            replay(eng, "sync")

        @block.gpsimd
        def _(eng):
            replay(eng, "gpsimd")

        @block.tensor
        def _(eng):
            replay(eng, "tensor")

        @block.vector
        def _(eng):
            replay(eng, "vector")

        @block.scalar
        def _(eng):
            replay(eng, "scalar")

    free_from(0)
    return nc


def prepare_inputs(L0_attn_x, L0_attn_g, L0_attn_uv_w, L0_attn_gamma,
                   L0_attn_beta, L0_attn_w_rel, L0_attn_o_w, L0_attn_res_scale,
                   L0_ffn_W1, L0_ffn_b1, L0_ffn_W2, L0_ffn_b2):
    f = np.float32
    x = np.asarray(L0_attn_x, f)
    uvwT = np.ascontiguousarray(np.asarray(L0_attn_uv_w, f).T)
    owT = np.ascontiguousarray(np.asarray(L0_attn_o_w, f).T)
    w1T = np.ascontiguousarray(np.asarray(L0_ffn_W1, f).T)
    w2T = np.ascontiguousarray(np.asarray(L0_ffn_W2, f).T)
    gamma = np.asarray(L0_attn_gamma, f)
    beta = np.asarray(L0_attn_beta, f)
    w_rel = np.asarray(L0_attn_w_rel, f)
    gsh = np.roll(gamma, -64, axis=1)
    bsh = np.roll(beta, -64, axis=1)
    gb = np.stack([gamma[0], beta[0], gamma[1], beta[1],
                   gsh[0], bsh[0], gsh[1], bsh[1]], axis=1).astype(f)
    gb = np.ascontiguousarray(gb)
    grep = np.broadcast_to(np.asarray(L0_attn_g, f).reshape(-1)[0:1].reshape(1, 1),
                           (128, 1)).copy()
    rs = np.ascontiguousarray(np.asarray(L0_attn_res_scale, f).reshape(8, 128).T)
    b1 = np.ascontiguousarray(np.asarray(L0_ffn_b1, f).reshape(32, 128).T)
    b2 = np.ascontiguousarray(np.asarray(L0_ffn_b2, f).reshape(8, 128).T)
    ident = np.eye(128, dtype=f)
    wrelc = np.broadcast_to(w_rel[2046:2047].reshape(1, 1), (128, 1)).astype(f).copy()
    consts = np.zeros((128, 8), f)
    consts[:, 0] = 128.0
    consts[:, 1] = 1e-5
    consts[:, 2] = 0.5
    consts[:, 3] = -0.5

    sin, cos = _trig_tables()

    per_h = {}
    for h in (0, 1):
        rot = np.r_[OWN * h:OWN * h + OWN,
                    (0 if h else OWN):(OWN if h else L)].astype(np.int64)
        idx = rot[:, None] - (OWN * h + np.arange(OWN))[None, :] + (L - 1)
        biasT = np.ascontiguousarray(
            np.take(w_rel, np.clip(idx, 0, w_rel.shape[0] - 1)).astype(f))
        cosT = np.ascontiguousarray(np.concatenate([cos[rot].T, cos[rot].T], axis=0))
        sinT = np.ascontiguousarray(np.concatenate([sin[rot].T, sin[rot].T], axis=0))
        per_h[h] = (rot, biasT, cosT, sinT)

    in_maps = []
    for c in range(8):
        b, h = c // 2, c % 2
        rot, biasT, cosT, sinT = per_h[h]
        xc = np.ascontiguousarray(x[b][rot])
        in_maps.append(dict(x=xc, uvwT=uvwT, owT=owT, w1T=w1T, w2T=w2T,
                            biasT=biasT, cosT=cosT, sinT=sinT, gb=gb, grep=grep,
                            rs=rs, b1=b1, b2=b2, ident=ident, consts=consts,
                            wrelc=wrelc))
    return in_maps


def kernel(**inputs):
    in_maps = prepare_inputs(**inputs)
    nc = build_program()
    res = run_bass_kernel_spmd(nc, in_maps, core_ids=list(range(8)))
    out = np.empty((B, L, D), np.float32)
    for c in range(8):
        b, h = c // 2, c % 2
        out[b, OWN * h:OWN * h + OWN] = res.results[c]["out"]
    return out


# version 24 (final)
# speedup vs baseline: 1.1325x; 1.0049x over previous
import numpy as np
import ml_dtypes
import concourse.bass as bass
import concourse.mybir as mybir
from concourse.ap import AP
from concourse.bass_utils import run_bass_kernel_spmd

F32 = mybir.dt.float32
BF16 = mybir.dt.bfloat16
AF = mybir.ActivationFunctionType
ALU = mybir.AluOpType
AX = mybir.AxisListType.X

B, L, D = 4, 4096, 1024
E, S, FF = 2048, 128, 4096
OWN = 2048
NCH = 8
NG = 32
ITC = 4
EPS = 1e-5

ENGINES = ("sync", "gpsimd", "tensor", "vector", "scalar")


class Sched:

    def __init__(self):
        from collections import defaultdict
        self.ops = {e: [] for e in ENGINES}
        self.cnt = defaultdict(int)
        self.dma_eng = {}

    def emit(self, eng, fn, waits=(), inc=None, dma=False, pre_drain=False):
        if fn is not None and (eng in ("vector", "scalar")
                               or (eng == "gpsimd" and not dma)):
            fn = (lambda f: lambda e_: (e_.drain(), f(e_))[1])(fn)
        val = None
        if inc is not None:
            self.cnt[inc] += 16 if dma else 1
            val = self.cnt[inc]
            if dma:
                self.dma_eng[inc] = eng
        for (s_, v_) in waits:
            assert v_ is None or isinstance(v_, int), (eng, s_, v_)
        ws = tuple((s_, v_) for (s_, v_) in waits if v_ is not None)
        self.ops[eng].append((fn, ws, inc, 16 if dma else 1))
        return val

    def tag_last(self, eng, inc, dma=False):
        self.cnt[inc] += 16 if dma else 1
        val = self.cnt[inc]
        fn, ws, inc0, n0 = self.ops[eng][-1]
        assert inc0 is None, inc0
        self.ops[eng][-1] = (fn, ws, inc, 16 if dma else 1)
        return val

    def barrier(self):
        for nm, eng in sorted(self.dma_eng.items()):
            self.ops[eng].append((None, ((nm, self.cnt[nm]),), None, 1))
        self.cnt["ph"] += len(ENGINES)
        target = self.cnt["ph"]
        for e in ENGINES:
            self.ops[e].append((lambda eng: eng.nop(), (), "ph", 1))
        for e in ENGINES:
            self.ops[e].append((None, (("ph", target),), None, 1))


def _trig_tables():
    try:
        import jax
        import jax.numpy as jnp
        cpu = jax.devices("cpu")[0]
        with jax.default_device(cpu):
            pos = jnp.arange(L, dtype=jnp.float32)[:, None]
            inv_freq = 10000.0 ** (jnp.arange(64, dtype=jnp.float32) / 64)
            t = pos * inv_freq[None, :]
            sin = np.asarray(jnp.sin(t))
            cos = np.asarray(jnp.cos(t))
    except Exception:
        inv = (10000.0 ** (np.arange(64, dtype=np.float64) / 64.0)).astype(np.float32)
        t = (np.arange(L, dtype=np.float32)[:, None] * inv[None, :]).astype(np.float32)
        sin = np.sin(t.astype(np.float64)).astype(np.float32)
        cos = np.cos(t.astype(np.float64)).astype(np.float32)
    return sin, cos



def rap(base, off, dims):
    return AP(base.tensor, off, [list(d) for d in dims])

def build_program(dump=(), phases=("p1", "c", "d")):
    nc = bass.Bass("TRN2", target_bir_lowering=False)

    X = nc.dram_tensor("x", [L, D], F32, kind="ExternalInput").ap()
    UVWT = nc.dram_tensor("uvwT", [D, 2 * E + S], F32, kind="ExternalInput").ap()
    OWT = nc.dram_tensor("owT", [E, D], F32, kind="ExternalInput").ap()
    W1T = nc.dram_tensor("w1T", [D, FF], F32, kind="ExternalInput").ap()
    W2T = nc.dram_tensor("w2T", [FF, D], F32, kind="ExternalInput").ap()
    BIAST = nc.dram_tensor("biasT", [L, OWN], F32, kind="ExternalInput").ap()
    COST = nc.dram_tensor("cosT", [128, L], F32, kind="ExternalInput").ap()
    SINT = nc.dram_tensor("sinT", [128, L], F32, kind="ExternalInput").ap()
    GB = nc.dram_tensor("gb", [128, 8], F32, kind="ExternalInput").ap()
    GREP = nc.dram_tensor("grep", [128, 1], F32, kind="ExternalInput").ap()
    RS = nc.dram_tensor("rs", [128, 8], F32, kind="ExternalInput").ap()
    B1 = nc.dram_tensor("b1", [128, 32], F32, kind="ExternalInput").ap()
    B2 = nc.dram_tensor("b2", [128, 8], F32, kind="ExternalInput").ap()
    IDENT = nc.dram_tensor("ident", [128, 128], F32, kind="ExternalInput").ap()
    CONSTS = nc.dram_tensor("consts", [128, 8], F32, kind="ExternalInput").ap()
    WRELC = nc.dram_tensor("wrelc", [128, 1], F32, kind="ExternalInput").ap()
    OUT = nc.dram_tensor("out", [OWN, D], F32, kind="ExternalOutput").ap()

    VSCR = nc.dram_tensor("v_scr", [L, E], BF16).ap()
    XNSCR = nc.dram_tensor("xn_scr", [D, OWN], BF16).ap()
    SSCR = nc.dram_tensor("s_scr", [D, OWN], BF16).ap()
    W1SCR = nc.dram_tensor("w1_scr", [D, FF], BF16).ap()
    UVUSCR = nc.dram_tensor("uvu_scr", [D, E], BF16).ap()
    W2SCR = nc.dram_tensor("w2_scr", [FF, D], BF16).ap()

    dumps = {}
    if dump:
        shapes = {"xn": [D, OWN], "v": [L, E], "k": [128, L], "q": [128, OWN],
                  "alpha": [128, 32], "s": [D, OWN], "w1t": [D, FF],
                  "w2t": [FF, D]}
        for name in dump:
            dumps[name] = nc.dram_tensor("dump_" + name, shapes[name], F32,
                                         kind="ExternalOutput").ap()

    sch = Sched()
    E_ = sch.emit
    guards = []

    def sb(name, shape, dtype):
        g = nc.sbuf_tensor("sb_" + name, shape, dtype)
        t = g.__enter__()
        guards.append(g)
        return t

    def free_from(mark):
        while len(guards) > mark:
            guards.pop().__exit__(None, None, None)

    consts = sb("consts", [128, 8], F32)
    gb = sb("gbv", [128, 8], F32)
    gbq = sb("gbq", [128, 4], F32)
    grep = sb("grep", [128, 1], F32)
    isq = sb("isq", [128, 1], F32)
    rs = sb("rs", [128, 8], F32)
    b1s = sb("b1s", [128, 32], F32)
    b2s = sb("b2s", [128, 8], F32)
    identb = sb("identb", [128, 128], BF16)
    identf = sb("identf", [128, 128], F32)
    alpha = sb("alpha", [128, NG], F32)
    kbf = sb("kbf", [128, L], BF16)
    qbf = sb("qbf", [128, OWN], BF16)
    wacc = sb("wacc", [128, 2], F32)
    wsc = sb("wsc", [128, 2], F32)
    wisq = sb("wisq", [128, 1], F32)

    nc.const_aps.register(F32, 0.5, consts[:, 2:3])
    nc.const_aps.register(F32, -0.5, consts[:, 3:4])

    ps = [nc.psum_tensor(f"ps{i}", [128, 512], F32) for i in range(8)]
    psg = []
    for p in ps:
        psg.append(p.__enter__())
        guards.append(p)

    for apdst, apsrc in ((consts[:], CONSTS), (gb[:], GB), (grep[:], GREP),
                         (rs[:], RS), (b1s[:], B1), (b2s[:], B2),
                         (identf[:], IDENT)):
        E_("sync", (lambda d_, s_: lambda eng: eng.dma_start(out=d_, in_=s_))(apdst, apsrc),
           inc="ld0", dma=True)
    d0 = sch.cnt["ld0"]
    E_("gpsimd", lambda eng: eng.dma_start(out=identb[:], in_=IDENT), inc="ldg", dma=True)
    E_("scalar", lambda eng: eng.activation(isq[:], consts[:, 0:1], AF.Ln),
       waits=(("ld0", d0),))
    E_("scalar", lambda eng: eng.drain())
    sv = E_("scalar", lambda eng: eng.activation(isq[:], isq[:], AF.Exp, scale=-0.5),
            inc="s")
    E_("sync", lambda eng: eng.dma_start(out=wisq[:], in_=WRELC), inc="ldw", dma=True)
    E_("vector", lambda eng: eng.tensor_scalar_mul(gbq[:, 0:2], gb[:, 0:2], isq[:, 0:1]),
       waits=(("s", sv),))
    E_("vector", lambda eng: eng.tensor_mul(wisq[:], wisq[:], isq[:, 0:1]),
       waits=(("ldw", sch.cnt["ldw"]),))
    p0_v = E_("vector", lambda eng: eng.tensor_scalar_mul(gbq[:, 2:4], gb[:, 4:6], isq[:, 0:1]),
              inc="v")

    mark_p1 = len(guards)
    cosT = sb("cosT", [128, L], F32)
    sinT = sb("sinT", [128, L], F32)
    E_("sync", lambda eng: eng.dma_start(out=cosT[:], in_=COST), inc="trg", dma=True)
    E_("sync", lambda eng: eng.dma_start(out=sinT[:], in_=SINT), inc="trg", dma=True)
    uvb = [sb(f"uvb{i}", [128, E + S], BF16) for i in range(8)]
    xtm = [sb(f"xtm{i}", [128, L], F32) for i in range(3)]
    sqscr = sb("sqscr", [128, 1024], F32)
    xnt = [sb(f"xnt{i}", [128, L], BF16) for i in range(2)]
    xnt_used = [None, None]
    xnf = [sb(f"xnf{i}", [128, L], BF16) for i in range(2)]
    vout = [sb(f"vout{i}", [128, E], BF16) for i in range(3)]
    tq = sb("tq", [128, 512], F32)
    tk = sb("tk", [128, 512], F32)
    rt = [sb(f"rt{i}", [128, 512], F32) for i in range(2)]
    tqs = sb("tqs", [128, 512], F32)
    tks = sb("tks", [128, 512], F32)
    wst = [sb(f"wst{i}", [128, 2048], F32) for i in range(2)]
    stat = sb("stat", [128, 40], F32)
    btm2 = [sb(f"btm{i}", [128, 128], BF16) for i in range(2)]
    ones_f = sb("ones_f", [128, 1], F32)
    onesr = sb("onesr", [1, 128], F32)

    for dc in range(8):
        E_("gpsimd", (lambda dc: lambda eng: eng.dma_start(
            out=uvb[dc][:], in_=UVWT[128 * dc:128 * dc + 128, E:]))(dc),
           inc="uvb", dma=True)
    uvb_gd = sch.cnt["uvb"]
    for dc in range(8):
        E_("gpsimd", (lambda dc: lambda eng: eng.dma_start(
            out=UVUSCR[128 * dc:128 * dc + 128, :],
            in_=UVWT[128 * dc:128 * dc + 128, 0:E]))(dc),
           inc="uvc", dma=True)

    E_("vector", lambda eng: eng.memset(wacc[:], 0.0), inc="v")

    wslot_used = [None, None]
    wacc_v = [None]

    def w_scale_stream(i):
        slot = i % 2
        if i < 16:
            src = W1T[128 * (i // 2):128 * (i // 2) + 128,
                      2048 * (i % 2):2048 * (i % 2) + 2048]
            col = 0
        else:
            j = i - 16
            src = rap(W2T, 256 * j * D, [[D, 128], [128 * D, 2], [1, D]])
            col = 1
        dv = E_("gpsimd", (lambda s_, sl: lambda eng: eng.dma_start(out=wst[sl][:], in_=s_))(src, slot),
                waits=((wslot_used[slot],) if wslot_used[slot] else ()), inc=f"ws{slot}", dma=True)
        E_("vector", (lambda sl: lambda eng: eng.reduce_sum(
            stat[:, 36:37], wst[sl][:], axis=AX, apply_absolute_value=True))(slot),
           waits=((f"ws{slot}", dv),))
        wacc_v[0] = E_("vector", (lambda c_: lambda eng: eng.tensor_add(
            wacc[:, c_:c_ + 1], wacc[:, c_:c_ + 1], stat[:, 36:37]))(col), inc="v")
        wslot_used[slot] = ("v", wacc_v[0])

    xtm_used = [None, None, None]
    xnf_used = [None, None]
    vout_used = [None, None, None]
    tb_used = [None, None]
    vb_used = [None, None]
    bb_used = [None, None]
    b5_used = [None]
    rope_v = [None]

    s_norms = {}
    dx_store = {}
    stats_sv = [None]

    def p1_xdma(c):
        slot = c % 3
        src = rap(X, 512 * c * D, [[D, 128], [128 * D, 4], [1, D]])
        dx_store[c] = E_("sync", (lambda s_, sl: lambda eng: eng.dma_start(
            out=xtm[sl][:], in_=s_))(src, slot),
            waits=((xtm_used[slot],) if xtm_used[slot] else ()), inc=f"x{slot}", dma=True)

    def p1_stats_a(c):
        slot = c % 3
        dx = dx_store[c]

        for g4 in range(4):
            xt = xtm[slot][:, 1024 * g4:1024 * g4 + 1024]
            E_("vector", (lambda xt, g4: lambda eng: eng.reduce_sum(
                stat[:, 10 + g4:11 + g4], xt, axis=AX))(xt, g4),
               waits=((f"x{slot}", dx),) if g4 == 0 else ())
        vv = E_("vector", lambda eng: eng.tensor_scalar_mul(
            stat[:, 0:4], stat[:, 10:14], -1.0 / 1024.0), inc="v")
        for g4 in range(4):
            xt = xtm[slot][:, 1024 * g4:1024 * g4 + 1024]
            E_("scalar", (lambda xt, g4: lambda eng: eng.activation(
                sqscr[:], xt, AF.Square, bias=stat[:, g4:g4 + 1],
                accum_out=stat[:, 4 + g4:5 + g4]))(xt, g4),
                waits=(("v", vv),) if g4 == 0 else ())
        stats_sv[0] = sch.tag_last("scalar", "s")

    def p1_stats_b(c):
        slot = c % 3
        sv = stats_sv[0]
        E_("vector", lambda eng: eng.tensor_scalar_mul(
            stat[:, 8:12], stat[:, 4:8], 1.0 / 1024.0), waits=(("s", sv),))
        E_("vector", lambda eng: eng.tensor_scalar_add(stat[:, 12:16], stat[:, 8:12], EPS))
        E_("vector", lambda eng: eng.reciprocal(stat[:, 16:20], stat[:, 12:16]))
        E_("vector", lambda eng: eng.tensor_mul(
            stat[:, 20:24], stat[:, 8:12], stat[:, 16:20]))
        E_("vector", lambda eng: eng.tensor_scalar(
            stat[:, 24:28], stat[:, 12:16], -0.5, 1.5, ALU.mult, ALU.add))
        for _ in range(3):
            E_("vector", lambda eng: eng.tensor_mul(stat[:, 16:20], stat[:, 24:28], stat[:, 24:28]))
            E_("vector", lambda eng: eng.tensor_mul(stat[:, 16:20], stat[:, 16:20], stat[:, 12:16]))
            E_("vector", lambda eng: eng.tensor_scalar(
                stat[:, 16:20], stat[:, 16:20], -0.5, 1.5, ALU.mult, ALU.add))
            E_("vector", lambda eng: eng.tensor_mul(stat[:, 24:28], stat[:, 24:28], stat[:, 16:20]))
        E_("vector", lambda eng: eng.tensor_scalar(
            stat[:, 28:32], stat[:, 20:24], -0.5, 1.5, ALU.mult, ALU.add))
        for _ in range(2):
            E_("vector", lambda eng: eng.tensor_mul(stat[:, 16:20], stat[:, 28:32], stat[:, 28:32]))
            E_("vector", lambda eng: eng.tensor_mul(stat[:, 16:20], stat[:, 16:20], stat[:, 20:24]))
            E_("vector", lambda eng: eng.tensor_scalar(
                stat[:, 16:20], stat[:, 16:20], -0.5, 1.5, ALU.mult, ALU.add))
            E_("vector", lambda eng: eng.tensor_mul(stat[:, 28:32], stat[:, 28:32], stat[:, 16:20]))
        E_("vector", lambda eng: eng.tensor_mul(stat[:, 28:32], stat[:, 28:32], stat[:, 20:24]))
        E_("vector", lambda eng: eng.tensor_scalar_max(stat[:, 28:32], stat[:, 28:32], EPS))
        E_("vector", lambda eng: eng.reciprocal(stat[:, 28:32], stat[:, 28:32]))
        E_("vector", lambda eng: eng.tensor_scalar_mul(
            alpha[:, 4 * c:4 * c + 4], stat[:, 28:32], grep[:, 0:1]))
        E_("vector", lambda eng: eng.tensor_mul(
            stat[:, 32:36], stat[:, 0:4], stat[:, 24:28]), inc="v")
        xn_t = xnt[c % 2]
        wxt = xnt_used[c % 2]
        for g4 in range(4):
            xt = xtm[slot][:, 1024 * g4:1024 * g4 + 1024]
            E_("vector", (lambda xt, g4, xn_t: lambda eng: eng.tensor_scalar(
                xn_t[:, 1024 * g4:1024 * g4 + 1024], xt, stat[:, 24 + g4:25 + g4],
                stat[:, 32 + g4:33 + g4], ALU.mult, ALU.add))(xt, g4, xn_t), inc="v",
                waits=((wxt,) if (g4 == 0 and wxt) else ()))
        s_norms[c] = ("v", sch.cnt["v"])
        xtm_used[slot] = s_norms[c]
        for i in (4 * c, 4 * c + 1, 4 * c + 2, 4 * c + 3):
            if i < 32:
                w_scale_stream(i)

    v_evacs = {}
    gd_xns = {}

    def p1_compute1(c):
        s_norm = s_norms[c]
        xn_t = xnt[c % 2]
        fslot = c % 2
        wv = xnf_used[fslot] or ()
        xf = xnf[fslot]
        for dc in range(8):
            bank = psg[dc % 2]
            for g4 in range(4):
                wt = ()
                if g4 == 0:
                    wt = (s_norm,)
                    if tb_used[dc % 2] is not None:
                        wt = wt + (tb_used[dc % 2],)
                    if dc < 2:
                        wt = wt + wv
                    if c == 0 and dc == 0:
                        wt = wt + (("ldg", 16),)
                E_("tensor", (lambda dc, g4, bank, xn_t: lambda eng: eng.matmul(
                    bank[:, 128 * g4:128 * g4 + 128],
                    xn_t[:, 1024 * g4 + 128 * dc:1024 * g4 + 128 * dc + 128],
                    identb[:], start=True, stop=True))(dc, g4, bank, xn_t),
                   waits=wt)
            tv = sch.tag_last("tensor", "t")
            tb_used[dc % 2] = ("s", E_("scalar", (lambda dc, bank, xf: lambda eng: eng.copy(
                xf[:, 512 * dc:512 * dc + 512], bank[:]))(dc, bank, xf),
                waits=(("t", tv),), inc="s"))
        xnt_used[c % 2] = ("t", sch.cnt["t"])
        v_evac = ("s", sch.cnt["s"])
        v_evacs[c] = v_evac
        gd_xn = None
        gd_xns[c] = None
        if c < 4:
            dstxn = rap(XNSCR, 512 * c, [[OWN, 128], [128 * OWN, 8], [1, 512]])
            gd_xns[c] = E_("gpsimd", (lambda xf, dstxn: lambda eng: eng.dma_start(
                out=dstxn, in_=xf[:]))(xf, dstxn),
                waits=(v_evac,), inc=f"xw{c % 2}", dma=True)

        def base_transposes(j4, silu_val):
            btm_ = btm2[j4 % 2]
            wt = (("s", silu_val),)
            if j4 == 0 and b5_used[0] is not None:
                wt = wt + (b5_used[0],)
            E_("tensor", (lambda j4, btm_: lambda eng: eng.matmul(
                psg[5][:, 128 * j4:128 * j4 + 128], btm_[:], identb[:],
                start=True, stop=True))(j4, btm_),
               waits=wt)
            E_("tensor", (lambda j4, btm_: lambda eng: eng.matmul(
                psg[6][0:64, 128 * j4:128 * j4 + 128], btm_[:, 64:128], identb[:],
                start=True, stop=True))(j4, btm_))
            E_("tensor", (lambda j4, btm_: lambda eng: eng.matmul(
                psg[6][64:128, 128 * j4:128 * j4 + 128], btm_[:, 0:64], identb[:],
                start=True, stop=True))(j4, btm_))

        silu_vals = [None] * 4
        for j4 in range(4):
            g = 4 * c + j4
            bbank = psg[4] if j4 % 2 == 0 else psg[7]
            for dc in range(8):
                wt = (("s", bb_used[j4 % 2]),) if (dc == 0 and bb_used[j4 % 2] is not None) else ()
                if j4 == 0 and dc == 0:
                    wt = wt + (v_evac,)
                if c == 0 and j4 == 0 and dc == 0:
                    wt = wt + (("uvb", uvb_gd),)
                E_("tensor", (lambda j4, dc, xf, bbank: lambda eng: eng.matmul(
                    bbank[:, 0:128], xf[:, 512 * dc + 128 * j4:512 * dc + 128 * j4 + 128],
                    uvb[dc][:, E:E + S], start=(dc == 0), stop=(dc == 7)))(j4, dc, xf, bbank),
                   waits=wt)
            tv = sch.tag_last("tensor", "t")
            silu_vals[j4] = bb_used[j4 % 2] = E_("scalar",
                (lambda g, bbank, btm_: lambda eng: eng.activation(
                    btm_[:], bbank[:, 0:128], AF.Silu, scale=alpha[:, g:g + 1]))(g, bbank, btm2[j4 % 2]),
                waits=(("t", tv),), inc="s")
            if j4 >= 1:
                base_transposes(j4 - 1, silu_vals[j4 - 1])
        base_transposes(3, silu_vals[3])
        tv = sch.tag_last("tensor", "t")
        jcols = slice(512 * c, 512 * c + 512)
        E_("vector", lambda eng: eng.tensor_scalar(
            tk[:], psg[5][:], gb[:, 2:3], gb[:, 3:4], ALU.mult, ALU.add),
            waits=(("t", tv),))
        E_("vector", lambda eng: eng.tensor_scalar(
            tks[:], psg[6][:], gb[:, 6:7], gb[:, 7:8], ALU.mult, ALU.add))
        if c < 4:
            E_("vector", lambda eng: eng.tensor_scalar(
                tq[:], psg[5][:], gbq[:, 0:1], gbq[:, 1:2], ALU.mult, ALU.add))
            E_("vector", lambda eng: eng.tensor_scalar(
                tqs[:], psg[6][:], gbq[:, 2:3], gbq[:, 3:4], ALU.mult, ALU.add))
        b5_used[0] = ("v", sch.cnt["v"] + 1)
        cs = cosT[:, jcols]
        sn = sinT[:, jcols]

        def rope(dst, t_, ts_):
            E_("vector", (lambda: lambda eng: eng.tensor_mul(
                rt[0][0:64, :], t_[0:64, :], cs[0:64, :]))(),
               waits=((("trg", 32),) if c == 0 else ()), pre_drain=True)
            E_("vector", (lambda: lambda eng: eng.tensor_mul(
                rt[1][0:64, :], ts_[0:64, :], sn[0:64, :]))())
            E_("vector", (lambda: lambda eng: eng.tensor_sub(
                dst[0:64, :], rt[0][0:64, :], rt[1][0:64, :]))(), pre_drain=True)
            E_("vector", (lambda: lambda eng: eng.tensor_mul(
                rt[0][64:128, :], t_[64:128, :], cs[64:128, :]))())
            E_("vector", (lambda: lambda eng: eng.tensor_mul(
                rt[1][64:128, :], ts_[64:128, :], sn[64:128, :]))())
            return E_("vector", (lambda: lambda eng: eng.tensor_add(
                dst[64:128, :], rt[0][64:128, :], rt[1][64:128, :]))(), inc="v",
                pre_drain=True)

        rope(kbf[:, jcols], tk, tks)
        if c < 4:
            rope(qbf[:, jcols], tq, tqs)
        rope_v[0] = sch.cnt["v"]


    def p1_compute2(c):
        fslot = c % 2
        xf = xnf[fslot]
        v_evac = v_evacs[c]
        gd_xn = gd_xns[c]
        vslot = c % 3
        for j4 in range(4):
            g = 4 * c + j4
            for eb in range(4):
                bi = (j4 * 4 + eb) % 2
                bank = psg[2 + bi]
                for dc in range(8):
                    wt = ()
                    if dc == 0:
                        wt = (("s", vb_used[bi]),)
                        if j4 == 0 and eb == 0:
                            wt = wt + (v_evac, ("uvb", uvb_gd))
                    E_("tensor", (lambda j4, eb, dc, bank, xf: lambda eng: eng.matmul(
                        bank[:], xf[:, 512 * dc + 128 * j4:512 * dc + 128 * j4 + 128],
                        uvb[dc][:, 512 * eb:512 * eb + 512],
                        start=(dc == 0), stop=(dc == 7)))(j4, eb, dc, bank, xf),
                       waits=wt)
                tv = sch.tag_last("tensor", "t")
                wv2 = (("t", tv),)
                if eb == 0 and vout_used[vslot] is not None:
                    wv2 = wv2 + ((f"vo{vslot}", vout_used[vslot]),)
                vb_used[bi] = E_("scalar", (lambda bank, vslot, eb, g: lambda eng: eng.activation(
                    vout[vslot][:, 512 * eb:512 * eb + 512], bank[:], AF.Silu,
                    scale=alpha[:, g:g + 1]))(bank, vslot, eb, g),
                    waits=wv2, inc="s")
            sv = sch.cnt["s"]
            vout_used[vslot] = E_("gpsimd", (lambda vslot, g: lambda eng: eng.dma_start(
                out=VSCR[128 * g:128 * g + 128, :], in_=vout[vslot][:]))(vslot, g),
                waits=(("s", sv),), inc=f"vo{vslot}", dma=True)
            vslot = (vslot + 1) % 3

        xnf_used[fslot] = ((f"xw{c % 2}", gd_xn),) if gd_xn else ()
    p1_xdma(0)
    p1_xdma(1)
    p1_stats_a(0)
    p1_stats_b(0)
    for c in range(NCH):
        if c + 2 < NCH:
            p1_xdma(c + 2)
        if c + 1 < NCH:
            p1_stats_a(c + 1)
        p1_compute1(c)
        if c + 1 < NCH:
            p1_stats_b(c + 1)
        p1_compute2(c)

    E_("vector", lambda eng: eng.memset(ones_f[:], 1.0))
    vv = E_("vector", lambda eng: eng.memset(onesr[:], 1.0), inc="v")
    for col in range(2):
        E_("tensor", (lambda col: lambda eng: eng.matmul(
            psg[6][0:1, col:col + 1], wacc[:, col:col + 1], ones_f[:],
            start=True, stop=True))(col),
           waits=((("v", vv), ("v", wacc_v[0])) if col == 0 else ()))
    tv = sch.tag_last("tensor", "t")
    E_("vector", lambda eng: eng.tensor_scalar_mul(
        stat[0:1, 36:38], psg[6][0:1, 0:2], 1.0 / (D * FF)), waits=(("t", tv),))
    E_("vector", lambda eng: eng.tensor_scalar_max(stat[0:1, 36:38], stat[0:1, 36:38], 1e-8),
       pre_drain=True)
    vv = E_("vector", lambda eng: eng.reciprocal(stat[0:1, 36:38], stat[0:1, 36:38]), inc="v",
            pre_drain=True)
    E_("tensor", lambda eng: eng.matmul(psg[6][:, 4:6], onesr[:], stat[0:1, 36:38],
                                        start=True, stop=True),
       waits=(("v", vv),))
    tv = sch.tag_last("tensor", "t")
    E_("vector", lambda eng: eng.tensor_copy(wsc[:], psg[6][:, 4:6]),
       waits=(("t", tv),), inc="v")

    if dumps:
        sch.barrier()
    end_p1 = (("t", sch.cnt["t"]), ("v", sch.cnt["v"]), ("s", sch.cnt["s"]),
              ("xw0", sch.cnt["xw0"]), ("xw1", sch.cnt["xw1"]),
              ("vo0", sch.cnt["vo0"]), ("vo1", sch.cnt["vo1"]),
              ("vo2", sch.cnt["vo2"]), ("uvc", sch.cnt["uvc"]))
    free_from(mark_p1)

    if dumps:
        dscr = sb("dscr", [128, 4096], F32)
        dbb = sb("dbb", [128, 4096], BF16)
        dmw_last = [None]

        def dump_pair(rd_fn, wr_fn, w=4096):
            gv = E_("gpsimd", rd_fn, waits=((("dmw", dmw_last[0]),) if dmw_last[0] else ()),
                    inc="dmr", dma=True)
            vv = E_("vector", (lambda w: lambda eng: eng.tensor_copy(
                dscr[:, 0:w], dbb[:, 0:w]))(w), waits=(("dmr", gv),), inc="v")
            dmw_last[0] = E_("gpsimd", wr_fn, waits=(("v", vv),), inc="dmw", dma=True)

        if "k" in dumps:
            vv = E_("vector", lambda eng: eng.tensor_copy(dscr[:, 0:L], kbf[:]), inc="v")
            dmw_last[0] = E_("gpsimd", lambda eng: eng.dma_start(out=dumps["k"], in_=dscr[:, 0:L]),
               waits=(("v", vv),), inc="dmw", dma=True)
        if "q" in dumps:
            vv = E_("vector", lambda eng: eng.tensor_copy(dscr[:, 0:OWN], qbf[:]),
                    waits=(("dmw", dmw_last[0]),) if dmw_last[0] else (), inc="v")
            dmw_last[0] = E_("gpsimd", lambda eng: eng.dma_start(out=dumps["q"], in_=dscr[:, 0:OWN]),
               waits=(("v", vv),), inc="dmw", dma=True)
        if "alpha" in dumps:
            E_("gpsimd", lambda eng: eng.dma_start(out=dumps["alpha"], in_=alpha[:]),
               inc="dmw2", dma=True)
        if "xn" in dumps:
            for it in range(ITC):
                src = rap(XNSCR, 512 * it, [[OWN, 128], [128 * OWN, 8], [1, 512]])
                dst = rap(dumps["xn"], 512 * it, [[OWN, 128], [128 * OWN, 8], [1, 512]])
                dump_pair((lambda src: lambda eng: eng.dma_start(out=dbb[:], in_=src))(src),
                          (lambda dst: lambda eng: eng.dma_start(out=dst, in_=dscr[:]))(dst))
        if "v" in dumps:
            for g in range(NG):
                dump_pair((lambda g: lambda eng: eng.dma_start(
                              out=dbb[:, 0:E], in_=VSCR[128 * g:128 * g + 128, :]))(g),
                          (lambda g: lambda eng: eng.dma_start(
                              out=dumps["v"][128 * g:128 * g + 128, :], in_=dscr[:, 0:E]))(g), w=E)
        sch.barrier()
        free_from(mark_p1)

    mark_c = len(guards)
    kts = sb("kts", [128, 32 * 512], BF16)
    usb = sb("usb", [128, 4 * 2048], BF16)
    gtm = usb
    gfm = sb("gfm", [128, 16 * 512], BF16)
    vwin = [sb(f"vwin{i}", [128, 16 * 512], BF16) for i in range(2)]
    btile = [sb(f"btile{i}", [128, 512], F32) for i in range(2)]
    ttile = [sb(f"ttile{i}", [128, 512], F32) for i in range(2)]
    xnc = [sb(f"xnc{i}", [128, 8 * 512], BF16) for i in range(2)]
    uvu = [sb(f"uvu{i}", [128, 8 * 512], BF16) for i in range(2)]
    owin = [sb(f"owin{i}", [128, 16 * 128], BF16) for i in range(2)]
    sout = sb("sout", [128, 8 * 512], BF16)
    wtern = [sb(f"wtern{i}", [128, 2048], F32) for i in range(2)]
    wsAB = [sb(f"wsAB{i}", [128, 4096], BF16) for i in range(2)]

    tern_items = ([("w1", i, j) for i in range(8) for j in range(2)]
                  + [("w2", i, 0) for i in range(16)])
    tern = {"si": 0, "in_used": [None, None], "ab_used": [None, None],
            "tw": [None, None]}

    def tern_step():
        if tern["si"] >= len(tern_items):
            return
        mat, i, j = tern_items[tern["si"]]
        tern["si"] += 1
        if mat == "w1":
            src = W1T[128 * i:128 * i + 128, 2048 * j:2048 * j + 2048]
            dst = W1SCR[128 * i:128 * i + 128, 2048 * j:2048 * j + 2048]
            col = 0
        else:
            src = rap(W2T, 256 * i * D, [[D, 128], [128 * D, 2], [1, D]])
            dst = rap(W2SCR, 256 * i * D, [[D, 128], [128 * D, 2], [1, D]])
            col = 1
        ts_ = tern["si"] % 2
        wt_ = wtern[ts_]
        wab = wsAB[ts_]
        dv = E_("sync", (lambda s_, wt_: lambda eng: eng.dma_start(out=wt_[:], in_=s_))(src, wt_),
                waits=((("s", tern["in_used"][ts_]),) if tern["in_used"][ts_] else ()),
                inc=f"tn{ts_}", dma=True)
        E_("scalar", (lambda col, wt_, wab: lambda eng: eng.activation(
            wab[:, 0:2048], wt_[:], AF.Sign, bias=-0.5, scale=wsc[:, col:col + 1]))(col, wt_, wab),
           waits=((f"tn{ts_}", dv),) + ((("v", tern["ab_used"][ts_]),) if tern["ab_used"][ts_] else ())
                 + (((f"tw{ts_}", tern["tw"][ts_]),) if tern["tw"][ts_] else ()))
        tern["in_used"][ts_] = E_("scalar", (lambda col, wt_, wab: lambda eng: eng.activation(
            wab[:, 2048:4096], wt_[:], AF.Sign, bias=0.5, scale=wsc[:, col:col + 1]))(col, wt_, wab),
            inc="s")
        tern["ab_used"][ts_] = E_("vector", (lambda wab: lambda eng: eng.tensor_add(
            wab[:, 0:2048], wab[:, 0:2048], wab[:, 2048:4096]))(wab),
            waits=(("s", tern["in_used"][ts_]),)
                  + (((f"tw{ts_}", tern["tw"][ts_]),) if tern["tw"][ts_] else ()), inc="v")
        tern["tw"][ts_] = E_("gpsimd", (lambda dst, wab: lambda eng: eng.dma_start(
            out=dst, in_=wab[:, 0:2048]))(dst, wab),
           waits=(("v", tern["ab_used"][ts_]),), inc=f"tw{ts_}", dma=True)

    cst = {"xnc_used": None, "kts_used": None, "usb_used": None,
           "uvu_used": [None, None], "vwin_used": [None, None],
           "btile_used": [None] * 2, "ttile_used": [None] * 2,
           "sout_used": None, "xnc_used2": [None, None],
           "sc_bank": [None, None], "u_bank": [None, None],
           "ou_bank": [None] * 4, "tr_bank": [None, None], "sp_bank": None,
           "owin_used": [None, None]}

    kt_store = {}

    def emit_scores(it_s):
        kt_v = [None] * 32
        icols_s = slice(512 * it_s, 512 * it_s + 512)
        for jt in range(32):
            bank = psg[jt % 2]
            wt = (("v", cst["sc_bank"][jt % 2]),)
            if jt == 0 and it_s == 0:
                wt = wt + end_p1 + (("t", cst["kts_used"]),)
            tv = E_("tensor", (lambda jt, bank, icols_s: lambda eng: eng.matmul(
                bank[:], kbf[:, 128 * jt:128 * jt + 128], qbf[:, icols_s],
                start=True, stop=True))(jt, bank, icols_s),
                waits=wt, inc="t")
            tslot = jt % 2
            variable = (16 <= jt <= 4 * it_s + 19)
            if variable:
                bslot = jt % 2
                wb_ = (("v", cst["btile_used"][bslot]),)
                if it_s == 0 and jt == 16:
                    wb_ = wb_ + end_p1
                b_dv = E_("sync", (lambda jt, bslot, icols_s: lambda eng: eng.dma_start(
                    out=btile[bslot][:], in_=BIAST[128 * jt:128 * jt + 128, icols_s]))(jt, bslot, icols_s),
                    waits=wb_, inc=f"bt{bslot}", dma=True)
                E_("vector", (lambda bslot, tslot, bank: lambda eng: eng.scalar_tensor_tensor(
                    ttile[tslot][:], btile[bslot][:], isq[:, 0:1], bank[:],
                    ALU.mult, ALU.add))(bslot, tslot, bank),
                   waits=(("t", tv), (f"bt{bslot}", b_dv), ("v", cst["ttile_used"][tslot])))
            else:
                E_("vector", (lambda tslot, bank: lambda eng: eng.tensor_scalar_add(
                    ttile[tslot][:], bank[:], wisq[:, 0:1]))(tslot, bank),
                   waits=(("t", tv), ("v", cst["ttile_used"][tslot])))
            vv = E_("vector", (lambda jt, tslot: lambda eng: eng.scalar_tensor_tensor(
                kts[:, 512 * jt:512 * jt + 512], ttile[tslot][:], 0.0, ttile[tslot][:],
                ALU.max, ALU.mult))(jt, tslot), inc="v", pre_drain=True)
            if variable:
                cst["btile_used"][jt % 2] = vv
            cst["sc_bank"][jt % 2] = vv
            cst["ttile_used"][tslot] = vv
            kt_v[jt] = vv
        kt_store[it_s] = kt_v

    def phase_c_itile(it):
        icols = slice(512 * it, 512 * it + 512)
        xslot = it % 2
        xn = xnc[xslot]
        src = rap(XNSCR, 512 * it, [[OWN, 128], [128 * OWN, 8], [1, 512]])
        wx = (("v", cst["xnc_used2"][xslot]),)
        if it == 0:
            wx = wx + end_p1
        xn_dv = E_("sync", (lambda src, xn: lambda eng: eng.dma_start(out=xn[:], in_=src))(src, xn),
                   waits=wx, inc=f"xnc{xslot}", dma=True)

        for eb in range(4):
            uslot = eb % 2
            for dc in range(8):
                wt = (("t", cst["uvu_used"][uslot]),) if dc == 0 else ()
                if dc == 0 and eb == 0 and cst["uvu_used"][0] is None:
                    wt = wt + end_p1
                E_("sync", (lambda uslot, dc, eb: lambda eng: eng.dma_start(
                    out=uvu[uslot][:, 512 * dc:512 * dc + 512],
                    in_=UVUSCR[128 * dc:128 * dc + 128, 512 * eb:512 * eb + 512]))(uslot, dc, eb),
                   waits=wt, inc=f"uu{uslot}", dma=True)
            u_dv = sch.cnt[f"uu{uslot}"]
            for isub in range(4):
                bank = psg[2 + (isub % 2)]
                for dc in range(8):
                    wt = ()
                    if dc == 0:
                        wt = ((f"uu{eb % 2}", u_dv), (f"xnc{xslot}", xn_dv))
                        if cst["u_bank"][isub % 2] is not None:
                            wt = wt + (cst["u_bank"][isub % 2],)
                        if eb == 0 and isub == 0:
                            wt = wt + (("v", cst["usb_used"]),)
                    E_("tensor", (lambda isub, dc, bank, uslot, xn: lambda eng: eng.matmul(
                        bank[:], xn[:, 512 * dc + 128 * isub:512 * dc + 128 * isub + 128],
                        uvu[uslot][:, 512 * dc:512 * dc + 512],
                        start=(dc == 0), stop=(dc == 7)))(isub, dc, bank, uslot, xn),
                       waits=wt)
                tv = sch.tag_last("tensor", "t")
                if isub == 3:
                    cst["uvu_used"][uslot] = tv
                g = 4 * it + isub
                cst["u_bank"][isub % 2] = ("s", E_("scalar",
                    (lambda bank, isub, eb, g: lambda eng: eng.activation(
                        usb[:, 2048 * isub + 512 * eb:2048 * isub + 512 * eb + 512],
                        bank[:], AF.Silu, scale=alpha[:, g:g + 1]))(bank, isub, eb, g),
                    waits=(("t", tv),), inc="s"))
            for _ in range(2):
                tern_step()
        u_done_s = sch.cnt["s"]
        for eb in range(4):
            for half in range(2):
                w2i = (2 * eb + half) % 2
                vw = vwin[w2i]
                wt = ((cst["vwin_used"][w2i],) if cst["vwin_used"][w2i] else ())
                vq = "sync"
                for q4 in range(2):
                    src = rap(VSCR, (2048 * half + 1024 * q4) * E + 512 * eb,
                              [[E, 128], [128 * E, 8], [1, 512]])
                    E_(vq, (lambda src, q4, vw: lambda eng: eng.dma_start(
                        out=vw[:, 4096 * q4:4096 * q4 + 4096], in_=src))(src, q4, vw),
                       waits=(wt if q4 == 0 else ()), inc=f"vw{w2i}", dma=True)
                v_dv = sch.cnt[f"vw{w2i}"]
                for isub in range(4):
                    bank = psg[4 + isub]
                    for jt16 in range(16):
                        jt = 16 * half + jt16
                        wt2 = ()
                        if jt16 == 0:
                            wt2 = ((f"vw{w2i}", v_dv),)
                            if half == 0 and cst["ou_bank"][isub] is not None:
                                wt2 = wt2 + (cst["ou_bank"][isub],)
                        if isub == 0 and eb == 0:
                            wt2 = wt2 + (("v", kt_store[it][jt]),)
                        E_("tensor", (lambda isub, jt, jt16, half, bank, vw: lambda eng: eng.matmul(
                            bank[:], kts[:, 512 * jt + 128 * isub:512 * jt + 128 * isub + 128],
                            vw[:, 512 * jt16:512 * jt16 + 512],
                            start=(half == 0 and jt16 == 0),
                            stop=(half == 1 and jt16 == 15)))(isub, jt, jt16, half, bank, vw),
                           waits=wt2)
                tv = sch.tag_last("tensor", "t")
                cst["vwin_used"][w2i] = ("t", tv)
                ou_last = tv
            for isub in range(4):
                bank = psg[4 + isub]
                cst["ou_bank"][isub] = ("v", E_("vector",
                    (lambda bank, isub, eb: lambda eng: eng.tensor_mul(
                        gtm[:, 2048 * isub + 512 * eb:2048 * isub + 512 * eb + 512],
                        bank[:], usb[:, 2048 * isub + 512 * eb:2048 * isub + 512 * eb + 512]))(bank, isub, eb),
                    waits=(("t", ou_last), ("s", u_done_s)), inc="v"))
        gate_done_v = sch.cnt["v"]
        cst["usb_used"] = gate_done_v

        if it + 1 < ITC:
            emit_scores(it + 1)

        for ec in range(16):
            bank = psg[4 + (ec % 2)]
            for isub in range(4):
                wt = ()
                if isub == 0:
                    wt = (("v", gate_done_v), ("s", cst["tr_bank"][ec % 2]))
                E_("tensor", (lambda ec, isub, bank: lambda eng: eng.matmul(
                    bank[:, 128 * isub:128 * isub + 128],
                    gtm[:, 2048 * isub + 128 * ec:2048 * isub + 128 * ec + 128],
                    identb[:], start=True, stop=True))(ec, isub, bank),
                   waits=wt)
            tv = sch.tag_last("tensor", "t")
            cst["tr_bank"][ec % 2] = E_("scalar", (lambda ec, bank: lambda eng: eng.copy(
                gfm[:, 512 * ec:512 * ec + 512], bank[:]))(ec, bank),
                waits=(("t", tv),), inc="s")
        gfm_done = sch.cnt["s"]

        for dc in range(8):
            oslot = dc % 2
            for h2 in range(2):
                src = rap(OWT, 1024 * h2 * D + 128 * dc,
                          [[D, 128], [128 * D, 8], [1, 128]])
                wt = (("t", cst["owin_used"][oslot]),) if h2 == 0 else ()
                E_("gpsimd", (lambda src, oslot, h2: lambda eng: eng.dma_start(
                    out=owin[oslot][:, 1024 * h2:1024 * h2 + 1024], in_=src))(src, oslot, h2),
                   waits=wt, inc=f"ow{oslot}", dma=True)
            o_dv = sch.cnt[f"ow{oslot}"]
            bank = psg[6 + (dc % 2)]
            for ec in range(16):
                wt = ()
                if ec == 0:
                    wt = (("v", cst["sp_bank"] if dc >= 2 else None),
                          ((f"ow{dc % 2}"), o_dv), ("s", gfm_done))
                E_("tensor", (lambda dc, ec, oslot, bank: lambda eng: eng.matmul(
                    bank[:], owin[oslot][:, 128 * ec:128 * ec + 128],
                    gfm[:, 512 * ec:512 * ec + 512],
                    start=(ec == 0), stop=(ec == 15)))(dc, ec, oslot, bank),
                   waits=wt)
            tv = sch.tag_last("tensor", "t")
            cst["owin_used"][oslot] = tv
            wt = (("t", tv),)
            if dc == 0 and cst["sout_used"] is not None:
                wt = wt + (("so", cst["sout_used"]),)
            cst["sp_bank"] = E_("vector", (lambda dc, bank, xn: lambda eng: eng.scalar_tensor_tensor(
                sout[:, 512 * dc:512 * dc + 512], xn[:, 512 * dc:512 * dc + 512],
                rs[:, dc:dc + 1], bank[:], ALU.mult, ALU.add))(dc, bank, xn),
                waits=wt, inc="v")
        s_done_v = sch.cnt["v"]
        cst["xnc_used2"][xslot] = s_done_v
        cst["kts_used"] = sch.cnt["t"]
        cst["marks"] = {"t": sch.cnt["t"], "gfm_s": gfm_done, "us": u_done_s}
        cst["ou_bank"][0] = ("s", gfm_done)
        cst["ou_bank"][1] = ("s", gfm_done)
        cst["ou_bank"][2] = ("v", s_done_v)
        cst["ou_bank"][3] = ("v", s_done_v)
        dst = rap(SSCR, 512 * it, [[OWN, 128], [128 * OWN, 8], [1, 512]])
        cst["sout_used"] = E_("gpsimd", (lambda dst: lambda eng: eng.dma_start(
            out=dst, in_=sout[:]))(dst), waits=(("v", s_done_v),), inc="so", dma=True)

    if "c" in phases:
        emit_scores(0)
        for it in range(ITC):
            phase_c_itile(it)
        while tern["si"] < len(tern_items):
            tern_step()

    if dumps or "c" not in phases:
        sch.barrier()
    end_c = (("t", sch.cnt["t"]), ("v", sch.cnt["v"]), ("s", sch.cnt["s"]),
             ("tw0", sch.cnt["tw0"]), ("so", sch.cnt["so"]),
             ("tw1", sch.cnt["tw1"]))
    free_from(mark_c)

    if dumps and ("s" in dumps or "w1t" in dumps or "w2t" in dumps):
        dscr2 = sb("dscr2", [128, 4096], F32)
        dbb2 = sb("dbb2", [128, 4096], BF16)
        d2_last = [None]

        def dump2(rd_fn, wr_fn, mul):
            gv = E_("gpsimd", rd_fn, waits=((("dw2", d2_last[0]),) if d2_last[0] else ()),
                    inc="dr2", dma=True)
            if mul:
                vv = E_("vector", lambda eng: eng.tensor_scalar_mul(dscr2[:], dbb2[:], 0.5),
                        waits=(("dr2", gv),), inc="v")
            else:
                vv = E_("vector", lambda eng: eng.tensor_copy(dscr2[:], dbb2[:]),
                        waits=(("dr2", gv),), inc="v")
            d2_last[0] = E_("gpsimd", wr_fn, waits=(("v", vv),), inc="dw2", dma=True)

        if "s" in dumps:
            for it in range(ITC):
                src = rap(SSCR, 512 * it, [[OWN, 128], [128 * OWN, 8], [1, 512]])
                dst = rap(dumps["s"], 512 * it, [[OWN, 128], [128 * OWN, 8], [1, 512]])
                dump2((lambda src: lambda eng: eng.dma_start(out=dbb2[:], in_=src))(src),
                      (lambda dst: lambda eng: eng.dma_start(out=dst, in_=dscr2[:]))(dst), False)
        if "w1t" in dumps:
            for i in range(8):
                dump2((lambda i: lambda eng: eng.dma_start(
                          out=dbb2[:], in_=W1SCR[128 * i:128 * i + 128, :]))(i),
                      (lambda i: lambda eng: eng.dma_start(
                          out=dumps["w1t"][128 * i:128 * i + 128, :], in_=dscr2[:]))(i), True)
        if "w2t" in dumps:
            for i in range(8):
                src = rap(W2SCR, 512 * i * D, [[D, 128], [128 * D, 4], [1, D]])
                dst = rap(dumps["w2t"], 512 * i * D, [[D, 128], [128 * D, 4], [1, D]])
                dump2((lambda src: lambda eng: eng.dma_start(out=dbb2[:], in_=src))(src),
                      (lambda dst: lambda eng: eng.dma_start(out=dst, in_=dscr2[:]))(dst), True)
        sch.barrier()
        free_from(mark_c)

    w1sb = [sb(f"w1sb{i}", [128, FF], BF16) for i in range(8)]
    hsb = sb("hsb", [128, 32 * 512], BF16)
    w2win = [sb(f"w2win{i}", [128, 32 * 128], BF16) for i in range(2)]
    xnd = [sb(f"xnd{i}", [128, 8 * 512], BF16) for i in range(2)]
    std = sb("std", [128, 8 * 512], BF16)
    gsb = sb("gsb", [128, 8 * 512], F32)
    sgsb = sb("sgsb", [128, 8 * 512], F32)
    xin = [sb(f"xin{i}", [128, 1024], F32) for i in range(2)]
    osb = [sb(f"osb{i}", [128, 1024], F32) for i in range(2)]

    mk = cst["marks"] if "c" in phases else {"t": 0, "gfm_s": 0, "us": 0}
    w1_early = (("t", mk["t"]), ("s", mk["gfm_s"]), ("s", mk["us"]),
                ("tw0", min(8 * 16, sch.cnt["tw0"])),
                ("tw1", min(8 * 16, sch.cnt["tw1"])))
    if "d" in phases:
        for dc in range(8):
            E_("sync", (lambda dc: lambda eng: eng.dma_start(
                out=w1sb[dc][:], in_=W1SCR[128 * dc:128 * dc + 128, :]))(dc),
               waits=(w1_early if dc == 0 else ()), inc="w1l", dma=True)
    w1_dv = sch.cnt["w1l"]

    dst_ = {"xnd_used": [None, None], "std_used": None, "hsb_used": None,
            "gsb_used": None, "sgsb_used": None,
            "h_bank": [None, None], "o_bank": [None, None],
            "w2_used": [None, None], "tr_bank": [None, None],
            "xin_used": [None, None], "osb_used": [None, None]}

    def phase_d_itile(it):
        icols = slice(512 * it, 512 * it + 512)
        xdslot = it % 2
        xnd_ = xnd[xdslot]
        src = rap(XNSCR, 512 * it, [[OWN, 128], [128 * OWN, 8], [1, 512]])
        wxd = (("t", dst_["xnd_used"][xdslot]),)
        if it == 0:
            wxd = wxd + (end_c[0], end_c[1])
        xn_dv = E_("sync", (lambda src, xnd_: lambda eng: eng.dma_start(out=xnd_[:], in_=src))(src, xnd_),
                   waits=wxd, inc=f"xnd{xdslot}", dma=True)
        src = rap(SSCR, 512 * it, [[OWN, 128], [128 * OWN, 8], [1, 512]])
        wsd = (("v", dst_["std_used"]),)
        if it == 0:
            wsd = wsd + (end_c[4],)
        s_dv = E_("sync", (lambda src: lambda eng: eng.dma_start(out=std[:], in_=src))(src),
                  waits=wsd, inc="ssd", dma=True)

        for fc in range(32):
            bank = psg[fc % 2]
            for dc in range(8):
                wt = ()
                if dc == 0:
                    wt = (("s", dst_["h_bank"][fc % 2]), (f"xnd{xdslot}", xn_dv),
                          ("w1l", w1_dv))
                E_("tensor", (lambda fc, dc, bank, xnd_: lambda eng: eng.matmul(
                    bank[:], w1sb[dc][:, 128 * fc:128 * fc + 128],
                    xnd_[:, 512 * dc:512 * dc + 512],
                    start=(dc == 0), stop=(dc == 7)))(fc, dc, bank, xnd_),
                   waits=wt)
            tv = sch.tag_last("tensor", "t")
            wt = (("t", tv),)
            if fc == 0:
                wt = wt + (("t", dst_["hsb_used"]),)
            dst_["h_bank"][fc % 2] = E_("scalar", (lambda fc, bank: lambda eng: eng.activation(
                hsb[:, 512 * fc:512 * fc + 512], bank[:], AF.Gelu,
                bias=b1s[:, fc:fc + 1], scale=0.5))(fc, bank),
                waits=wt, inc="s")
        h_done_s = sch.cnt["s"]

        for dc in range(8):
            wslot = dc % 2
            for h2 in range(2):
                src = rap(W2SCR, 2048 * h2 * D + 128 * dc,
                          [[D, 128], [128 * D, 16], [1, 128]])
                wt = (("t", dst_["w2_used"][wslot]),) if h2 == 0 else ()
                if it == 0 and dc == 0 and h2 == 0:
                    wt = wt + (end_c[3], end_c[5])
                E_("sync", (lambda src, wslot, h2: lambda eng: eng.dma_start(
                    out=w2win[wslot][:, 2048 * h2:2048 * h2 + 2048], in_=src))(src, wslot, h2),
                   waits=wt, inc=f"w2{wslot}", dma=True)
            w2_dv = sch.cnt[f"w2{wslot}"]
            bank = psg[2 + (dc % 2)]
            for fc in range(32):
                wt = ()
                if fc == 0:
                    wt = (("s", dst_["o_bank"][dc % 2]), (f"w2{dc % 2}", w2_dv), ("s", h_done_s))
                    if dc == 0:
                        wt = wt + (("v", dst_["gsb_used"]),)
                E_("tensor", (lambda dc, fc, wslot, bank: lambda eng: eng.matmul(
                    bank[:], w2win[wslot][:, 128 * fc:128 * fc + 128],
                    hsb[:, 512 * fc:512 * fc + 512],
                    start=(fc == 0), stop=(fc == 31)))(dc, fc, wslot, bank),
                   waits=wt)
            tv = sch.tag_last("tensor", "t")
            dst_["w2_used"][wslot] = tv
            wgs = (("t", tv),)
            if it == 0 and dc == 0:
                wgs = wgs + (end_c[4],)
            dst_["o_bank"][dc % 2] = E_("scalar", (lambda dc, bank: lambda eng: eng.activation(
                gsb[:, 512 * dc:512 * dc + 512], bank[:], AF.Silu,
                bias=b2s[:, dc:dc + 1], scale=0.5))(dc, bank),
                waits=wgs, inc="s")
        g_done_s = sch.cnt["s"]
        dst_["hsb_used"] = sch.cnt["t"]

        for dc in range(8):
            wt = ()
            if dc == 0:
                wt = (("s", g_done_s), ("ssd", s_dv), ("t", dst_["sgsb_used"]))
                if it == 0:
                    wt = wt + (end_c[3], end_c[4], end_c[5])
            E_("vector", (lambda dc: lambda eng: eng.tensor_mul(
                sgsb[:, 512 * dc:512 * dc + 512], gsb[:, 512 * dc:512 * dc + 512],
                std[:, 512 * dc:512 * dc + 512]))(dc), waits=wt)
        sg_done_v = sch.tag_last("vector", "v")
        dst_["std_used"] = sg_done_v
        dst_["gsb_used"] = sg_done_v

        for isub in range(4):
            i0 = 512 * it + 128 * isub
            xslot = isub % 2
            wxi = (("v", dst_["xin_used"][xslot]),)
            if it == 0 and isub == 0:
                wxi = wxi + (end_c[3], end_c[5])
            x_dv = E_("sync", (lambda i0, xslot: lambda eng: eng.dma_start(
                out=xin[xslot][:], in_=X[i0:i0 + 128, :]))(i0, xslot),
                waits=wxi, inc=f"xi{xslot}", dma=True)
            tb = isub % 2
            bankA = psg[4 + 2 * tb]
            bankB = psg[5 + 2 * tb]
            for dc in range(8):
                bank = bankA if dc < 4 else bankB
                off = 128 * (dc % 4)
                wt = ()
                if dc == 0:
                    wt = (("v", sg_done_v), ("v", dst_["tr_bank"][tb]))
                E_("tensor", (lambda dc, isub, bank, off: lambda eng: eng.matmul(
                    bank[:, off:off + 128],
                    sgsb[:, 512 * dc + 128 * isub:512 * dc + 128 * isub + 128],
                    identf[:], start=True, stop=True))(dc, isub, bank, off),
                   waits=wt)
            tv = sch.tag_last("tensor", "t")
            oslot = isub % 2
            E_("vector", (lambda bankA, oslot, xslot: lambda eng: eng.tensor_add(
                osb[oslot][:, 0:512], bankA[:], xin[xslot][:, 0:512]))(bankA, oslot, xslot),
               waits=(("t", tv), (f"xi{xslot}", x_dv))
                     + (((f"ou{oslot}", dst_["osb_used"][oslot]),)
                        if dst_["osb_used"][oslot] else ()))
            vv = E_("vector", (lambda bankB, oslot, xslot: lambda eng: eng.tensor_add(
                osb[oslot][:, 512:1024], bankB[:], xin[xslot][:, 512:1024]))(bankB, oslot, xslot),
               inc="v")
            dst_["tr_bank"][tb] = vv
            dst_["xin_used"][xslot] = vv
            dst_["osb_used"][oslot] = E_("gpsimd", (lambda i0, oslot: lambda eng: eng.dma_start(
                out=OUT[i0:i0 + 128, :], in_=osb[oslot][:]))(i0, oslot),
                waits=(("v", vv),), inc=f"ou{oslot}", dma=True)
        dst_["xnd_used"][xdslot] = sch.cnt["t"]
        dst_["sgsb_used"] = sch.cnt["t"]

    if "d" in phases:
        for it in range(ITC):
            phase_d_itile(it)

    E_("gpsimd", None, waits=(("ou0", sch.cnt["ou0"]), ("ou1", sch.cnt["ou1"])))

    from contextlib import ExitStack
    names = sorted(sch.cnt.keys())
    with ExitStack() as stack:
        SEM = {nm: stack.enter_context(nc.semaphore("sem_" + nm)) for nm in names}
        block = stack.enter_context(nc.Block())

        def replay(eng, name):
            for fn, waits, inc, n in sch.ops[name]:
                for snm, val in waits:
                    eng.wait_ge(SEM[snm], val)
                if fn is None:
                    continue
                inst = fn(eng)
                if inc is not None:
                    inst.then_inc(SEM[inc], n)

        @block.sync
        def _(eng):
            replay(eng, "sync")

        @block.gpsimd
        def _(eng):
            replay(eng, "gpsimd")

        @block.tensor
        def _(eng):
            replay(eng, "tensor")

        @block.vector
        def _(eng):
            replay(eng, "vector")

        @block.scalar
        def _(eng):
            replay(eng, "scalar")

    free_from(0)
    return nc


def prepare_inputs(L0_attn_x, L0_attn_g, L0_attn_uv_w, L0_attn_gamma,
                   L0_attn_beta, L0_attn_w_rel, L0_attn_o_w, L0_attn_res_scale,
                   L0_ffn_W1, L0_ffn_b1, L0_ffn_W2, L0_ffn_b2):
    f = np.float32
    x = np.asarray(L0_attn_x, f)
    uvwT = np.ascontiguousarray(np.asarray(L0_attn_uv_w, f).T)
    owT = np.ascontiguousarray(np.asarray(L0_attn_o_w, f).T)
    w1T = np.ascontiguousarray(np.asarray(L0_ffn_W1, f).T)
    w2T = np.ascontiguousarray(np.asarray(L0_ffn_W2, f).T)
    gamma = np.asarray(L0_attn_gamma, f)
    beta = np.asarray(L0_attn_beta, f)
    w_rel = np.asarray(L0_attn_w_rel, f)
    gsh = np.roll(gamma, -64, axis=1)
    bsh = np.roll(beta, -64, axis=1)
    gb = np.stack([gamma[0], beta[0], gamma[1], beta[1],
                   gsh[0], bsh[0], gsh[1], bsh[1]], axis=1).astype(f)
    gb = np.ascontiguousarray(gb)
    grep = np.broadcast_to(np.asarray(L0_attn_g, f).reshape(-1)[0:1].reshape(1, 1),
                           (128, 1)).copy()
    rs = np.ascontiguousarray(np.asarray(L0_attn_res_scale, f).reshape(8, 128).T)
    b1 = np.ascontiguousarray(np.asarray(L0_ffn_b1, f).reshape(32, 128).T)
    b2 = np.ascontiguousarray(np.asarray(L0_ffn_b2, f).reshape(8, 128).T)
    ident = np.eye(128, dtype=f)
    wrelc = np.broadcast_to(w_rel[2046:2047].reshape(1, 1), (128, 1)).astype(f).copy()
    consts = np.zeros((128, 8), f)
    consts[:, 0] = 128.0
    consts[:, 1] = 1e-5
    consts[:, 2] = 0.5
    consts[:, 3] = -0.5

    sin, cos = _trig_tables()

    per_h = {}
    for h in (0, 1):
        rot = np.r_[OWN * h:OWN * h + OWN,
                    (0 if h else OWN):(OWN if h else L)].astype(np.int64)
        idx = rot[:, None] - (OWN * h + np.arange(OWN))[None, :] + (L - 1)
        biasT = np.ascontiguousarray(
            np.take(w_rel, np.clip(idx, 0, w_rel.shape[0] - 1)).astype(f))
        cosT = np.ascontiguousarray(np.concatenate([cos[rot].T, cos[rot].T], axis=0))
        sinT = np.ascontiguousarray(np.concatenate([sin[rot].T, sin[rot].T], axis=0))
        per_h[h] = (rot, biasT, cosT, sinT)

    in_maps = []
    for c in range(8):
        b, h = c // 2, c % 2
        rot, biasT, cosT, sinT = per_h[h]
        xc = np.ascontiguousarray(x[b][rot])
        in_maps.append(dict(x=xc, uvwT=uvwT, owT=owT, w1T=w1T, w2T=w2T,
                            biasT=biasT, cosT=cosT, sinT=sinT, gb=gb, grep=grep,
                            rs=rs, b1=b1, b2=b2, ident=ident, consts=consts,
                            wrelc=wrelc))
    return in_maps


def kernel(**inputs):
    in_maps = prepare_inputs(**inputs)
    nc = build_program()
    res = run_bass_kernel_spmd(nc, in_maps, core_ids=list(range(8)))
    out = np.empty((B, L, D), np.float32)
    for c in range(8):
        b, h = c // 2, c % 2
        out[b, OWN * h:OWN * h + OWN] = res.results[c]["out"]
    return out
